# version 22
# baseline (speedup 1.0000x reference)
import math
from contextlib import ExitStack

import numpy as np
import ml_dtypes

import concourse.bass as bass
import concourse.mybir as mybir
from concourse.bass_utils import run_bass_kernel_spmd

F32 = mybir.dt.float32
BF16 = mybir.dt.bfloat16
AF = mybir.ActivationFunctionType
ALU = mybir.AluOpType
AX = mybir.AxisListType
BF = ml_dtypes.bfloat16

D = 1024
NH = 16
DH = 64
FH = 8
FP = 128
FE = 8
ENGS = ("tensor", "vector", "scalar", "gpsimd", "sync")

C_EXPB = -8.0
C_EPS = 1e-5
C_MH = -0.5
C_QS_ATTN = 1.0 / math.sqrt(DH)
C_QS_FFN = 1.0 / math.sqrt(FP)


class FakeIns:
    __slots__ = ("incs",)

    def __init__(self):
        self.incs = []

    def then_inc(self, hdl, amt):
        self.incs.append((hdl, amt))
        return self


class RecEng:

    def __init__(self, rec):
        self._rec = rec

    def __getattr__(self, name):
        rec = self._rec

        def meth(*a, **kw):
            fi = FakeIns()
            rec.append((name, a, kw, fi))
            return fi
        return meth


class KB:

    def __init__(self, nc, blk, stack):
        self.nc, self.blk, self.stack = nc, blk, stack
        self.cnt = {}
        self.hdl = {}
        self.seen = {e: {} for e in ENGS}
        self.mir = {e: [] for e in ENGS}
        self.rec = {e: [] for e in ENGS}

    def sems(self, *names):
        for n in names:
            self.hdl[n] = self.stack.enter_context(self.nc.semaphore(n))
            self.cnt[n] = 0

    def run(self, eng, fn):
        fn(Em(self, eng, RecEng(self.rec[eng])))

    def flush(self):
        for eng in ENGS:
            rec = self.rec[eng]
            if not rec:
                continue

            def body(E, rec=rec):
                for it in rec:
                    if it[0] == "__wait__":
                        E.wait_ge(it[1], it[2])
                    else:
                        name, a, kw, fi = it
                        ins = getattr(E, name)(*a, **kw)
                        for hdl, amt in fi.incs:
                            ins.then_inc(hdl, amt)
            getattr(self.blk, eng)(body)

    def snap(self):
        return dict(self.cnt)

    def check_deadlock(self):
        ptr = {e: 0 for e in ENGS}
        cnt = {s: 0 for s in self.cnt}
        progress = True
        while progress:
            progress = False
            for e in ENGS:
                while ptr[e] < len(self.mir[e]):
                    it = self.mir[e][ptr[e]]
                    if it[0] == "w":
                        if cnt[it[1]] < it[2]:
                            break
                    else:
                        cnt[it[1]] += it[2]
                    ptr[e] += 1
                    progress = True
        stuck = {e: ptr[e] for e in ENGS if ptr[e] < len(self.mir[e])}
        if stuck:
            det = {}
            for e in stuck:
                it = self.mir[e][ptr[e]]
                det[e] = f"op {ptr[e]}/{len(self.mir[e])} waits {it[1]}>={it[2]} (now {cnt[it[1]]})"
            raise RuntimeError(f"DEADLOCK in built program: {det}")


class Em:
    def __init__(self, k, eng, E):
        self.k, self.eng, self.E = k, eng, E

    def w(self, *toks):
        flat = []
        for t in toks:
            if t is None:
                continue
            if isinstance(t, (list, tuple)) and t and isinstance(t[0], (list, tuple)):
                flat.extend(t)
            else:
                flat.append(t)
        for t in flat:
            if t is None:
                continue
            s, v = t
            if v <= 0 or self.k.seen[self.eng].get(s, 0) >= v:
                continue
            self.k.rec[self.eng].append(("__wait__", self.k.hdl[s], v))
            self.k.seen[self.eng][s] = v
            self.k.mir[self.eng].append(("w", s, v))

    def bar(self, snapshot):
        self.w(*snapshot.items())

    def inc(self, ins, sem, amt=1):
        self.k.cnt[sem] += amt
        ins.then_inc(self.k.hdl[sem], amt)
        self.k.mir[self.eng].append(("i", sem, amt))
        return (sem, self.k.cnt[sem])

    def tok(self, sem):
        return (sem, self.k.cnt[sem])


def build_program(S=4096, parts=("f1", "at", "f2"), dbg=()):
    NS = S // 128
    SUP = min(1024, S)
    NSUP = S // SUP
    QC = min(512, SUP)
    NQC = SUP // QC
    nc = bass.Bass("TRN2", target_bir_lowering=False)

    x_in = nc.dram_tensor("x", [S, D], F32, kind="ExternalInput").ap()
    w_qkv = nc.dram_tensor("w_qkv", [3 * D, D], F32, kind="ExternalInput").ap()
    w_out = nc.dram_tensor("w_out", [D, D], F32, kind="ExternalInput").ap()
    qw = nc.dram_tensor("qw", [FH * FP, D], F32, kind="ExternalInput").ap()
    kw = nc.dram_tensor("kw", [FH * FP, D], F32, kind="ExternalInput").ap()
    vw = nc.dram_tensor("vw", [FH * FE, D, FP], F32, kind="ExternalInput").ap()
    ow = nc.dram_tensor("ow", [FH * FE, FP, D], F32, kind="ExternalInput").ap()
    selv = nc.dram_tensor("selv", [FH * FE, D], F32, kind="ExternalInput").ap()
    selo = nc.dram_tensor("selo", [FH * FE, D], F32, kind="ExternalInput").ap()
    rs_in = nc.dram_tensor("rs", [1, 1], F32, kind="ExternalInput").ap()
    identf_in = nc.dram_tensor("identf", [128, 128], F32, kind="ExternalInput").ap()
    identb_in = nc.dram_tensor("identb", [128, 128], BF16, kind="ExternalInput").ap()
    e8_in = nc.dram_tensor("e8", [8, 1024], BF16, kind="ExternalInput").ap()
    cos_in = nc.dram_tensor("ropecos", [S, DH], F32, kind="ExternalInput").ap()
    s1_in = nc.dram_tensor("ropes1", [S, DH], F32, kind="ExternalInput").ap()
    consts_in = nc.dram_tensor("consts", [128, 8], F32, kind="ExternalInput").ap()
    out = nc.dram_tensor("out", [S, D], F32, kind="ExternalOutput").ap()

    def scr(name, shape, dt):
        kind = "ExternalOutput" if name in dbg else "Internal"
        return nc.dram_tensor(name, shape, dt, kind=kind).ap()

    xA = scr("xA", [S, D], F32)
    xB = scr("xB", [S, D], F32)
    hn_h = scr("hn_h", [S, D], BF16)
    fq_h = scr("fq_h", [FH, 128, S], BF16)
    fk_h = scr("fk_h", [FH, 128, S], BF16)
    fv_h = scr("fv_h", [FH, S, FP], BF16)
    aq_h = scr("aq_h", [S, D], BF16)
    ak_h = scr("ak_h", [S, D], BF16)
    av_h = scr("av_h", [S, D], BF16)
    p_selT = scr("p_selT", [D, 128], BF16)
    p_qwT = scr("p_qwT", [D, FH * FP], BF16)
    p_kwT = scr("p_kwT", [D, FH * FP], BF16)
    p_wqkvT = scr("p_wqkvT", [D, 3 * D], BF16)
    p_woutT = scr("p_woutT", [D, D], BF16)
    p_owb = scr("p_owb", [FH * FE, FP, D], BF16)
    p_vwb = scr("p_vwb", [FH, 8, 128, 2, 512], BF16)

    _uid = [0]

    def _nm(base="t"):
        _uid[0] += 1
        return f"{base}_{_uid[0]}"

    stack = ExitStack()
    with stack:
        blk = stack.enter_context(nc.Block())
        k = KB(nc, blk, stack)
        k.sems("cT", "cV", "cS", "cP", "gW", "gS", "sIN", "sT", "sW",
               "sR0", "sR1", "sL0", "sL1", "gV0", "gV1", "gO0", "gO1",
               "gS0", "gS1", "gR0", "gR1", "sW0", "sW1", "sC", "sP0", "sP1",
               "sO0", "sO1", "sH0", "sH1")
        sb = lambda shape, dt: stack.enter_context(nc.sbuf_tensor(_nm(), shape, dt))
        prep_toks = []
        prep_toks2 = []

        ct = sb([128, 8], F32)
        identf = sb([128, 128], F32)
        identb = sb([128, 128], BF16)
        ones_rowf = sb([1, 64], F32)
        ones_colb = sb([128, 1], BF16)
        e8c = sb([8, 1024], BF16)
        rs_v = sb([128, 1], F32)
        rs_o = sb([128, 1], F32)

        from concourse.hw_specs import get_activation_tables
        _tables = list(get_activation_tables(nc.m.arch).values())

        def _set_id(*funcs):
            for i, fs in enumerate(_tables):
                if all(f in fs for f in funcs):
                    return i
            raise ValueError(funcs)
        ID_NLEXP = _set_id(AF.Ln, AF.Exp)
        ID_SIG = _set_id(AF.Sigmoid)

        def preload_tbl(e, set_id):
            e.E.add_instruction(mybir.InstLoadActFuncSet(
                name=nc.get_next_instruction_name(),
                act_func_set_id=set_id, ins=[], outs=[]))

        nc.const_aps.register(F32, C_EXPB, ct[:, 0:1])
        nc.const_aps.register(F32, C_EPS, ct[:, 1:2])
        nc.const_aps.register(F32, C_MH, ct[:, 2:3])
        nc.const_aps.register(F32, C_QS_ATTN, ct[:, 3:4])
        nc.const_aps.register(F32, C_QS_FFN, ct[:, 4:5])

        def g_init(e):
            e.inc(e.E.dma_start(ct[:], consts_in), "gW", 16)
            e.inc(e.E.dma_start(identf[:], identf_in), "gW", 16)
            e.inc(e.E.dma_start(identb[:], identb_in), "gW", 16)
            e.inc(e.E.dma_start(rs_v[:], rs_in.partition_broadcast(128).squeeze(1)),
                  "gW", 16)
            e.inc(e.E.dma_start(e8c[:], e8_in), "gW", 16)
        k.run("gpsimd", g_init)
        t_ginit = ("gW", k.cnt["gW"])

        def s_tblinit(e):
            preload_tbl(e, ID_NLEXP)
        k.run("scalar", s_tblinit)

        def v_init(e):
            e.w(t_ginit)
            e.inc(e.E.memset(ones_rowf[:], 1.0), "cV")
            e.inc(e.E.memset(ones_colb[:], 1.0), "cV")
            e.inc(e.E.tensor_scalar_mul(rs_o[:], rs_v[:], 0.5), "cV")
        k.run("vector", v_init)

        def prep(ps, pin, pps, pst):
            if True:
                jobs = []
                for r in range(8):
                    jobs.append((w_qkv[r * 128:(r + 1) * 128, :], 128, p_wqkvT,
                                 C_QS_ATTN, r * 128))
                for r in range(8, 24):
                    jobs.append((w_qkv[r * 128:(r + 1) * 128, :], 128, p_wqkvT,
                                 1.0, r * 128))
                for r in range(8):
                    jobs.append((w_out[r * 128:(r + 1) * 128, :], 128, p_woutT,
                                 1.0, r * 128))
                for r in range(8):
                    jobs.append((qw[r * 128:(r + 1) * 128, :], 128, p_qwT,
                                 C_QS_FFN, r * 128))
                for r in range(8):
                    jobs.append((kw[r * 128:(r + 1) * 128, :], 128, p_kwT,
                                 1.0, r * 128))
                jobs.append((selv[:, :], 64, p_selT, 1.0, 0))
                jobs.append((selo[:, :], 64, p_selT, 1.0, 64))

                def g_cast(e):
                    for he in range(FH * FE):
                        e.inc(e.E.dma_start(p_owb[he], ow[he]), "gW", 16)
                    for h in range(FH):
                        for dc in range(8):
                            for eh in range(2):
                                e.inc(e.E.dma_start(
                                    p_vwb[h, dc, :, eh, :]
                                    .rearrange("p (ee c) -> p ee c", ee=4),
                                    vw[h * 8 + eh * 4:h * 8 + eh * 4 + 4,
                                       dc * 128:(dc + 1) * 128, :]
                                    .rearrange("ee d c -> d ee c")), "gW", 16)
                k.run("gpsimd", g_cast)

                ev_tok = [None, None]
                st_tok = [None, None]
                for j, (src, nr, dst, scale, co) in enumerate(jobs):
                    b = j % 2

                    def s_load(e, src=src, nr=nr, b=b, ev=ev_tok[b]):
                        e.w(ev)
                        e.inc(e.E.dma_start(pin[b][0:nr, :], src), "sP%d" % b, 16)
                    k.run("sync", s_load)
                    in_tok = ("sP%d" % b, k.cnt["sP%d" % b])

                    def t_tr(e, nr=nr, b=b, in_tok=in_tok, ev=ev_tok[b]):
                        e.w(t_ginit, in_tok, ev)
                        ins = None
                        for c in range(8):
                            ins = e.E.transpose(pps[b][:, c * nr:(c + 1) * nr],
                                                pin[b][0:nr, c * 128:(c + 1) * 128],
                                                identf[0:nr, 0:nr])
                        e.inc(ins, "cT")
                    k.run("tensor", t_tr)
                    tr_tok = ("cT", k.cnt["cT"])

                    def s_ev(e, b=b, scale=scale, nr=nr, tr=tr_tok, st=st_tok[b]):
                        e.w(tr, st)
                        if scale == 1.0:
                            ins = e.E.activation(pst[b][:, 0:8 * nr],
                                                 pps[b][:, 0:8 * nr], AF.Copy)
                        else:
                            ins = e.E.activation(pst[b][:, 0:8 * nr],
                                                 pps[b][:, 0:8 * nr], AF.Copy,
                                                 scale=scale)
                        e.inc(ins, "cS")
                    k.run("scalar", s_ev)
                    ev_tok[b] = ("cS", k.cnt["cS"])

                    def g_st(e, b=b, dst=dst, co=co, nr=nr, ev=ev_tok[b]):
                        e.w(ev)
                        dview = dst.rearrange("(c p) r -> p c r", p=128)
                        e.inc(e.E.dma_start(
                            dview[:, :, co:co + nr],
                            pst[b][:, 0:8 * nr].rearrange("p (c r) -> p c r", c=8)),
                            "gS%d" % b, 16)
                    k.run("gpsimd", g_st)
                    st_tok[b] = ("gS%d" % b, k.cnt["gS%d" % b])


        def ln_phase(src, bar, res_tok):
            with ExitStack() as ps:
                xt = [ps.enter_context(nc.sbuf_tensor(_nm(), [128, D], F32)) for _ in range(2)]
                sq = ps.enter_context(nc.sbuf_tensor(_nm(), [128, D], F32))
                hnt = [ps.enter_context(nc.sbuf_tensor(_nm(), [128, D], BF16)) for _ in range(2)]
                st = ps.enter_context(nc.sbuf_tensor(_nm(), [128, 2, 8], F32))
                hn_done = [None, None]
                last_vneg = [None, None]
                hn_stored = [None, None]
                for sc in range(NS):
                    b = sc % 2
                    s_ = st[:, b, :]

                    def s_ld(e, sc=sc, b=b, hd=hn_done[b], bar=bar, rt=res_tok,
                             hs=hn_stored[b], lv=last_vneg[b]):
                        e.bar(bar)
                        e.w(rt, hd, hs, lv)
                        e.inc(e.E.dma_start(xt[b][:], src[sc * 128:(sc + 1) * 128, :]),
                              "sR%d" % b, 16)
                    k.run("sync", s_ld)
                    tin = ("sR%d" % b, k.cnt["sR%d" % b])

                    def v_s1(e, b=b, s_=s_, tin=tin, bar=bar, hd=hn_done[b]):
                        e.bar(bar)
                        e.w(tin, hd)
                        e.inc(e.E.reduce_sum(s_[:, 0:1], xt[b][:], axis=AX.X), "cV")
                    k.run("vector", v_s1)
                    t_s1 = ("cV", k.cnt["cV"])

                    def v_s2(e, b=b, s_=s_, tin=tin, bar=bar, lv=last_vneg[b]):
                        e.bar(bar)
                        e.w(tin, lv)
                        e.inc(e.E.tensor_mul(sq[:], xt[b][:], xt[b][:]), "cV")
                        e.w(e.tok("cV"))
                        e.inc(e.E.reduce_sum(s_[:, 1:2], sq[:], axis=AX.X), "cV")
                    k.run("vector", v_s2)
                    t_s2 = ("cV", k.cnt["cV"])

                    def v_mid(e, s_=s_, t_s1=t_s1, t_s2=t_s2):
                        e.w(t_s1, t_s2)
                        e.inc(e.E.tensor_scalar_mul(s_[:, 2:3], s_[:, 0:1], 1.0 / D), "cV")
                        e.inc(e.E.tensor_scalar_mul(s_[:, 3:4], s_[:, 1:2], 1.0 / D), "cV")
                        e.w(e.tok("cV"))
                        e.inc(e.E.tensor_mul(s_[:, 4:5], s_[:, 2:3], s_[:, 2:3]), "cV")
                        e.w(e.tok("cV"))
                        e.inc(e.E.tensor_sub(s_[:, 3:4], s_[:, 3:4], s_[:, 4:5]), "cV")
                    k.run("vector", v_mid)
                    t_var = ("cV", k.cnt["cV"])

                    def s_mid(e, s_=s_, t_var=t_var):
                        e.w(t_var)
                        e.inc(e.E.activation(s_[:, 5:6], s_[:, 3:4], AF.Ln,
                                             bias=C_EPS), "cS")
                        e.w(e.tok("cS"))
                        e.inc(e.E.activation(s_[:, 5:6], s_[:, 5:6], AF.Exp,
                                             scale=C_MH), "cS")
                    k.run("scalar", s_mid)
                    t_rstd = ("cS", k.cnt["cS"])

                    def v_neg(e, s_=s_, t_rstd=t_rstd):
                        e.w(t_rstd)
                        e.inc(e.E.tensor_scalar(s_[:, 6:7], s_[:, 2:3], s_[:, 5:6],
                                                -1.0, ALU.mult, ALU.mult), "cV")
                    k.run("vector", v_neg)
                    last_vneg[b] = ("cV", k.cnt["cV"])

                    def v_hn(e, b=b, s_=s_, tn=last_vneg[b], hs=hn_stored[b]):
                        e.w(tn, hs)
                        e.inc(e.E.tensor_scalar(hnt[b][:], xt[b][:], s_[:, 5:6],
                                                s_[:, 6:7], ALU.mult, ALU.add), "cV")
                    k.run("vector", v_hn)
                    hn_done[b] = ("cV", k.cnt["cV"])

                    def g_st(e, b=b, sc=sc, hd=hn_done[b]):
                        e.w(hd)
                        e.inc(e.E.dma_start(hn_h[sc * 128:(sc + 1) * 128, :],
                                            hnt[b][:]), "gS%d" % b, 16)
                    k.run("gpsimd", g_st)
                    hn_stored[b] = ("gS%d" % b, k.cnt["gS%d" % b])
            return (("gS0", k.cnt["gS0"]), ("gS1", k.cnt["gS1"]))

        def load_hnf(hnf, tok_hn, free_tok, bar):
            def s_ld(e, bar=bar):
                e.bar(bar)
                e.w(tok_hn, free_tok)
                for c in range(8):
                    e.inc(e.E.dma_start_transpose(
                        hnf[:, c, :], hn_h[:, c * 128:(c + 1) * 128]), "sT", 16)
            k.run("sync", s_ld)
            return ("sT", k.cnt["sT"])

        def emit_sel(hnf, g_all, tok_hnf, bar, ps):
            if True:
                selT = ps.enter_context(nc.sbuf_tensor(_nm(), [128, 8, 128], BF16))
                lg = [ps.enter_context(nc.psum_tensor(_nm("ps"), [128, 128], F32)) for _ in range(2)]
                pr = ps.enter_context(nc.sbuf_tensor(_nm(), [128, 16, 8], F32))
                mr = ps.enter_context(nc.sbuf_tensor(_nm(), [128, 16, 8], F32))
                vv = ps.enter_context(nc.sbuf_tensor(_nm(), [128, 16, 4], F32))

                def s_w(e, bar=bar):
                    e.bar(bar)
                    e.w(*prep_toks)
                    e.inc(e.E.dma_start(
                        selT[:], p_selT.rearrange("(c p) r -> p c r", p=128)), "sC", 16)
                k.run("sync", s_w)
                t_selT = ("sC", k.cnt["sC"])

                def s_sig_tbl(e):
                    preload_tbl(e, ID_SIG)
                k.run("scalar", s_sig_tbl)
                sig_done = [None, None]
                gate_done = None
                for sc in range(NS):
                    b = sc % 2

                    def t_mm(e, sc=sc, b=b, t_selT=t_selT, tok_hnf=tok_hnf,
                             sd=sig_done[b], bar=bar):
                        e.bar(bar)
                        e.w(t_selT, tok_hnf, sd)
                        ins = None
                        for d in range(8):
                            ins = e.E.matmul(lg[b][:],
                                             hnf[:, d, sc * 128:(sc + 1) * 128],
                                             selT[:, d, :],
                                             start=(d == 0), stop=(d == 7))
                        e.inc(ins, "cT")
                    k.run("tensor", t_mm)
                    t_lg = ("cT", k.cnt["cT"])

                    def s_sig(e, b=b, t_lg=t_lg, gd=gate_done, bar=bar):
                        e.bar(bar)
                        e.w(t_lg, gd)
                        e.inc(e.E.activation(pr[:].rearrange("p a b -> p (a b)"),
                                             lg[b][:], AF.Sigmoid), "cS")
                    k.run("scalar", s_sig)
                    sig_done[b] = ("cS", k.cnt["cS"])

                    def v_top(e, sc=sc, sd=sig_done[b], bar=bar):
                        e.bar(bar)
                        e.w(sd)
                        e.inc(e.E.reduce_max(vv[:, :, 0:1].squeeze(2), pr[:],
                                             axis=AX.X), "cV")
                        e.w(e.tok("cV"))
                        e.inc(e.E.tensor_tensor(mr[:], pr[:],
                                                vv[:, :, 0:1].broadcast_to([128, 16, 8]),
                                                ALU.is_ge), "cV")
                        e.w(e.tok("cV"))
                        e.inc(e.E.scalar_tensor_tensor(mr[:], mr[:], -1e9, pr[:],
                                                       ALU.mult, ALU.add), "cV")
                        e.w(e.tok("cV"))
                        e.inc(e.E.reduce_max(vv[:, :, 1:2].squeeze(2), mr[:],
                                             axis=AX.X), "cV")
                        e.w(e.tok("cV"))
                        e.inc(e.E.tensor_add(vv[:, :, 2:3], vv[:, :, 0:1],
                                             vv[:, :, 1:2]), "cV")
                        e.w(e.tok("cV"))
                        e.inc(e.E.tensor_scalar_max(vv[:, :, 2:3], vv[:, :, 2:3],
                                                    1e-9), "cV")
                        e.w(e.tok("cV"))
                        e.inc(e.E.reciprocal(vv[:, :, 2:3], vv[:, :, 2:3]), "cV")
                        e.w(e.tok("cV"))
                        e.inc(e.E.tensor_scalar(vv[:, 0:8, 3:4], vv[:, 0:8, 2:3],
                                                rs_v[:], None, ALU.mult), "cV")
                        e.inc(e.E.tensor_scalar(vv[:, 8:16, 3:4], vv[:, 8:16, 2:3],
                                                rs_o[:], None, ALU.mult), "cV")
                        e.w(e.tok("cV"))
                        e.inc(e.E.tensor_tensor(mr[:], pr[:],
                                                vv[:, :, 1:2].broadcast_to([128, 16, 8]),
                                                ALU.is_ge), "cV")
                        e.w(e.tok("cV"))
                        e.inc(e.E.tensor_mul(mr[:], mr[:], pr[:]), "cV")
                        e.w(e.tok("cV"))
                        e.inc(e.E.tensor_tensor(
                            g_all[:, sc, :].rearrange("p (a b) -> p a b", a=16),
                            mr[:], vv[:, :, 3:4].broadcast_to([128, 16, 8]),
                            ALU.mult), "cV")
                    k.run("vector", v_top)
                    gate_done = ("cV", k.cnt["cV"])

                def s_nl_tbl(e):
                    preload_tbl(e, ID_NLEXP)
                k.run("scalar", s_nl_tbl)
            return gate_done

        def emit_ffn_qk(hnf, tok_hnf, bar, ps):
            if True:
                wbuf = [ps.enter_context(nc.sbuf_tensor(_nm(), [128, 8, 128], BF16))
                        for _ in range(2)]
                qp = [ps.enter_context(nc.psum_tensor(_nm("ps"), [128, 512], F32)) for _ in range(2)]
                stg = [ps.enter_context(nc.sbuf_tensor(_nm(), [128, S], BF16)) for _ in range(2)]
                wfree = [None, None]
                evtok = [None, None]
                sttok = [None, None]
                NSQ = S // 512
                for part, (wsrc, wdst) in enumerate(((p_qwT, fq_h), (p_kwT, fk_h))):
                    for pc in range(8):
                        wb = (part * 8 + pc) % 2

                        def s_w(e, wsrc=wsrc, pc=pc, wb=wb, wf=wfree[wb], bar=bar):
                            e.bar(bar)
                            e.w(*prep_toks, wf)
                            e.inc(e.E.dma_start(
                                wbuf[wb][:],
                                wsrc.rearrange("(c p) r -> p c r", p=128)
                                [:, :, pc * 128:(pc + 1) * 128]), "sW%d" % wb, 16)
                        k.run("sync", s_w)
                        t_w = ("sW%d" % wb, k.cnt["sW%d" % wb])

                        for scq in range(NSQ):
                            b = scq % 2

                            def t_mm(e, wb=wb, b=b, scq=scq, t_w=t_w,
                                     tok_hnf=tok_hnf, pf=evtok[b], bar=bar):
                                e.bar(bar)
                                e.w(t_w, tok_hnf, pf)
                                ins = None
                                for d in range(8):
                                    ins = e.E.matmul(
                                        qp[b][:], wbuf[wb][:, d, :],
                                        hnf[:, d, scq * 512:(scq + 1) * 512],
                                        start=(d == 0), stop=(d == 7))
                                e.inc(ins, "cT")
                            k.run("tensor", t_mm)
                            t_mm_t = ("cT", k.cnt["cT"])
                            if scq == NSQ - 1:
                                wfree[wb] = t_mm_t

                            wb2 = (part * 8 + pc) % 2

                            def s_ev(e, b=b, wb2=wb2, scq=scq, t_mm_t=t_mm_t,
                                     sf=(sttok[wb2] if scq == 0 else None), bar=bar):
                                e.bar(bar)
                                e.w(t_mm_t, sf)
                                e.inc(e.E.activation(
                                    stg[wb2][:, scq * 512:(scq + 1) * 512],
                                    qp[b][:], AF.Copy), "cS")
                            k.run("scalar", s_ev)
                            evtok[b] = ("cS", k.cnt["cS"])
                            if scq == NSQ - 1:
                                def g_st(e, wb2=wb2, pc=pc, wdst=wdst, ev=evtok[b]):
                                    e.w(ev)
                                    e.inc(e.E.dma_start(wdst[pc], stg[wb2][:]),
                                          "gS%d" % wb2, 16)
                                k.run("gpsimd", g_st)
                                sttok[wb2] = ("gS%d" % wb2, k.cnt["gS%d" % wb2])
            return (("gS0", k.cnt["gS0"]), ("gS1", k.cnt["gS1"]))

        def emit_ffn_v(hnf, g_all, tok_hnf, tok_gates, bar, ps):
            if True:
                vwb = [ps.enter_context(nc.sbuf_tensor(_nm(), [128, 8, 2, 512], BF16))
                       for _ in range(2)]
                xv = [ps.enter_context(nc.psum_tensor(_nm("ps"), [128, D], F32)) for _ in range(2)]
                vacc = ps.enter_context(nc.sbuf_tensor(_nm(), [128, FP], F32))
                vst = [ps.enter_context(nc.sbuf_tensor(_nm(), [128, NS, FP], BF16))
                       for _ in range(2)]
                vw_free = [None, None]
                vst_free = [None, None]
                xv_free = [None, None]
                vacc_free = None
                for h in range(FH):
                    hb = h % 2

                    def g_w(e, h=h, hb=hb, vf=vw_free[hb], bar=bar):
                        e.bar(bar)
                        e.w(*prep_toks2, vf)
                        e.inc(e.E.dma_start(
                            vwb[hb][:],
                            p_vwb[h].rearrange("c p e f -> p c e f")),
                            "gV%d" % hb, 16)
                    k.run("sync", g_w)
                    t_vw = ("gV%d" % hb, k.cnt["gV%d" % hb])

                    for sc in range(NS):
                        b = sc % 2

                        def t_mm(e, hb=hb, b=b, sc=sc, t_vw=t_vw, tok_hnf=tok_hnf,
                                 xf=xv_free[b], bar=bar):
                            e.bar(bar)
                            e.w(t_vw, tok_hnf, xf)
                            for eh in range(2):
                                ins = None
                                for dd in range(8):
                                    ins = e.E.matmul(
                                        xv[b][:, eh * 512:(eh + 1) * 512],
                                        hnf[:, dd, sc * 128:(sc + 1) * 128],
                                        vwb[hb][:, dd, eh, :],
                                        start=(dd == 0), stop=(dd == 7))
                                e.inc(ins, "cT")
                        k.run("tensor", t_mm)
                        t_xv = ("cT", k.cnt["cT"])

                        def v_g(e, h=h, b=b, sc=sc, t_xv=t_xv, vaf=vacc_free,
                                tg=tok_gates, bar=bar):
                            e.bar(bar)
                            e.w(t_xv, vaf, tg)
                            for ei in range(8):
                                gcol = g_all[:, sc, h * 8 + ei:h * 8 + ei + 1]
                                if ei == 0:
                                    e.inc(e.E.tensor_scalar(
                                        vacc[:], xv[b][:, 0:128], gcol, None,
                                        ALU.mult), "cV")
                                else:
                                    e.inc(e.E.scalar_tensor_tensor(
                                        vacc[:], xv[b][:, ei * 128:(ei + 1) * 128],
                                        gcol, vacc[:], ALU.mult, ALU.add), "cV")
                                e.w(e.tok("cV"))
                        k.run("vector", v_g)
                        xv_free[b] = ("cV", k.cnt["cV"])

                        def s_ev(e, hb=hb, sc=sc, tg=xv_free[b], vf=vst_free[hb],
                                 bar=bar):
                            e.bar(bar)
                            e.w(tg, vf)
                            e.inc(e.E.activation(vst[hb][:, sc, :], vacc[:],
                                                 AF.Copy), "cS")
                        k.run("scalar", s_ev)
                        vacc_free = ("cS", k.cnt["cS"])
                    vw_free[hb] = ("cT", k.cnt["cT"])

                    def g_st(e, h=h, hb=hb, tv=vacc_free):
                        e.w(tv)
                        e.inc(e.E.dma_start(
                            fv_h[h].rearrange("(sc p) c -> p sc c", p=128),
                            vst[hb][:]), "gS%d" % hb, 16)
                    k.run("gpsimd", g_st)
                    vst_free[hb] = ("gS%d" % hb, k.cnt["gS%d" % hb])
            return (("gS0", k.cnt["gS0"]), ("gS1", k.cnt["gS1"]))

        def make_resid(ps):
            B = {
                "xr": [ps.enter_context(nc.sbuf_tensor(_nm(), [128, D], F32))
                       for _ in range(2)],
                "xo": [ps.enter_context(nc.sbuf_tensor(_nm(), [128, D], F32))
                       for _ in range(2)],
                "sq": ps.enter_context(nc.sbuf_tensor(_nm(), [128, D], F32)),
                "st": ps.enter_context(nc.sbuf_tensor(_nm(), [128, 2, 8], F32)),
                "hnt": [ps.enter_context(nc.sbuf_tensor(_nm(), [128, D], BF16))
                        for _ in range(2)],
                "hn_done": [None, None],
                "sq_free": None,
            }
            return B

        def emit_resid_super(B, src, dst, sup, out_evac, res_done, oacc, bar,
                             fuse_hn):
            xr, xo, sqf, stf, hnt = B["xr"], B["xo"], B["sq"], B["st"], B["hnt"]
            for i8 in range(SUP // 128):
                rb = i8 % 2
                row = sup * SUP + i8 * 128
                s_ = stf[:, rb, :]

                def s_ld(e, rb=rb, row=row, oe=out_evac, rd=res_done, bar=bar,
                         hd=B["hn_done"][rb]):
                    e.bar(bar)
                    e.w(oe, rd, hd)
                    e.inc(e.E.dma_start(xr[rb][:], src[row:row + 128, :]),
                          "sR%d" % rb, 16)
                k.run("sync", s_ld)
                t_x = ("sR%d" % rb, k.cnt["sR%d" % rb])

                def v_add(e, rb=rb, i8=i8, t_x=t_x, oe=out_evac, rd=res_done,
                          hd=B["hn_done"][rb]):
                    e.w(t_x, oe, rd, hd)
                    e.inc(e.E.tensor_add(xo[rb][:], oacc[:, i8, :], xr[rb][:]),
                          "cV")
                k.run("vector", v_add)
                t_add = ("cV", k.cnt["cV"])

                def g_st(e, rb=rb, row=row, t_add=t_add, dst=dst):
                    e.w(t_add)
                    e.inc(e.E.dma_start(dst[row:row + 128, :], xo[rb][:]),
                          "sO%d" % rb, 16)
                k.run("sync", g_st)
                res_done = (("sO0", k.cnt["sO0"]), ("sO1", k.cnt["sO1"]))

                if fuse_hn:
                    def v_st(e, rb=rb, s_=s_, t_add=t_add, sqf_f=B["sq_free"]):
                        e.w(t_add, sqf_f)
                        e.inc(e.E.tensor_mul(sqf[:], xo[rb][:], xo[rb][:]), "cV")
                        e.inc(e.E.reduce_sum(s_[:, 0:1], xo[rb][:], axis=AX.X), "cV")
                        e.w(e.tok("cV"))
                        e.inc(e.E.reduce_sum(s_[:, 1:2], sqf[:], axis=AX.X), "cV")
                        e.w(e.tok("cV"))
                        e.inc(e.E.tensor_scalar_mul(s_[:, 2:3], s_[:, 0:1], 1.0 / D),
                              "cV")
                        e.inc(e.E.tensor_scalar_mul(s_[:, 3:4], s_[:, 1:2], 1.0 / D),
                              "cV")
                        e.w(e.tok("cV"))
                        e.inc(e.E.tensor_mul(s_[:, 4:5], s_[:, 2:3], s_[:, 2:3]),
                              "cV")
                        e.w(e.tok("cV"))
                        e.inc(e.E.tensor_sub(s_[:, 3:4], s_[:, 3:4], s_[:, 4:5]),
                              "cV")
                    k.run("vector", v_st)
                    t_var = ("cV", k.cnt["cV"])
                    B["sq_free"] = t_var

                    def s_mid(e, s_=s_, t_var=t_var):
                        e.w(t_var)
                        e.inc(e.E.activation(s_[:, 5:6], s_[:, 3:4], AF.Ln,
                                             bias=C_EPS), "cS")
                        e.w(e.tok("cS"))
                        e.inc(e.E.activation(s_[:, 5:6], s_[:, 5:6], AF.Exp,
                                             scale=C_MH), "cS")
                    k.run("scalar", s_mid)
                    t_rstd = ("cS", k.cnt["cS"])

                    def v_hn(e, rb=rb, s_=s_, t_rstd=t_rstd,
                             hd=B["hn_done"][rb]):
                        e.w(t_rstd, hd)
                        e.inc(e.E.tensor_scalar(s_[:, 6:7], s_[:, 2:3], s_[:, 5:6],
                                                -1.0, ALU.mult, ALU.mult), "cV")
                        e.w(e.tok("cV"))
                        e.inc(e.E.tensor_scalar(hnt[rb][:], xo[rb][:], s_[:, 5:6],
                                                s_[:, 6:7], ALU.mult, ALU.add), "cV")
                    k.run("vector", v_hn)
                    t_hn = ("cV", k.cnt["cV"])

                    def g_hn(e, rb=rb, row=row, t_hn=t_hn):
                        e.w(t_hn, ("sT", k.cnt["sT"]))
                        e.inc(e.E.dma_start(hn_h[row:row + 128, :], hnt[rb][:]),
                              "sH%d" % rb, 16)
                    k.run("sync", g_hn)
                    B["hn_done"][rb] = ("sH%d" % rb, k.cnt["sH%d" % rb])
            return res_done

        def emit_ffn_attn(src, dst, g_all, tok_qk, tok_v, tok_gates, bar, fuse_hn):
            with ExitStack() as ps:
                kfb = [ps.enter_context(nc.sbuf_tensor(_nm(), [128, S], BF16)) for _ in range(2)]
                qhb = [ps.enter_context(nc.sbuf_tensor(_nm(), [128, SUP], BF16)) for _ in range(2)]
                vtb = [ps.enter_context(nc.sbuf_tensor(_nm(), [128, NS, FP], BF16))
                       for _ in range(2)]
                owb = [ps.enter_context(nc.sbuf_tensor(_nm(), [128, 8, D], BF16))
                       for _ in range(2)]
                Pt = ps.enter_context(nc.sbuf_tensor(_nm(), [128, NS, 512], BF16))
                Ps2 = ps.enter_context(nc.sbuf_tensor(_nm(), [128, NS // 2, 512], BF16))
                SC = [ps.enter_context(nc.psum_tensor(_nm("ps"), [128, 512], F32)) for _ in range(2)]
                CTX = ps.enter_context(nc.psum_tensor(_nm("ps"), [128, 512], F32))
                LP = ps.enter_context(nc.psum_tensor(_nm("ps"), [128, 512], F32))
                GBp = ps.enter_context(nc.psum_tensor(_nm("ps"), [128, 512], F32))
                GBg = ps.enter_context(nc.psum_tensor(_nm("ps"), [128, 512], F32))
                OUT = GBp
                OUT2 = [ps.enter_context(nc.psum_tensor(_nm("ps"), [128, 512], F32))
                        for _ in range(2)]
                rl_s = ps.enter_context(nc.sbuf_tensor(_nm(), [1, 512], F32))
                rlc = ps.enter_context(nc.sbuf_tensor(_nm(), [128, 4], F32))
                Gp = [ps.enter_context(nc.sbuf_tensor(_nm(), [8, 512], BF16)) for _ in range(2)]
                cs2 = ps.enter_context(nc.sbuf_tensor(_nm(), [128, 512], F32))
                yTs = ps.enter_context(nc.sbuf_tensor(_nm(), [128, 8, 512], BF16))
                oacc = ps.enter_context(nc.sbuf_tensor(_nm(), [128, SUP // 128, D], F32))
                RB = make_resid(ps)

                kf_free = [None, None]
                qh_free = [None, None]
                vt_free = [None, None]
                ow_free = [None, None]
                ld_tok = {}

                def issue_loads(idx, sup, h):
                    b = idx % 2

                    def s_ld(e, sup=sup, h=h, b=b, kf=kf_free[b], qf=qh_free[b],
                             vf=vt_free[b], bar=bar):
                        e.bar(bar)
                        e.w(*tok_qk, *tok_v, kf, qf, vf)
                        e.inc(e.E.dma_start(kfb[b][:], fk_h[h]), "sL%d" % b, 16)
                        e.inc(e.E.dma_start(
                            qhb[b][:], fq_h[h][:, sup * SUP:(sup + 1) * SUP]),
                            "sL%d" % b, 16)
                        e.inc(e.E.dma_start(
                            vtb[b][:],
                            fv_h[h].rearrange("(sc p) c -> p sc c", p=128)),
                            "sL%d" % b, 16)
                    k.run("sync", s_ld)
                    t_s = ("sL%d" % b, k.cnt["sL%d" % b])

                    def g_ow(e, h=h, b=b, of=ow_free[b], bar=bar):
                        e.bar(bar)
                        e.w(*prep_toks2, of)
                        e.inc(e.E.dma_start(
                            owb[b][:], p_owb[h * 8:(h + 1) * 8]
                            .rearrange("ee p c -> p ee c")), "gO%d" % b, 16)
                    k.run("sync", g_ow)
                    ld_tok[idx] = (t_s, ("gO%d" % b, k.cnt["gO%d" % b]))

                st = {
                    "pt_free": None,
                    "ctx_free": None,
                    "l_free": None,
                    "gp_free": [None, None],
                    "gbp_free": None,
                    "gbp_hold": None,
                    "gbp_rl": None,
                    "gbg_free": None,
                    "cs2_free": None,
                    "rlc_free": None,
                    "yts_free": None,
                    "out_free": None,
                    "out2_free": None,
                    "res_done": None,
                    "rl_s_free": None,
                    "ps2_free": None,
                }

                def make_tail(c):
                    (ci, b, sup, h, qc, t_ow, t_recip, t_g4, pvl_toks) = c
                    cb = ci % 2
                    ts = {"y": [], "out_free": None, "rlc": None}

                    def p_rl():
                        def t_rlt(e, t_recip=t_recip, of=st["out_free"],
                                  rf=st["rlc_free"], gb=st["gbp_hold"]):
                            e.w(t_recip, of, rf, gb)
                            ins = None
                            for qsub in range(4):
                                ins = e.E.transpose(
                                    OUT[0:128, 256 + qsub:256 + qsub + 1],
                                    rl_s[0:1, qsub * 128:(qsub + 1) * 128],
                                    identf[0:1, 0:1])
                            e.inc(ins, "cT")
                        k.run("tensor", t_rlt)
                        t_rlt_t = ("cT", k.cnt["cT"])
                        st["gbp_free"] = t_rlt_t

                        def v_rlc(e, t_rlt_t=t_rlt_t):
                            e.w(t_rlt_t)
                            e.inc(e.E.tensor_copy(rlc[:], OUT[0:128, 256:260]), "cV")
                            e.w(e.tok("cV"), pvl_toks[-1])
                            e.inc(e.E.tensor_copy(cs2[:], CTX[:]), "cV")
                        k.run("vector", v_rlc)
                        t_rlc = ("cV", k.cnt["cV"])
                        ts["rlc"] = t_rlc
                        st["rlc_free"] = t_rlc
                        st["ctx_free"] = t_rlc
                        st["cs2_free"] = t_rlc
                        st["out_free"] = t_rlc
                        st["gbp_rl"] = t_rlc

                    def p_ggy(ei):
                        def t_gg(e, ei=ei, cb=cb, gp=t_g4, gf=st["gbg_free"],
                                 yp=(ts["y"][ei - 1] if ei else None)):
                            e.w(gp, gf, yp)
                            e.inc(e.E.matmul(
                                GBg[:], e8c[:, ei * 128:(ei + 1) * 128],
                                Gp[cb][:], start=True, stop=True), "cT")
                        k.run("tensor", t_gg)
                        t_ggt = ("cT", k.cnt["cT"])

                        def v_y(e, ei=ei, t_ggt=t_ggt, yf=st["yts_free"],
                                c2=st["cs2_free"]):
                            e.w(t_ggt, yf, c2)
                            e.inc(e.E.tensor_tensor(
                                yTs[:, ei, :], cs2[:], GBg[:], ALU.mult), "cV")
                        k.run("vector", v_y)
                        ts["y"].append(("cV", k.cnt["cV"]))
                        if ei == 7:
                            st["gbg_free"] = ts["y"][-1]

                    def p_om(qsub, half):
                        of = ts["out_free"] if ts["out_free"] else st["out_free"]
                        ob = (qsub * 2 + half) % 2

                        def t_om(e, qsub=qsub, half=half, ob=ob, t_ow=t_ow, of=of,
                                 t_rlc=ts["rlc"], y_toks=list(ts["y"])):
                            e.w(t_ow, of, t_rlc)
                            ins = None
                            for ei in range(8):
                                e.w(y_toks[ei])
                                ins = e.E.matmul(
                                    OUT2[ob][:],
                                    yTs[:, ei, qsub * 128:(qsub + 1) * 128],
                                    owb[b][:, ei, half * 512:(half + 1) * 512],
                                    start=(ei == 0), stop=(ei == 7))
                            e.inc(ins, "cT")
                        k.run("tensor", t_om)
                        t_out = ("cT", k.cnt["cT"])

                        def ev(e, h=h, qsub=qsub, half=half, ob=ob,
                               sc8=qc * 4 + qsub, t_out=t_out,
                               rd=st["res_done"], t_rlc=ts["rlc"]):
                            e.w(t_out, rd, t_rlc)
                            dstp = oacc[:, sc8, half * 512:(half + 1) * 512]
                            if h == 0:
                                e.inc(e.E.tensor_scalar(
                                    dstp, OUT2[ob][:], rlc[:, qsub:qsub + 1], None,
                                    ALU.mult), "cV")
                            else:
                                e.inc(e.E.scalar_tensor_tensor(
                                    dstp, OUT2[ob][:], rlc[:, qsub:qsub + 1], dstp,
                                    ALU.mult, ALU.add), "cV")
                        k.run("vector", ev)
                        ts["out_free"] = ("cV", k.cnt["cV"])
                        if qsub == 3 and half == 1:
                            st["yts_free"] = ("cT", k.cnt["cT"])
                            st["out2_free"] = ts["out_free"]

                    pieces = [p_rl]
                    for ei in range(8):
                        pieces.append(lambda ei=ei: p_ggy(ei))
                    for qsub in range(4):
                        for half in range(2):
                            pieces.append(lambda q=qsub, hf=half: p_om(q, hf))
                    return pieces


                pend = []
                pend_sink = {}
                def pend_done():
                    pb = pend_sink.pop("pb", None)
                    if pb is not None:
                        ow_free[pb] = ("cT", k.cnt["cT"])
                issue_loads(0, 0, 0)
                idx = 0
                resid_q = None
                for sup in range(NSUP):
                    for h in range(FH):
                        b = idx % 2
                        t_ld, t_ow = ld_tok[idx]

                        for qc in range(NQC):
                            ci = idx * NQC + qc
                            cb = ci % 2
                            scbase = (sup * SUP + qc * QC) // 128

                            def t_go(e, h=h, scbase=scbase, bar=bar,
                                     tg=tok_gates, gf=st["gbp_free"],
                                     gr=st["gbp_rl"], gh=st["gbp_hold"]):
                                e.bar(bar)
                                e.w(tg, gf, gr, gh)
                                ins = None
                                for qsub in range(4):
                                    ins = e.E.transpose(
                                        GBp[0:8, qsub * 128:(qsub + 1) * 128],
                                        g_all[:, scbase + qsub,
                                              64 + h * 8:64 + h * 8 + 8],
                                        identf[:])
                                e.inc(ins, "cT")
                            k.run("tensor", t_go)
                            t_goT = ("cT", k.cnt["cT"])

                            def v_g4(e, cb=cb, t_goT=t_goT, gpf=st["gp_free"][cb]):
                                e.w(t_goT, gpf)
                                e.inc(e.E.tensor_copy(Gp[cb][:], GBp[0:8, :]), "cV")
                            k.run("vector", v_g4)
                            t_g4 = ("cV", k.cnt["cV"])
                            st["gbp_hold"] = t_g4

                            exp_toks = []
                            mm_toks = []
                            pvl_toks = []
                            ps2_toks = []
                            prev_pt = st["pt_free"]
                            prev_ps2 = st["ps2_free"]

                            def emit_sc(kc):
                                sb_ = kc % 2

                                def t_sc(e, b=b, sb_=sb_, kc=kc, qc=qc, t_ld=t_ld,
                                         bar=bar,
                                         ef=(exp_toks[kc - 2] if kc >= 2 else None)):
                                    e.bar(bar)
                                    e.w(t_ld, ef)
                                    e.inc(e.E.matmul(
                                        SC[sb_][:],
                                        kfb[b][:, kc * 128:(kc + 1) * 128],
                                        qhb[b][:, qc * QC:(qc + 1) * QC],
                                        start=True, stop=True), "cT")
                                k.run("tensor", t_sc)
                                mm_toks.append(("cT", k.cnt["cT"]))

                                def s_ex(e, sb_=sb_, kc=kc, mt=mm_toks[kc], bar=bar,
                                         ptf=(prev_pt[kc] if prev_pt else None)):
                                    e.bar(bar)
                                    e.w(mt, ptf)
                                    e.inc(e.E.activation(
                                        Pt[:, kc, :], SC[sb_][:], AF.Exp,
                                        bias=C_EXPB), "cS")
                                k.run("scalar", s_ex)
                                exp_toks.append(("cS", k.cnt["cS"]))

                            def emit_ps2(kc):
                                p2 = kc // 2

                                def g_ps(e, kc=kc, p2=p2, te=exp_toks[kc],
                                         pf=(prev_ps2[p2] if prev_ps2 else None)):
                                    e.w(te, pf)
                                    e.inc(e.E.tensor_add(
                                        Ps2[:, p2, :], Pt[:, kc - 1, :],
                                        Pt[:, kc, :]), "cP")
                                k.run("gpsimd", g_ps)
                                ps2_toks.append(("cP", k.cnt["cP"]))

                            def emit_pvl(kc):
                                p2 = kc // 2

                                def t_pvl(e, b=b, kc=kc, p2=p2, te=exp_toks[kc],
                                          cf=(st["ctx_free"] if kc == 0 else None),
                                          lf=(st["l_free"] if kc == 0 else None),
                                          pst=(ps2_toks[p2] if kc % 2 == 1 else None)):
                                    e.w(te, cf, lf)
                                    ins = e.E.matmul(
                                        CTX[:], vtb[b][:, kc, :], Pt[:, kc, :],
                                        start=(kc == 0), stop=(kc == NS - 1))
                                    if kc % 2 == 1:
                                        e.w(pst)
                                        ins = e.E.matmul(
                                            LP[0:1, :], ones_colb[:], Ps2[:, p2, :],
                                            start=(kc == 1), stop=(kc == NS - 1))
                                    e.inc(ins, "cT")
                                k.run("tensor", t_pvl)
                                pvl_toks.append(("cT", k.cnt["cT"]))

                            for kc in range(NS):
                                emit_sc(kc)
                                if kc % 2 == 1:
                                    emit_ps2(kc)
                                if kc == 1 and pend:
                                    pend.pop(0)()
                                if kc >= 2:
                                    emit_pvl(kc - 2)
                                if kc >= 3 and pend:
                                    pend.pop(0)()
                                    if not pend:
                                        pend_done()
                                if (kc == min(24, NS - 1) and not pend
                                        and resid_q is not None):
                                    st["res_done"] = emit_resid_super(
                                        RB, src, dst, resid_q, st["out2_free"],
                                        st["res_done"], oacc, bar, fuse_hn)
                                    resid_q = None
                            emit_pvl(NS - 2)
                            emit_pvl(NS - 1)
                            while pend:
                                pend.pop(0)()
                            pend_done()
                            if resid_q is not None:
                                st["res_done"] = emit_resid_super(
                                    RB, src, dst, resid_q, st["out2_free"],
                                    st["res_done"], oacc, bar, fuse_hn)
                                resid_q = None

                            def v_recip(e, pl=pvl_toks[-1], rsf=st.get("rl_s_free")):
                                e.w(pl, rsf)
                                e.inc(e.E.reciprocal(rl_s[:], LP[0:1, :]), "cV")
                            k.run("vector", v_recip)
                            t_recip = ("cV", k.cnt["cV"])
                            st["l_free"] = t_recip
                            st["rl_s_free"] = None

                            st["pt_free"] = pvl_toks
                            st["ps2_free"] = [pvl_toks[2 * p + 1]
                                              for p in range(NS // 2)]
                            pend = make_tail((ci, b, sup, h, qc, t_ow, t_recip,
                                              t_g4, pvl_toks))
                            if qc == NQC - 1:
                                pend_sink["pb"] = b

                        kf_free[b] = mm_toks[-1]
                        qh_free[b] = mm_toks[-1]
                        vt_free[b] = pvl_toks[-1]
                        if idx + 1 < NSUP * FH:
                            nsup, nh = divmod(idx + 1, FH)
                            issue_loads(idx + 1, nsup, nh)
                        idx += 1

                    resid_q = sup
                while pend:
                    pend.pop(0)()
                pend_done()
                if resid_q is not None:
                    st["res_done"] = emit_resid_super(
                        RB, src, dst, resid_q, st["out2_free"], st["res_done"],
                        oacc, bar, fuse_hn)
            return st["res_done"], (("gS0", k.cnt["gS0"]), ("gS1", k.cnt["gS1"]))


        def ffn_full(src, dst, bar, res_tok, hn_in, fuse_next):
            if hn_in is None:
                tok_hn = ln_phase(src, bar, res_tok)
            else:
                tok_hn = hn_in
            bar1 = k.snap()
            with ExitStack() as ps:
                g_all = ps.enter_context(nc.sbuf_tensor(_nm(), [128, NS, 128], F32))
                with ExitStack() as p1:
                    hnf = p1.enter_context(nc.sbuf_tensor(_nm(), [128, 8, S], BF16))
                    tok_hnf = load_hnf(hnf, tok_hn, None, bar1)
                    tok_gates = emit_sel(hnf, g_all, tok_hnf, bar1, p1)
                    tok_qk = emit_ffn_qk(hnf, tok_hnf, bar1, p1)
                    tok_v = emit_ffn_v(hnf, g_all, tok_hnf, tok_gates, bar1, p1)
                bar4 = k.snap()
                tok_res, hn_out = emit_ffn_attn(src, dst, g_all, tok_qk, tok_v,
                                                tok_gates, bar4, fuse_next)
            return k.snap(), tok_res, hn_out

        def emit_attn_qkv(tok_hnf, hnf, bar):
            with ExitStack() as ps:
                cosT = ps.enter_context(nc.sbuf_tensor(_nm(), [128, NS, 64], F32))
                s1T = ps.enter_context(nc.sbuf_tensor(_nm(), [128, NS, 64], F32))
                wqb = [ps.enter_context(nc.sbuf_tensor(_nm(), [128, 8, 512], BF16))
                       for _ in range(2)]
                qp = [ps.enter_context(nc.psum_tensor(_nm("ps"), [128, 512], F32)) for _ in range(2)]
                t1 = ps.enter_context(nc.sbuf_tensor(_nm(), [128, 8, 64], F32))
                t2 = ps.enter_context(nc.sbuf_tensor(_nm(), [128, 8, 64], F32))
                stg = [ps.enter_context(nc.sbuf_tensor(_nm(), [128, 512], BF16)) for _ in range(2)]

                def s_tbl(e, bar=bar):
                    e.bar(bar)
                    e.inc(e.E.dma_start(
                        cosT[:], cos_in.rearrange("(sc p) d -> p sc d", p=128)), "sC", 16)
                    e.inc(e.E.dma_start(
                        s1T[:], s1_in.rearrange("(sc p) d -> p sc d", p=128)), "sC", 16)
                k.run("sync", s_tbl)
                t_tbl = ("sC", k.cnt["sC"])

                wfree = [None, None]
                evtok = [None, None]
                sttok = [None, None]
                rope_busy = None
                for cc in range(6):
                    wb = cc % 2
                    is_v = cc >= 4

                    def s_w(e, cc=cc, wb=wb, wf=wfree[wb], bar=bar):
                        e.bar(bar)
                        e.w(*prep_toks, wf)
                        e.inc(e.E.dma_start(
                            wqb[wb][:],
                            p_wqkvT.rearrange("(c p) r -> p c r", p=128)
                            [:, :, cc * 512:(cc + 1) * 512]), "sW%d" % wb, 16)
                    k.run("sync", s_w)
                    t_w = ("sW%d" % wb, k.cnt["sW%d" % wb])

                    for sc in range(NS):
                        b = sc % 2

                        def t_mm(e, wb=wb, b=b, sc=sc, t_w=t_w, tok_hnf=tok_hnf,
                                 pf=evtok[b], bar=bar):
                            e.bar(bar)
                            e.w(t_w, tok_hnf, pf)
                            ins = None
                            for d in range(8):
                                ins = e.E.matmul(
                                    qp[b][:], hnf[:, d, sc * 128:(sc + 1) * 128],
                                    wqb[wb][:, d, :], start=(d == 0), stop=(d == 7))
                            e.inc(ins, "cT")
                        k.run("tensor", t_mm)
                        t_mm_t = ("cT", k.cnt["cT"])
                        if sc == NS - 1:
                            wfree[wb] = t_mm_t

                        if is_v:
                            def s_ev(e, b=b, t_mm_t=t_mm_t, sf=sttok[b], bar=bar):
                                e.bar(bar)
                                e.w(t_mm_t, sf)
                                e.inc(e.E.activation(stg[b][:], qp[b][:], AF.Copy),
                                      "cS")
                            k.run("scalar", s_ev)
                            evtok[b] = ("cS", k.cnt["cS"])
                        else:
                            def v_rope(e, b=b, sc=sc, t_mm_t=t_mm_t, t_tbl=t_tbl,
                                       sf=sttok[b], bar=bar, rbusy=rope_busy):
                                e.bar(bar)
                                e.w(t_mm_t, t_tbl, sf, rbusy)
                                qv = qp[b][:].rearrange("p (a b) -> p a b", a=8)
                                e.inc(e.E.tensor_tensor(
                                    t1[:], qv,
                                    cosT[:, sc:sc + 1, :].broadcast_to([128, 8, 64]),
                                    ALU.mult), "cV")
                                e.inc(e.E.tensor_tensor(
                                    t2[:, :, 0:32], qv[:, :, 32:64],
                                    s1T[:, sc:sc + 1, 0:32].broadcast_to([128, 8, 32]),
                                    ALU.mult), "cV")
                                e.inc(e.E.tensor_tensor(
                                    t2[:, :, 32:64], qv[:, :, 0:32],
                                    s1T[:, sc:sc + 1, 32:64].broadcast_to([128, 8, 32]),
                                    ALU.mult), "cV")
                                e.w(e.tok("cV"))
                                e.inc(e.E.tensor_tensor(
                                    stg[b][:].rearrange("p (a b) -> p a b", a=8),
                                    t1[:], t2[:], ALU.add), "cV")
                            k.run("vector", v_rope)
                            rope_busy = ("cV", k.cnt["cV"])
                            evtok[b] = rope_busy

                        def g_st(e, b=b, cc=cc, sc=sc, ev=evtok[b]):
                            e.w(ev)
                            dsts = (aq_h, ak_h, av_h)[cc // 2]
                            co = (cc % 2) * 512
                            e.inc(e.E.dma_start(
                                dsts[sc * 128:(sc + 1) * 128, co:co + 512],
                                stg[b][:]), "gS%d" % b, 16)
                        k.run("gpsimd", g_st)
                        sttok[b] = ("gS%d" % b, k.cnt["gS%d" % b])
            return (("gS0", k.cnt["gS0"]), ("gS1", k.cnt["gS1"]))

        def emit_attn_attn(src, dst, tok_qkv, bar, fuse_hn):
            with ExitStack() as ps:
                kpb = [ps.enter_context(nc.sbuf_tensor(_nm(), [128, S], BF16)) for _ in range(2)]
                qpb = [ps.enter_context(nc.sbuf_tensor(_nm(), [128, SUP], BF16)) for _ in range(2)]
                vab = [[ps.enter_context(nc.sbuf_tensor(_nm(), [128, NS, 65], BF16))
                        for _ in range(2)] for _ in range(2)]
                woT = ps.enter_context(nc.sbuf_tensor(_nm(), [128, 8, D], BF16))
                ctxf = [ps.enter_context(nc.sbuf_tensor(_nm(), [128, SUP], BF16))
                        for _ in range(8)]
                Pt2 = [ps.enter_context(nc.sbuf_tensor(_nm(), [128, NS, 512], BF16))
                       for _ in range(2)]
                SCa = [ps.enter_context(nc.psum_tensor(_nm("ps"), [128, 1024], F32))
                       for _ in range(2)]
                CTXA = [ps.enter_context(nc.psum_tensor(_nm("ps"), [65, 512], F32))
                        for _ in range(2)]
                RLP = ps.enter_context(nc.psum_tensor(_nm("ps"), [64, 512], F32))
                OUT = ps.enter_context(nc.psum_tensor(_nm("ps"), [128, 512], F32))
                rla = [ps.enter_context(nc.sbuf_tensor(_nm(), [1, 512], F32))
                       for _ in range(2)]
                cs_ = ps.enter_context(nc.sbuf_tensor(_nm(), [64, 512], F32))
                oacc = ps.enter_context(nc.sbuf_tensor(_nm(), [128, SUP // 128, D], F32))
                RB = make_resid(ps)

                def v_ones(e, bar=bar):
                    e.bar(bar)
                    for bb in range(2):
                        for hh in range(2):
                            e.inc(e.E.memset(vab[bb][hh][:, :, 64:65], 1.0), "cV")
                k.run("vector", v_ones)
                t_ones = ("cV", k.cnt["cV"])

                def s_wo(e, bar=bar):
                    e.bar(bar)
                    e.w(*prep_toks)
                    e.inc(e.E.dma_start(
                        woT[:], p_woutT.rearrange("(c p) r -> p c r", p=128)), "sC", 16)
                k.run("sync", s_wo)
                t_wo = ("sC", k.cnt["sC"])

                kp_free = [None, None]
                qp_free = [None, None]
                va_free = [None, None]
                ld_tok = {}

                def issue_loads(idx, sup, pair):
                    b = idx % 2

                    def s_ld(e, sup=sup, pair=pair, b=b, kf=kp_free[b],
                             qf=qp_free[b], vf=va_free[b], t_ones=t_ones, bar=bar):
                        e.bar(bar)
                        e.w(*tok_qkv, t_ones, kf, qf, vf)
                        e.inc(e.E.dma_start_transpose(
                            kpb[b][:], ak_h[:, pair * 128:(pair + 1) * 128]),
                            "sL%d" % b, 16)
                        e.inc(e.E.dma_start_transpose(
                            qpb[b][:], aq_h[sup * SUP:(sup + 1) * SUP,
                                            pair * 128:(pair + 1) * 128]),
                            "sL%d" % b, 16)
                        for hh in range(2):
                            hcol = (pair * 2 + hh) * 64
                            e.inc(e.E.dma_start(
                                vab[b][hh][:, :, 0:64],
                                av_h[:, hcol:hcol + 64]
                                .rearrange("(sc p) c -> p sc c", p=128)),
                                "sL%d" % b, 16)
                    k.run("sync", s_ld)
                    ld_tok[idx] = ("sL%d" % b, k.cnt["sL%d" % b])

                st = {
                    "pt_free": [None, None],
                    "ctx_free": [None, None],
                    "rlp_free": None,
                    "cs_free": None,
                    "rla_free": [None, None],
                    "out_free": None,
                    "res_done": None,
                }
                ctxf_free = [None] * 8

                def make_tail(c):
                    (b, sup, pair, qc, recips, pv_last) = c
                    ts = {}

                    def p_hh(hh):
                        def t_rlp(e, hh=hh, tr=recips[hh], rf=st["rlp_free"]):
                            e.w(tr, rf)
                            e.inc(e.E.matmul(RLP[:], ones_rowf[:, 0:64],
                                             rla[hh][:], start=True, stop=True),
                                  "cT")
                        k.run("tensor", t_rlp)
                        t_rlp_t = ("cT", k.cnt["cT"])
                        st["rla_free"][hh] = t_rlp_t

                        def v_cs(e, hh=hh, pl=pv_last[hh], cf=st["cs_free"]):
                            e.w(pl, cf)
                            e.inc(e.E.tensor_copy(cs_[:], CTXA[hh][0:64, :]), "cV")
                        k.run("vector", v_cs)
                        t_cs = ("cV", k.cnt["cV"])
                        st["ctx_free"][hh] = t_cs

                        def v_n(e, pair=pair, hh=hh, qc=qc, t_cs=t_cs,
                                t_rlp_t=t_rlp_t, cff=ctxf_free[pair]):
                            e.w(t_cs, t_rlp_t, cff)
                            e.inc(e.E.tensor_tensor(
                                ctxf[pair][hh * 64:(hh + 1) * 64,
                                           qc * QC:(qc + 1) * QC],
                                cs_[:], RLP[:], ALU.mult), "cV")
                        k.run("vector", v_n)
                        t_n = ("cV", k.cnt["cV"])
                        st["cs_free"] = t_n
                        st["rlp_free"] = t_n
                        ts[hh] = t_n

                    return [lambda: p_hh(0), lambda: p_hh(1)], ts

                pend = []
                issue_loads(0, 0, 0)
                idx = 0
                resid_q = None
                for sup in range(NSUP):
                    for pair in range(8):
                        b = idx % 2
                        t_ld = ld_tok[idx]

                        for qc in range(NQC):
                            exp_toks = [[], []]
                            mm_toks = [[], []]
                            pv_toks = [[], []]
                            prev_pt = [st["pt_free"][0], st["pt_free"][1]]

                            def emit_sc(p, hh):
                                base = hh * 64

                                def t_sc(e, b=b, p=p, hh=hh, base=base, qc=qc,
                                         t_ld=t_ld, bar=bar,
                                         ef=(exp_toks[hh][p - 1] if p >= 1 else None)):
                                    e.bar(bar)
                                    e.w(t_ld, ef)
                                    ins = None
                                    for half in range(2):
                                        kc = 2 * p + half
                                        ins = e.E.matmul(
                                            SCa[hh][:, half * 512:(half + 1) * 512],
                                            kpb[b][base:base + 64,
                                                   kc * 128:(kc + 1) * 128],
                                            qpb[b][base:base + 64,
                                                   qc * QC:(qc + 1) * QC],
                                            start=True, stop=True)
                                    e.inc(ins, "cT")
                                k.run("tensor", t_sc)
                                mm_toks[hh].append(("cT", k.cnt["cT"]))

                                def s_ex(e, p=p, hh=hh, mt=mm_toks[hh][p], bar=bar,
                                         ptf=(prev_pt[hh][p] if prev_pt[hh] else None)):
                                    e.bar(bar)
                                    e.w(mt, ptf)
                                    e.inc(e.E.activation(
                                        Pt2[hh][:, 2 * p:2 * p + 2, :]
                                        .rearrange("p a b -> p (a b)"),
                                        SCa[hh][:], AF.Exp, bias=C_EXPB), "cS")
                                k.run("scalar", s_ex)
                                exp_toks[hh].append(("cS", k.cnt["cS"]))

                            def emit_pv(p, hh):
                                def t_pv(e, b=b, p=p, hh=hh, te=exp_toks[hh][p],
                                         cf=(st["ctx_free"][hh] if p == 0 else None)):
                                    e.w(te, cf)
                                    ins = None
                                    for kc in (2 * p, 2 * p + 1):
                                        ins = e.E.matmul(
                                            CTXA[hh][:], vab[b][hh][:, kc, :],
                                            Pt2[hh][:, kc, :],
                                            start=(kc == 0), stop=(kc == NS - 1))
                                    e.inc(ins, "cT")
                                k.run("tensor", t_pv)
                                pv_toks[hh].append(("cT", k.cnt["cT"]))

                            for p in range(NS // 2):
                                emit_sc(p, 0)
                                emit_sc(p, 1)
                                if p <= 1 and pend:
                                    pend.pop(0)()
                                if p >= 2:
                                    emit_pv(p - 2, 0)
                                    emit_pv(p - 2, 1)
                                if (p == min(10, NS // 2 - 1) and not pend
                                        and resid_q is not None):
                                    st["res_done"] = emit_resid_super(
                                        RB, src, dst, resid_q, st["out_free"],
                                        st["res_done"], oacc, bar, fuse_hn)
                                    resid_q = None
                            for pp in (NS // 2 - 2, NS // 2 - 1):
                                emit_pv(pp, 0)
                                emit_pv(pp, 1)
                            while pend:
                                pend.pop(0)()
                            if resid_q is not None:
                                st["res_done"] = emit_resid_super(
                                    RB, src, dst, resid_q, st["out_free"],
                                    st["res_done"], oacc, bar, fuse_hn)
                                resid_q = None

                            recips = []
                            for hh in range(2):
                                def v_recip(e, hh=hh, pl=pv_toks[hh][-1],
                                            rf=st["rla_free"][hh]):
                                    e.w(pl, rf)
                                    e.inc(e.E.reciprocal(rla[hh][:],
                                                         CTXA[hh][64:65, :]), "cV")
                                k.run("vector", v_recip)
                                recips.append(("cV", k.cnt["cV"]))
                            st["pt_free"] = [pv_toks[0], pv_toks[1]]
                            pieces, _ = make_tail((b, sup, pair, qc, recips,
                                                   [pv_toks[0][-1], pv_toks[1][-1]]))
                            pend = pieces

                        kp_free[b] = mm_toks[1][-1]
                        qp_free[b] = mm_toks[1][-1]
                        va_free[b] = pv_toks[1][-1]
                        if idx + 1 < NSUP * 8:
                            nsup, npair = divmod(idx + 1, 8)
                            issue_loads(idx + 1, nsup, npair)
                        idx += 1

                    def make_wout(wsup):
                        def piece(qc, qsub, half):
                            def t_wom(e, qc=qc, qsub=qsub, half=half, t_wo=t_wo):
                                e.w(("cV", k.cnt["cV"]), t_wo, st["out_free"])
                                ins = None
                                for pc in range(8):
                                    ins = e.E.matmul(
                                        OUT[:],
                                        ctxf[pc][:, qc * QC + qsub * 128:
                                                 qc * QC + (qsub + 1) * 128],
                                        woT[:, pc, half * 512:(half + 1) * 512],
                                        start=(pc == 0), stop=(pc == 7))
                                e.inc(ins, "cT")
                            k.run("tensor", t_wom)
                            t_o = ("cT", k.cnt["cT"])

                            def v_ev(e, qc=qc, qsub=qsub, half=half, t_o=t_o,
                                     rd=st["res_done"]):
                                e.w(t_o, rd)
                                e.inc(e.E.tensor_copy(
                                    oacc[:, qc * 4 + qsub,
                                         half * 512:(half + 1) * 512],
                                    OUT[:]), "cV")
                            k.run("vector", v_ev)
                            st["out_free"] = ("cV", k.cnt["cV"])
                            if qc == NQC - 1 and qsub == 3 and half == 1:
                                for pc in range(8):
                                    ctxf_free[pc] = ("cT", k.cnt["cT"])
                        ps_ = []
                        for qc in range(NQC):
                            for qsub in range(4):
                                for half in range(2):
                                    ps_.append(lambda q=qc, s=qsub, hf=half:
                                               piece(q, s, hf))
                        return ps_
                    pend = pend + make_wout(sup)
                    resid_q = sup
                while pend:
                    pend.pop(0)()
                if resid_q is not None:
                    st["res_done"] = emit_resid_super(
                        RB, src, dst, resid_q, st["out_free"], st["res_done"],
                        oacc, bar, fuse_hn)
            return st["res_done"], (("gS0", k.cnt["gS0"]), ("gS1", k.cnt["gS1"]))


        def attn_full(src, dst, bar, res_tok, hn_in, fuse_next):
            if hn_in is None:
                tok_hn = ln_phase(src, bar, res_tok)
            else:
                tok_hn = hn_in
            bar1 = k.snap()
            with ExitStack() as ps:
                with ExitStack() as p1:
                    hnf = p1.enter_context(nc.sbuf_tensor(_nm(), [128, 8, S], BF16))
                    tok_hnf = load_hnf(hnf, tok_hn, None, bar1)
                    tok_qkv = emit_attn_qkv(tok_hnf, hnf, bar1)
                bar2 = k.snap()
                tok_res, hn_out = emit_attn_attn(src, dst, tok_qkv, bar2, fuse_next)
            return k.snap(), tok_res, hn_out

        with ExitStack() as _prep_ps:
            _pin = [_prep_ps.enter_context(nc.sbuf_tensor(_nm(), [128, D], F32)) for _ in range(2)]
            _pps = [_prep_ps.enter_context(nc.psum_tensor(_nm("ps"), [128, D], F32)) for _ in range(2)]
            _pst = [_prep_ps.enter_context(nc.sbuf_tensor(_nm(), [128, D], BF16)) for _ in range(2)]
            hn0_tok = ln_phase(x_in, k.snap(), None)
            prep(_prep_ps, _pin, _pps, _pst)
        prep_toks.extend(((("gS0", k.cnt["gS0"])), (("gS1", k.cnt["gS1"]))))
        prep_toks2.append(("gW", k.cnt["gW"]))

        chain = []
        if "f1" in parts:
            chain.append(("f", None))
        if "at" in parts:
            chain.append(("a", None))
        if "f2" in parts:
            chain.append(("f", None))
        srcs = [x_in, xA, xB]
        cur_bar = k.snap()
        res_tok = None
        hn_tok = hn0_tok
        src = x_in
        for i, (kind, _) in enumerate(chain):
            dst = out if i == len(chain) - 1 else srcs[i + 1]
            fuse_next = i < len(chain) - 1
            if kind == "f":
                cur_bar, res_tok, hn_tok = ffn_full(src, dst, cur_bar, res_tok,
                                                    hn_tok, fuse_next)
            else:
                cur_bar, res_tok, hn_tok = attn_full(src, dst, cur_bar, res_tok,
                                                     hn_tok, fuse_next)
            src = dst

        def s_fin(e):
            e.w(*[(s, k.cnt[s]) for s in k.cnt])
        k.run("sync", s_fin)

        k.check_deadlock()
        k.flush()
    return nc



def _host_constants(S):
    half = DH // 2
    inv = 1.0 / (10000.0 ** (np.arange(half, dtype=np.float64) / half))
    ang = np.arange(S, dtype=np.float64)[:, None] * inv[None, :]
    cos = np.cos(ang).astype(np.float32)
    sin = np.sin(ang).astype(np.float32)
    ropecos = np.concatenate([cos, cos], axis=1)
    ropes1 = np.concatenate([-sin, sin], axis=1)
    consts = np.zeros((128, 8), np.float32)
    consts[:, 0] = C_EXPB
    consts[:, 1] = C_EPS
    consts[:, 2] = C_MH
    consts[:, 3] = C_QS_ATTN
    consts[:, 4] = C_QS_FFN
    identf = np.eye(128, dtype=np.float32)
    identb = np.eye(128, dtype=np.float32).astype(BF)
    e8 = np.zeros((8, 1024), np.float32)
    for e in range(8):
        e8[e, e * 128:(e + 1) * 128] = 1.0
    e8 = e8.astype(BF)
    return ropecos, ropes1, consts, identf, identb, e8


def kernel(L0_attn_x, L0_attn_w_qkv, L0_attn_w_out, L0_ffn_q_w, L0_ffn_k_w,
           L0_ffn_v_w, L0_ffn_o_w, L0_ffn_sel_v_w, L0_ffn_sel_o_w,
           L0_ffn_route_scale):
    S = L0_attn_x.shape[1]
    ropecos, ropes1, consts, identf, identb, e8 = _host_constants(S)
    nc = build_program(S=S)
    common = {
        "w_qkv": np.ascontiguousarray(L0_attn_w_qkv),
        "w_out": np.ascontiguousarray(L0_attn_w_out),
        "qw": np.ascontiguousarray(L0_ffn_q_w),
        "kw": np.ascontiguousarray(L0_ffn_k_w),
        "vw": np.ascontiguousarray(L0_ffn_v_w),
        "ow": np.ascontiguousarray(L0_ffn_o_w),
        "selv": np.ascontiguousarray(L0_ffn_sel_v_w),
        "selo": np.ascontiguousarray(L0_ffn_sel_o_w),
        "rs": np.ascontiguousarray(L0_ffn_route_scale).reshape(1, 1),
        "ropecos": ropecos, "ropes1": ropes1, "consts": consts,
        "identf": identf, "identb": identb, "e8": e8,
    }
    in_maps = []
    for c in range(8):
        m = dict(common)
        m["x"] = np.ascontiguousarray(L0_attn_x[c])
        in_maps.append(m)
    res = run_bass_kernel_spmd(nc, in_maps, core_ids=list(range(8)))
    return np.stack([res.results[c]["out"] for c in range(8)], axis=0)


# version 23
# speedup vs baseline: 1.0119x; 1.0119x over previous
import math
from contextlib import ExitStack

import numpy as np
import ml_dtypes

import concourse.bass as bass
import concourse.mybir as mybir
from concourse.bass_utils import run_bass_kernel_spmd

F32 = mybir.dt.float32
BF16 = mybir.dt.bfloat16
AF = mybir.ActivationFunctionType
ALU = mybir.AluOpType
AX = mybir.AxisListType
BF = ml_dtypes.bfloat16

D = 1024
NH = 16
DH = 64
FH = 8
FP = 128
FE = 8
ENGS = ("tensor", "vector", "scalar", "gpsimd", "sync")

C_EXPB = -8.0
C_EPS = 1e-5
C_MH = -0.5
C_QS_ATTN = 1.0 / math.sqrt(DH)
C_QS_FFN = 1.0 / math.sqrt(FP)


class FakeIns:
    __slots__ = ("incs",)

    def __init__(self):
        self.incs = []

    def then_inc(self, hdl, amt):
        self.incs.append((hdl, amt))
        return self


class RecEng:

    def __init__(self, rec):
        self._rec = rec

    def __getattr__(self, name):
        rec = self._rec

        def meth(*a, **kw):
            fi = FakeIns()
            rec.append((name, a, kw, fi))
            return fi
        return meth


class KB:

    def __init__(self, nc, blk, stack):
        self.nc, self.blk, self.stack = nc, blk, stack
        self.cnt = {}
        self.hdl = {}
        self.seen = {e: {} for e in ENGS}
        self.mir = {e: [] for e in ENGS}
        self.rec = {e: [] for e in ENGS}

    def sems(self, *names):
        for n in names:
            self.hdl[n] = self.stack.enter_context(self.nc.semaphore(n))
            self.cnt[n] = 0

    def run(self, eng, fn):
        fn(Em(self, eng, RecEng(self.rec[eng])))

    def flush(self):
        for eng in ENGS:
            rec = self.rec[eng]
            if not rec:
                continue

            def body(E, rec=rec):
                for it in rec:
                    if it[0] == "__wait__":
                        E.wait_ge(it[1], it[2])
                    else:
                        name, a, kw, fi = it
                        ins = getattr(E, name)(*a, **kw)
                        for hdl, amt in fi.incs:
                            ins.then_inc(hdl, amt)
            getattr(self.blk, eng)(body)

    def snap(self):
        return dict(self.cnt)

    def check_deadlock(self):
        ptr = {e: 0 for e in ENGS}
        cnt = {s: 0 for s in self.cnt}
        progress = True
        while progress:
            progress = False
            for e in ENGS:
                while ptr[e] < len(self.mir[e]):
                    it = self.mir[e][ptr[e]]
                    if it[0] == "w":
                        if cnt[it[1]] < it[2]:
                            break
                    else:
                        cnt[it[1]] += it[2]
                    ptr[e] += 1
                    progress = True
        stuck = {e: ptr[e] for e in ENGS if ptr[e] < len(self.mir[e])}
        if stuck:
            det = {}
            for e in stuck:
                it = self.mir[e][ptr[e]]
                det[e] = f"op {ptr[e]}/{len(self.mir[e])} waits {it[1]}>={it[2]} (now {cnt[it[1]]})"
            raise RuntimeError(f"DEADLOCK in built program: {det}")


class Em:
    def __init__(self, k, eng, E):
        self.k, self.eng, self.E = k, eng, E

    def w(self, *toks):
        flat = []
        for t in toks:
            if t is None:
                continue
            if isinstance(t, (list, tuple)) and t and isinstance(t[0], (list, tuple)):
                flat.extend(t)
            else:
                flat.append(t)
        for t in flat:
            if t is None:
                continue
            s, v = t
            if v <= 0 or self.k.seen[self.eng].get(s, 0) >= v:
                continue
            self.k.rec[self.eng].append(("__wait__", self.k.hdl[s], v))
            self.k.seen[self.eng][s] = v
            self.k.mir[self.eng].append(("w", s, v))

    def bar(self, snapshot):
        self.w(*snapshot.items())

    def inc(self, ins, sem, amt=1):
        self.k.cnt[sem] += amt
        ins.then_inc(self.k.hdl[sem], amt)
        self.k.mir[self.eng].append(("i", sem, amt))
        return (sem, self.k.cnt[sem])

    def tok(self, sem):
        return (sem, self.k.cnt[sem])


def build_program(S=4096, parts=("f1", "at", "f2"), dbg=()):
    NS = S // 128
    SUP = min(1024, S)
    NSUP = S // SUP
    QC = min(512, SUP)
    NQC = SUP // QC
    nc = bass.Bass("TRN2", target_bir_lowering=False)

    x_in = nc.dram_tensor("x", [S, D], F32, kind="ExternalInput").ap()
    w_qkv = nc.dram_tensor("w_qkv", [3 * D, D], F32, kind="ExternalInput").ap()
    w_out = nc.dram_tensor("w_out", [D, D], F32, kind="ExternalInput").ap()
    qw = nc.dram_tensor("qw", [FH * FP, D], F32, kind="ExternalInput").ap()
    kw = nc.dram_tensor("kw", [FH * FP, D], F32, kind="ExternalInput").ap()
    vw = nc.dram_tensor("vw", [FH * FE, D, FP], F32, kind="ExternalInput").ap()
    ow = nc.dram_tensor("ow", [FH * FE, FP, D], F32, kind="ExternalInput").ap()
    selv = nc.dram_tensor("selv", [FH * FE, D], F32, kind="ExternalInput").ap()
    selo = nc.dram_tensor("selo", [FH * FE, D], F32, kind="ExternalInput").ap()
    rs_in = nc.dram_tensor("rs", [1, 1], F32, kind="ExternalInput").ap()
    identf_in = nc.dram_tensor("identf", [128, 128], F32, kind="ExternalInput").ap()
    identb_in = nc.dram_tensor("identb", [128, 128], BF16, kind="ExternalInput").ap()
    e8_in = nc.dram_tensor("e8", [8, 1024], BF16, kind="ExternalInput").ap()
    cos_in = nc.dram_tensor("ropecos", [S, DH], F32, kind="ExternalInput").ap()
    s1_in = nc.dram_tensor("ropes1", [S, DH], F32, kind="ExternalInput").ap()
    consts_in = nc.dram_tensor("consts", [128, 8], F32, kind="ExternalInput").ap()
    out = nc.dram_tensor("out", [S, D], F32, kind="ExternalOutput").ap()

    def scr(name, shape, dt):
        kind = "ExternalOutput" if name in dbg else "Internal"
        return nc.dram_tensor(name, shape, dt, kind=kind).ap()

    xA = scr("xA", [S, D], F32)
    xB = scr("xB", [S, D], F32)
    hn_h = scr("hn_h", [S, D], BF16)
    fq_h = scr("fq_h", [FH, 128, S], BF16)
    fk_h = scr("fk_h", [FH, 128, S], BF16)
    fv_h = scr("fv_h", [FH, S, FP], BF16)
    aq_h = scr("aq_h", [S, D], BF16)
    ak_h = scr("ak_h", [S, D], BF16)
    av_h = scr("av_h", [S, D], BF16)
    p_selT = scr("p_selT", [D, 128], BF16)
    p_qwT = scr("p_qwT", [D, FH * FP], BF16)
    p_kwT = scr("p_kwT", [D, FH * FP], BF16)
    p_wqkvT = scr("p_wqkvT", [D, 3 * D], BF16)
    p_woutT = scr("p_woutT", [D, D], BF16)
    p_owb = scr("p_owb", [FH * FE, FP, D], BF16)
    p_vwb = scr("p_vwb", [FH, 8, 128, 2, 512], BF16)

    _uid = [0]

    def _nm(base="t"):
        _uid[0] += 1
        return f"{base}_{_uid[0]}"

    stack = ExitStack()
    with stack:
        blk = stack.enter_context(nc.Block())
        k = KB(nc, blk, stack)
        k.sems("cT", "cV", "cS", "cP", "gW", "gS", "sIN", "sT", "sW",
               "sR0", "sR1", "sL0", "sL1", "gV0", "gV1", "gO0", "gO1",
               "gS0", "gS1", "gR0", "gR1", "sW0", "sW1", "sC", "sP0", "sP1",
               "sO0", "sO1", "sH0", "sH1")
        sb = lambda shape, dt: stack.enter_context(nc.sbuf_tensor(_nm(), shape, dt))
        prep_toks = []
        prep_toks2 = []

        ct = sb([128, 8], F32)
        identf = sb([128, 128], F32)
        identb = sb([128, 128], BF16)
        ones_rowf = sb([1, 64], F32)
        ones_colb = sb([128, 1], BF16)
        e8c = sb([8, 1024], BF16)
        rs_v = sb([128, 1], F32)
        rs_o = sb([128, 1], F32)

        from concourse.hw_specs import get_activation_tables
        _tables = list(get_activation_tables(nc.m.arch).values())

        def _set_id(*funcs):
            for i, fs in enumerate(_tables):
                if all(f in fs for f in funcs):
                    return i
            raise ValueError(funcs)
        ID_NLEXP = _set_id(AF.Ln, AF.Exp)
        ID_SIG = _set_id(AF.Sigmoid)

        def preload_tbl(e, set_id):
            e.E.add_instruction(mybir.InstLoadActFuncSet(
                name=nc.get_next_instruction_name(),
                act_func_set_id=set_id, ins=[], outs=[]))

        nc.const_aps.register(F32, C_EXPB, ct[:, 0:1])
        nc.const_aps.register(F32, C_EPS, ct[:, 1:2])
        nc.const_aps.register(F32, C_MH, ct[:, 2:3])
        nc.const_aps.register(F32, C_QS_ATTN, ct[:, 3:4])
        nc.const_aps.register(F32, C_QS_FFN, ct[:, 4:5])

        def g_init(e):
            e.inc(e.E.dma_start(ct[:], consts_in), "gW", 16)
            e.inc(e.E.dma_start(identf[:], identf_in), "gW", 16)
            e.inc(e.E.dma_start(identb[:], identb_in), "gW", 16)
            e.inc(e.E.dma_start(rs_v[:], rs_in.partition_broadcast(128).squeeze(1)),
                  "gW", 16)
            e.inc(e.E.dma_start(e8c[:], e8_in), "gW", 16)
        k.run("gpsimd", g_init)
        t_ginit = ("gW", k.cnt["gW"])

        def s_tblinit(e):
            preload_tbl(e, ID_NLEXP)
        k.run("scalar", s_tblinit)

        def v_init(e):
            e.w(t_ginit)
            e.inc(e.E.memset(ones_rowf[:], 1.0), "cV")
            e.inc(e.E.memset(ones_colb[:], 1.0), "cV")
            e.inc(e.E.tensor_scalar_mul(rs_o[:], rs_v[:], 0.5), "cV")
        k.run("vector", v_init)

        def prep(ps, pin, pps, pst):
            if True:
                jobs = []
                for r in range(8):
                    jobs.append((w_qkv[r * 128:(r + 1) * 128, :], 128, p_wqkvT,
                                 C_QS_ATTN, r * 128))
                for r in range(8, 24):
                    jobs.append((w_qkv[r * 128:(r + 1) * 128, :], 128, p_wqkvT,
                                 1.0, r * 128))
                for r in range(8):
                    jobs.append((w_out[r * 128:(r + 1) * 128, :], 128, p_woutT,
                                 1.0, r * 128))
                for r in range(8):
                    jobs.append((qw[r * 128:(r + 1) * 128, :], 128, p_qwT,
                                 C_QS_FFN, r * 128))
                for r in range(8):
                    jobs.append((kw[r * 128:(r + 1) * 128, :], 128, p_kwT,
                                 1.0, r * 128))
                jobs.append((selv[:, :], 64, p_selT, 1.0, 0))
                jobs.append((selo[:, :], 64, p_selT, 1.0, 64))

                def g_cast(e):
                    for he in range(FH * FE):
                        e.inc(e.E.dma_start(p_owb[he], ow[he]), "gW", 16)
                    for h in range(FH):
                        for dc in range(8):
                            for eh in range(2):
                                e.inc(e.E.dma_start(
                                    p_vwb[h, dc, :, eh, :]
                                    .rearrange("p (ee c) -> p ee c", ee=4),
                                    vw[h * 8 + eh * 4:h * 8 + eh * 4 + 4,
                                       dc * 128:(dc + 1) * 128, :]
                                    .rearrange("ee d c -> d ee c")), "gW", 16)
                k.run("gpsimd", g_cast)

                ev_tok = [None, None]
                st_tok = [None, None]
                for j, (src, nr, dst, scale, co) in enumerate(jobs):
                    b = j % 2

                    def s_load(e, src=src, nr=nr, b=b, ev=ev_tok[b]):
                        e.w(ev)
                        e.inc(e.E.dma_start(pin[b][0:nr, :], src), "sP%d" % b, 16)
                    k.run("sync", s_load)
                    in_tok = ("sP%d" % b, k.cnt["sP%d" % b])

                    def t_tr(e, nr=nr, b=b, in_tok=in_tok, ev=ev_tok[b]):
                        e.w(t_ginit, in_tok, ev)
                        ins = None
                        for c in range(8):
                            ins = e.E.transpose(pps[b][:, c * nr:(c + 1) * nr],
                                                pin[b][0:nr, c * 128:(c + 1) * 128],
                                                identf[0:nr, 0:nr])
                        e.inc(ins, "cT")
                    k.run("tensor", t_tr)
                    tr_tok = ("cT", k.cnt["cT"])

                    def s_ev(e, b=b, scale=scale, nr=nr, tr=tr_tok, st=st_tok[b]):
                        e.w(tr, st)
                        if scale == 1.0:
                            ins = e.E.activation(pst[b][:, 0:8 * nr],
                                                 pps[b][:, 0:8 * nr], AF.Copy)
                        else:
                            ins = e.E.activation(pst[b][:, 0:8 * nr],
                                                 pps[b][:, 0:8 * nr], AF.Copy,
                                                 scale=scale)
                        e.inc(ins, "cS")
                    k.run("scalar", s_ev)
                    ev_tok[b] = ("cS", k.cnt["cS"])

                    def g_st(e, b=b, dst=dst, co=co, nr=nr, ev=ev_tok[b]):
                        e.w(ev)
                        dview = dst.rearrange("(c p) r -> p c r", p=128)
                        e.inc(e.E.dma_start(
                            dview[:, :, co:co + nr],
                            pst[b][:, 0:8 * nr].rearrange("p (c r) -> p c r", c=8)),
                            "gS%d" % b, 16)
                    k.run("gpsimd", g_st)
                    st_tok[b] = ("gS%d" % b, k.cnt["gS%d" % b])


        def ln_phase(src, bar, res_tok):
            with ExitStack() as ps:
                xt = [ps.enter_context(nc.sbuf_tensor(_nm(), [128, D], F32)) for _ in range(2)]
                sq = ps.enter_context(nc.sbuf_tensor(_nm(), [128, D], F32))
                hnt = [ps.enter_context(nc.sbuf_tensor(_nm(), [128, D], BF16)) for _ in range(2)]
                st = ps.enter_context(nc.sbuf_tensor(_nm(), [128, 2, 8], F32))
                hn_done = [None, None]
                last_vneg = [None, None]
                hn_stored = [None, None]
                for sc in range(NS):
                    b = sc % 2
                    s_ = st[:, b, :]

                    def s_ld(e, sc=sc, b=b, hd=hn_done[b], bar=bar, rt=res_tok,
                             hs=hn_stored[b], lv=last_vneg[b]):
                        e.bar(bar)
                        e.w(rt, hd, hs, lv)
                        e.inc(e.E.dma_start(xt[b][:], src[sc * 128:(sc + 1) * 128, :]),
                              "sR%d" % b, 16)
                    k.run("sync", s_ld)
                    tin = ("sR%d" % b, k.cnt["sR%d" % b])

                    def v_s1(e, b=b, s_=s_, tin=tin, bar=bar, hd=hn_done[b]):
                        e.bar(bar)
                        e.w(tin, hd)
                        e.inc(e.E.reduce_sum(s_[:, 0:1], xt[b][:], axis=AX.X), "cV")
                    k.run("vector", v_s1)
                    t_s1 = ("cV", k.cnt["cV"])

                    def v_s2(e, b=b, s_=s_, tin=tin, bar=bar, lv=last_vneg[b]):
                        e.bar(bar)
                        e.w(tin, lv)
                        e.inc(e.E.tensor_mul(sq[:], xt[b][:], xt[b][:]), "cV")
                        e.w(e.tok("cV"))
                        e.inc(e.E.reduce_sum(s_[:, 1:2], sq[:], axis=AX.X), "cV")
                    k.run("vector", v_s2)
                    t_s2 = ("cV", k.cnt["cV"])

                    def v_mid(e, s_=s_, t_s1=t_s1, t_s2=t_s2):
                        e.w(t_s1, t_s2)
                        e.inc(e.E.tensor_scalar_mul(s_[:, 2:3], s_[:, 0:1], 1.0 / D), "cV")
                        e.inc(e.E.tensor_scalar_mul(s_[:, 3:4], s_[:, 1:2], 1.0 / D), "cV")
                        e.w(e.tok("cV"))
                        e.inc(e.E.tensor_mul(s_[:, 4:5], s_[:, 2:3], s_[:, 2:3]), "cV")
                        e.w(e.tok("cV"))
                        e.inc(e.E.tensor_sub(s_[:, 3:4], s_[:, 3:4], s_[:, 4:5]), "cV")
                    k.run("vector", v_mid)
                    t_var = ("cV", k.cnt["cV"])

                    def s_mid(e, s_=s_, t_var=t_var):
                        e.w(t_var)
                        e.inc(e.E.activation(s_[:, 5:6], s_[:, 3:4], AF.Ln,
                                             bias=C_EPS), "cS")
                        e.w(e.tok("cS"))
                        e.inc(e.E.activation(s_[:, 5:6], s_[:, 5:6], AF.Exp,
                                             scale=C_MH), "cS")
                    k.run("scalar", s_mid)
                    t_rstd = ("cS", k.cnt["cS"])

                    def v_neg(e, s_=s_, t_rstd=t_rstd):
                        e.w(t_rstd)
                        e.inc(e.E.tensor_scalar(s_[:, 6:7], s_[:, 2:3], s_[:, 5:6],
                                                -1.0, ALU.mult, ALU.mult), "cV")
                    k.run("vector", v_neg)
                    last_vneg[b] = ("cV", k.cnt["cV"])

                    def v_hn(e, b=b, s_=s_, tn=last_vneg[b], hs=hn_stored[b]):
                        e.w(tn, hs)
                        e.inc(e.E.tensor_scalar(hnt[b][:], xt[b][:], s_[:, 5:6],
                                                s_[:, 6:7], ALU.mult, ALU.add), "cV")
                    k.run("vector", v_hn)
                    hn_done[b] = ("cV", k.cnt["cV"])

                    def g_st(e, b=b, sc=sc, hd=hn_done[b]):
                        e.w(hd)
                        e.inc(e.E.dma_start(hn_h[sc * 128:(sc + 1) * 128, :],
                                            hnt[b][:]), "gS%d" % b, 16)
                    k.run("gpsimd", g_st)
                    hn_stored[b] = ("gS%d" % b, k.cnt["gS%d" % b])
            return (("gS0", k.cnt["gS0"]), ("gS1", k.cnt["gS1"]))

        def load_hnf(hnf, tok_hn, free_tok, bar):
            def s_ld(e, bar=bar):
                e.bar(bar)
                e.w(tok_hn, free_tok)
                for c in range(8):
                    e.inc(e.E.dma_start_transpose(
                        hnf[:, c, :], hn_h[:, c * 128:(c + 1) * 128]), "sT", 16)
            k.run("sync", s_ld)
            return ("sT", k.cnt["sT"])

        def emit_sel(hnf, g_all, tok_hnf, bar, ps):
            if True:
                selT = ps.enter_context(nc.sbuf_tensor(_nm(), [128, 8, 128], BF16))
                lg = [ps.enter_context(nc.psum_tensor(_nm("ps"), [128, 128], F32)) for _ in range(2)]
                pr = ps.enter_context(nc.sbuf_tensor(_nm(), [128, 16, 8], F32))
                mr = ps.enter_context(nc.sbuf_tensor(_nm(), [128, 16, 8], F32))
                vv = ps.enter_context(nc.sbuf_tensor(_nm(), [128, 16, 4], F32))

                def s_w(e, bar=bar):
                    e.bar(bar)
                    e.w(*prep_toks)
                    e.inc(e.E.dma_start(
                        selT[:], p_selT.rearrange("(c p) r -> p c r", p=128)), "sC", 16)
                k.run("sync", s_w)
                t_selT = ("sC", k.cnt["sC"])

                def s_sig_tbl(e):
                    preload_tbl(e, ID_SIG)
                k.run("scalar", s_sig_tbl)
                sig_done = [None, None]
                gate_done = None
                for sc in range(NS):
                    b = sc % 2

                    def t_mm(e, sc=sc, b=b, t_selT=t_selT, tok_hnf=tok_hnf,
                             sd=sig_done[b], bar=bar):
                        e.bar(bar)
                        e.w(t_selT, tok_hnf, sd)
                        ins = None
                        for d in range(8):
                            ins = e.E.matmul(lg[b][:],
                                             hnf[:, d, sc * 128:(sc + 1) * 128],
                                             selT[:, d, :],
                                             start=(d == 0), stop=(d == 7))
                        e.inc(ins, "cT")
                    k.run("tensor", t_mm)
                    t_lg = ("cT", k.cnt["cT"])

                    def s_sig(e, b=b, t_lg=t_lg, gd=gate_done, bar=bar):
                        e.bar(bar)
                        e.w(t_lg, gd)
                        e.inc(e.E.activation(pr[:].rearrange("p a b -> p (a b)"),
                                             lg[b][:], AF.Sigmoid), "cS")
                    k.run("scalar", s_sig)
                    sig_done[b] = ("cS", k.cnt["cS"])

                    def v_top(e, sc=sc, sd=sig_done[b], bar=bar):
                        e.bar(bar)
                        e.w(sd)
                        e.inc(e.E.reduce_max(vv[:, :, 0:1].squeeze(2), pr[:],
                                             axis=AX.X), "cV")
                        e.w(e.tok("cV"))
                        e.inc(e.E.tensor_tensor(mr[:], pr[:],
                                                vv[:, :, 0:1].broadcast_to([128, 16, 8]),
                                                ALU.is_ge), "cV")
                        e.w(e.tok("cV"))
                        e.inc(e.E.scalar_tensor_tensor(mr[:], mr[:], -1e9, pr[:],
                                                       ALU.mult, ALU.add), "cV")
                        e.w(e.tok("cV"))
                        e.inc(e.E.reduce_max(vv[:, :, 1:2].squeeze(2), mr[:],
                                             axis=AX.X), "cV")
                        e.w(e.tok("cV"))
                        e.inc(e.E.tensor_add(vv[:, :, 2:3], vv[:, :, 0:1],
                                             vv[:, :, 1:2]), "cV")
                        e.w(e.tok("cV"))
                        e.inc(e.E.tensor_scalar_max(vv[:, :, 2:3], vv[:, :, 2:3],
                                                    1e-9), "cV")
                        e.w(e.tok("cV"))
                        e.inc(e.E.reciprocal(vv[:, :, 2:3], vv[:, :, 2:3]), "cV")
                        e.w(e.tok("cV"))
                        e.inc(e.E.tensor_scalar(vv[:, 0:8, 3:4], vv[:, 0:8, 2:3],
                                                rs_v[:], None, ALU.mult), "cV")
                        e.inc(e.E.tensor_scalar(vv[:, 8:16, 3:4], vv[:, 8:16, 2:3],
                                                rs_o[:], None, ALU.mult), "cV")
                        e.w(e.tok("cV"))
                        e.inc(e.E.tensor_tensor(mr[:], pr[:],
                                                vv[:, :, 1:2].broadcast_to([128, 16, 8]),
                                                ALU.is_ge), "cV")
                        e.w(e.tok("cV"))
                        e.inc(e.E.tensor_mul(mr[:], mr[:], pr[:]), "cV")
                        e.w(e.tok("cV"))
                        e.inc(e.E.tensor_tensor(
                            g_all[:, sc, :].rearrange("p (a b) -> p a b", a=16),
                            mr[:], vv[:, :, 3:4].broadcast_to([128, 16, 8]),
                            ALU.mult), "cV")
                    k.run("vector", v_top)
                    gate_done = ("cV", k.cnt["cV"])

                def s_nl_tbl(e):
                    preload_tbl(e, ID_NLEXP)
                k.run("scalar", s_nl_tbl)
            return gate_done

        def emit_ffn_qk(hnf, tok_hnf, bar, ps):
            if True:
                wbuf = [ps.enter_context(nc.sbuf_tensor(_nm(), [128, 8, 128], BF16))
                        for _ in range(2)]
                qp = [ps.enter_context(nc.psum_tensor(_nm("ps"), [128, 512], F32)) for _ in range(2)]
                stg = [ps.enter_context(nc.sbuf_tensor(_nm(), [128, S], BF16)) for _ in range(2)]
                wfree = [None, None]
                evtok = [None, None]
                sttok = [None, None]
                NSQ = S // 512
                for part, (wsrc, wdst) in enumerate(((p_qwT, fq_h), (p_kwT, fk_h))):
                    for pc in range(8):
                        wb = (part * 8 + pc) % 2

                        def s_w(e, wsrc=wsrc, pc=pc, wb=wb, wf=wfree[wb], bar=bar):
                            e.bar(bar)
                            e.w(*prep_toks, wf)
                            e.inc(e.E.dma_start(
                                wbuf[wb][:],
                                wsrc.rearrange("(c p) r -> p c r", p=128)
                                [:, :, pc * 128:(pc + 1) * 128]), "sW%d" % wb, 16)
                        k.run("sync", s_w)
                        t_w = ("sW%d" % wb, k.cnt["sW%d" % wb])

                        for scq in range(NSQ):
                            b = scq % 2

                            def t_mm(e, wb=wb, b=b, scq=scq, t_w=t_w,
                                     tok_hnf=tok_hnf, pf=evtok[b], bar=bar):
                                e.bar(bar)
                                e.w(t_w, tok_hnf, pf)
                                ins = None
                                for d in range(8):
                                    ins = e.E.matmul(
                                        qp[b][:], wbuf[wb][:, d, :],
                                        hnf[:, d, scq * 512:(scq + 1) * 512],
                                        start=(d == 0), stop=(d == 7))
                                e.inc(ins, "cT")
                            k.run("tensor", t_mm)
                            t_mm_t = ("cT", k.cnt["cT"])
                            if scq == NSQ - 1:
                                wfree[wb] = t_mm_t

                            wb2 = (part * 8 + pc) % 2

                            def s_ev(e, b=b, wb2=wb2, scq=scq, t_mm_t=t_mm_t,
                                     sf=(sttok[wb2] if scq == 0 else None), bar=bar):
                                e.bar(bar)
                                e.w(t_mm_t, sf)
                                e.inc(e.E.activation(
                                    stg[wb2][:, scq * 512:(scq + 1) * 512],
                                    qp[b][:], AF.Copy), "cS")
                            k.run("scalar", s_ev)
                            evtok[b] = ("cS", k.cnt["cS"])
                            if scq == NSQ - 1:
                                def g_st(e, wb2=wb2, pc=pc, wdst=wdst, ev=evtok[b]):
                                    e.w(ev)
                                    e.inc(e.E.dma_start(wdst[pc], stg[wb2][:]),
                                          "gS%d" % wb2, 16)
                                k.run("gpsimd", g_st)
                                sttok[wb2] = ("gS%d" % wb2, k.cnt["gS%d" % wb2])
            return (("gS0", k.cnt["gS0"]), ("gS1", k.cnt["gS1"]))

        def emit_ffn_v(hnf, g_all, tok_hnf, tok_gates, bar, ps):
            if True:
                vwb = [ps.enter_context(nc.sbuf_tensor(_nm(), [128, 8, 2, 512], BF16))
                       for _ in range(2)]
                xv = [ps.enter_context(nc.psum_tensor(_nm("ps"), [128, D], F32)) for _ in range(2)]
                vacc = ps.enter_context(nc.sbuf_tensor(_nm(), [128, FP], F32))
                vst = [ps.enter_context(nc.sbuf_tensor(_nm(), [128, NS, FP], BF16))
                       for _ in range(2)]
                vw_free = [None, None]
                vst_free = [None, None]
                xv_free = [None, None]
                vacc_free = None
                for h in range(FH):
                    hb = h % 2

                    def g_w(e, h=h, hb=hb, vf=vw_free[hb], bar=bar):
                        e.bar(bar)
                        e.w(*prep_toks2, vf)
                        e.inc(e.E.dma_start(
                            vwb[hb][:],
                            p_vwb[h].rearrange("c p e f -> p c e f")),
                            "gV%d" % hb, 16)
                    k.run("sync", g_w)
                    t_vw = ("gV%d" % hb, k.cnt["gV%d" % hb])

                    for sc in range(NS):
                        b = sc % 2

                        def t_mm(e, hb=hb, b=b, sc=sc, t_vw=t_vw, tok_hnf=tok_hnf,
                                 xf=xv_free[b], bar=bar):
                            e.bar(bar)
                            e.w(t_vw, tok_hnf, xf)
                            for eh in range(2):
                                ins = None
                                for dd in range(8):
                                    ins = e.E.matmul(
                                        xv[b][:, eh * 512:(eh + 1) * 512],
                                        hnf[:, dd, sc * 128:(sc + 1) * 128],
                                        vwb[hb][:, dd, eh, :],
                                        start=(dd == 0), stop=(dd == 7))
                                e.inc(ins, "cT")
                        k.run("tensor", t_mm)
                        t_xv = ("cT", k.cnt["cT"])

                        def v_g(e, h=h, b=b, sc=sc, t_xv=t_xv, vaf=vacc_free,
                                tg=tok_gates, bar=bar):
                            e.bar(bar)
                            e.w(t_xv, vaf, tg)
                            for ei in range(8):
                                gcol = g_all[:, sc, h * 8 + ei:h * 8 + ei + 1]
                                if ei == 0:
                                    e.inc(e.E.tensor_scalar(
                                        vacc[:], xv[b][:, 0:128], gcol, None,
                                        ALU.mult), "cV")
                                else:
                                    e.inc(e.E.scalar_tensor_tensor(
                                        vacc[:], xv[b][:, ei * 128:(ei + 1) * 128],
                                        gcol, vacc[:], ALU.mult, ALU.add), "cV")
                                e.w(e.tok("cV"))
                        k.run("vector", v_g)
                        xv_free[b] = ("cV", k.cnt["cV"])

                        def s_ev(e, hb=hb, sc=sc, tg=xv_free[b], vf=vst_free[hb],
                                 bar=bar):
                            e.bar(bar)
                            e.w(tg, vf)
                            e.inc(e.E.activation(vst[hb][:, sc, :], vacc[:],
                                                 AF.Copy), "cS")
                        k.run("scalar", s_ev)
                        vacc_free = ("cS", k.cnt["cS"])
                    vw_free[hb] = ("cT", k.cnt["cT"])

                    def g_st(e, h=h, hb=hb, tv=vacc_free):
                        e.w(tv)
                        e.inc(e.E.dma_start(
                            fv_h[h].rearrange("(sc p) c -> p sc c", p=128),
                            vst[hb][:]), "gS%d" % hb, 16)
                    k.run("gpsimd", g_st)
                    vst_free[hb] = ("gS%d" % hb, k.cnt["gS%d" % hb])
            return (("gS0", k.cnt["gS0"]), ("gS1", k.cnt["gS1"]))

        def make_resid(ps):
            B = {
                "xr": [ps.enter_context(nc.sbuf_tensor(_nm(), [128, D], F32))
                       for _ in range(2)],
                "xo": [ps.enter_context(nc.sbuf_tensor(_nm(), [128, D], F32))
                       for _ in range(2)],
                "sq": ps.enter_context(nc.sbuf_tensor(_nm(), [128, D], F32)),
                "st": ps.enter_context(nc.sbuf_tensor(_nm(), [128, 2, 8], F32)),
                "hnt": [ps.enter_context(nc.sbuf_tensor(_nm(), [128, D], BF16))
                        for _ in range(2)],
                "hn_done": [None, None],
                "sq_free": None,
                "res_done": None,
            }
            return B

        def resid_pieces(B, src, dst, sup, oacc, bar, fuse_hn, st_):
            out = []
            for i8 in range(SUP // 128):
                out.append(lambda i=i8: emit_resid_tile(B, src, dst, sup, i,
                                                        oacc, bar, fuse_hn, st_))
            return out

        def emit_resid_tile(B, src, dst, sup, i8, oacc, bar, fuse_hn, st_):
            xr, xo, sqf, stf, hnt = B["xr"], B["xo"], B["sq"], B["st"], B["hnt"]
            out_evac = st_["oe_key"]()
            res_done = B.get("res_done")
            if True:
                rb = i8 % 2
                row = sup * SUP + i8 * 128
                s_ = stf[:, rb, :]

                def s_ld(e, rb=rb, row=row, oe=out_evac, rd=res_done, bar=bar,
                         hd=B["hn_done"][rb]):
                    e.bar(bar)
                    e.w(oe, rd, hd)
                    e.inc(e.E.dma_start(xr[rb][:], src[row:row + 128, :]),
                          "sR%d" % rb, 16)
                k.run("sync", s_ld)
                t_x = ("sR%d" % rb, k.cnt["sR%d" % rb])

                def v_add(e, rb=rb, i8=i8, t_x=t_x, oe=out_evac, rd=res_done,
                          hd=B["hn_done"][rb]):
                    e.w(t_x, oe, rd, hd)
                    e.inc(e.E.tensor_add(xo[rb][:], oacc[:, i8, :], xr[rb][:]),
                          "cV")
                k.run("vector", v_add)
                t_add = ("cV", k.cnt["cV"])

                def g_st(e, rb=rb, row=row, t_add=t_add, dst=dst):
                    e.w(t_add)
                    e.inc(e.E.dma_start(dst[row:row + 128, :], xo[rb][:]),
                          "gR%d" % rb, 16)
                k.run("gpsimd", g_st)
                B["res_done"] = (("gR0", k.cnt["gR0"]), ("gR1", k.cnt["gR1"]))

                if fuse_hn:
                    def v_st(e, rb=rb, s_=s_, t_add=t_add, sqf_f=B["sq_free"]):
                        e.w(t_add, sqf_f)
                        e.inc(e.E.tensor_mul(sqf[:], xo[rb][:], xo[rb][:]), "cV")
                        e.inc(e.E.reduce_sum(s_[:, 0:1], xo[rb][:], axis=AX.X), "cV")
                        e.w(e.tok("cV"))
                        e.inc(e.E.reduce_sum(s_[:, 1:2], sqf[:], axis=AX.X), "cV")
                        e.w(e.tok("cV"))
                        e.inc(e.E.tensor_scalar_mul(s_[:, 2:3], s_[:, 0:1], 1.0 / D),
                              "cV")
                        e.inc(e.E.tensor_scalar_mul(s_[:, 3:4], s_[:, 1:2], 1.0 / D),
                              "cV")
                        e.w(e.tok("cV"))
                        e.inc(e.E.tensor_mul(s_[:, 4:5], s_[:, 2:3], s_[:, 2:3]),
                              "cV")
                        e.w(e.tok("cV"))
                        e.inc(e.E.tensor_sub(s_[:, 3:4], s_[:, 3:4], s_[:, 4:5]),
                              "cV")
                    k.run("vector", v_st)
                    t_var = ("cV", k.cnt["cV"])
                    B["sq_free"] = t_var

                    def s_mid(e, s_=s_, t_var=t_var):
                        e.w(t_var)
                        e.inc(e.E.activation(s_[:, 5:6], s_[:, 3:4], AF.Ln,
                                             bias=C_EPS), "cS")
                        e.w(e.tok("cS"))
                        e.inc(e.E.activation(s_[:, 5:6], s_[:, 5:6], AF.Exp,
                                             scale=C_MH), "cS")
                    k.run("scalar", s_mid)
                    t_rstd = ("cS", k.cnt["cS"])

                    def v_hn(e, rb=rb, s_=s_, t_rstd=t_rstd,
                             hd=B["hn_done"][rb]):
                        e.w(t_rstd, hd)
                        e.inc(e.E.tensor_scalar(s_[:, 6:7], s_[:, 2:3], s_[:, 5:6],
                                                -1.0, ALU.mult, ALU.mult), "cV")
                        e.w(e.tok("cV"))
                        e.inc(e.E.tensor_scalar(hnt[rb][:], xo[rb][:], s_[:, 5:6],
                                                s_[:, 6:7], ALU.mult, ALU.add), "cV")
                    k.run("vector", v_hn)
                    t_hn = ("cV", k.cnt["cV"])

                    def g_hn(e, rb=rb, row=row, t_hn=t_hn):
                        e.w(t_hn, ("sT", k.cnt["sT"]))
                        e.inc(e.E.dma_start(hn_h[row:row + 128, :], hnt[rb][:]),
                              "gS%d" % rb, 16)
                    k.run("gpsimd", g_hn)
                    B["hn_done"][rb] = ("gS%d" % rb, k.cnt["gS%d" % rb])

        def emit_ffn_attn(src, dst, g_all, tok_qk, tok_v, tok_gates, bar, fuse_hn):
            with ExitStack() as ps:
                kfb = [ps.enter_context(nc.sbuf_tensor(_nm(), [128, S], BF16)) for _ in range(2)]
                qhb = [ps.enter_context(nc.sbuf_tensor(_nm(), [128, SUP], BF16)) for _ in range(2)]
                vtb = [ps.enter_context(nc.sbuf_tensor(_nm(), [128, NS, FP], BF16))
                       for _ in range(2)]
                owb = [ps.enter_context(nc.sbuf_tensor(_nm(), [128, 8, D], BF16))
                       for _ in range(2)]
                Pt = ps.enter_context(nc.sbuf_tensor(_nm(), [128, NS, 512], BF16))
                Ps2 = ps.enter_context(nc.sbuf_tensor(_nm(), [128, NS // 2, 512], BF16))
                SC = [ps.enter_context(nc.psum_tensor(_nm("ps"), [128, 512], F32)) for _ in range(2)]
                CTX = ps.enter_context(nc.psum_tensor(_nm("ps"), [128, 512], F32))
                LP = ps.enter_context(nc.psum_tensor(_nm("ps"), [128, 512], F32))
                GBp = ps.enter_context(nc.psum_tensor(_nm("ps"), [128, 512], F32))
                GBg = ps.enter_context(nc.psum_tensor(_nm("ps"), [128, 512], F32))
                OUT = GBp
                OUT2 = [ps.enter_context(nc.psum_tensor(_nm("ps"), [128, 512], F32))
                        for _ in range(2)]
                rl_s = ps.enter_context(nc.sbuf_tensor(_nm(), [1, 512], F32))
                rlc = ps.enter_context(nc.sbuf_tensor(_nm(), [128, 4], F32))
                Gp = [ps.enter_context(nc.sbuf_tensor(_nm(), [8, 512], BF16)) for _ in range(2)]
                cs2 = ps.enter_context(nc.sbuf_tensor(_nm(), [128, 512], F32))
                yTs = ps.enter_context(nc.sbuf_tensor(_nm(), [128, 8, 512], BF16))
                oacc = ps.enter_context(nc.sbuf_tensor(_nm(), [128, SUP // 128, D], F32))
                RB = make_resid(ps)

                kf_free = [None, None]
                qh_free = [None, None]
                vt_free = [None, None]
                ow_free = [None, None]
                ld_tok = {}

                def issue_loads(idx, sup, h):
                    b = idx % 2

                    def s_ld(e, sup=sup, h=h, b=b, kf=kf_free[b], qf=qh_free[b],
                             vf=vt_free[b], bar=bar):
                        e.bar(bar)
                        e.w(*tok_qk, *tok_v, kf, qf, vf)
                        e.inc(e.E.dma_start(kfb[b][:], fk_h[h]), "sL%d" % b, 16)
                        e.inc(e.E.dma_start(
                            qhb[b][:], fq_h[h][:, sup * SUP:(sup + 1) * SUP]),
                            "sL%d" % b, 16)
                        e.inc(e.E.dma_start(
                            vtb[b][:],
                            fv_h[h].rearrange("(sc p) c -> p sc c", p=128)),
                            "sL%d" % b, 16)
                    k.run("sync", s_ld)
                    t_s = ("sL%d" % b, k.cnt["sL%d" % b])

                    def g_ow(e, h=h, b=b, of=ow_free[b], bar=bar):
                        e.bar(bar)
                        e.w(*prep_toks2, of)
                        e.inc(e.E.dma_start(
                            owb[b][:], p_owb[h * 8:(h + 1) * 8]
                            .rearrange("ee p c -> p ee c")), "gO%d" % b, 16)
                    k.run("sync", g_ow)
                    ld_tok[idx] = (t_s, ("gO%d" % b, k.cnt["gO%d" % b]))

                st = {
                    "pt_free": None,
                    "ctx_free": None,
                    "l_free": None,
                    "gp_free": [None, None],
                    "gbp_free": None,
                    "gbp_hold": None,
                    "gbp_rl": None,
                    "gbg_free": None,
                    "cs2_free": None,
                    "rlc_free": None,
                    "yts_free": None,
                    "out_free": None,
                    "out2_free": None,
                    "res_done": None,
                    "rl_s_free": None,
                    "ps2_free": None,
                }

                def make_tail(c):
                    (ci, b, sup, h, qc, t_ow, t_recip, t_g4, pvl_toks, owner) = c
                    cb = ci % 2
                    ts = {"y": [], "out_free": None, "rlc": None}

                    def p_rl():
                        def t_rlt(e, t_recip=t_recip, of=st["out_free"],
                                  rf=st["rlc_free"], gb=st["gbp_hold"]):
                            e.w(t_recip, of, rf, gb)
                            ins = None
                            for qsub in range(4):
                                ins = e.E.transpose(
                                    OUT[0:128, 256 + qsub:256 + qsub + 1],
                                    rl_s[0:1, qsub * 128:(qsub + 1) * 128],
                                    identf[0:1, 0:1])
                            e.inc(ins, "cT")
                        k.run("tensor", t_rlt)
                        t_rlt_t = ("cT", k.cnt["cT"])
                        st["gbp_free"] = t_rlt_t

                        def v_rlc(e, t_rlt_t=t_rlt_t):
                            e.w(t_rlt_t)
                            e.inc(e.E.tensor_copy(rlc[:], OUT[0:128, 256:260]), "cV")
                            e.w(e.tok("cV"), pvl_toks[-1])
                            e.inc(e.E.tensor_copy(cs2[:], CTX[:]), "cV")
                        k.run("vector", v_rlc)
                        t_rlc = ("cV", k.cnt["cV"])
                        ts["rlc"] = t_rlc
                        st["rlc_free"] = t_rlc
                        st["ctx_free"] = t_rlc
                        st["cs2_free"] = t_rlc
                        st["out_free"] = t_rlc
                        st["gbp_rl"] = t_rlc

                    def p_ggy(ei):
                        def t_gg(e, ei=ei, cb=cb, gp=t_g4, gf=st["gbg_free"],
                                 yp=(ts["y"][ei - 1] if ei else None)):
                            e.w(gp, gf, yp)
                            e.inc(e.E.matmul(
                                GBg[:], e8c[:, ei * 128:(ei + 1) * 128],
                                Gp[cb][:], start=True, stop=True), "cT")
                        k.run("tensor", t_gg)
                        t_ggt = ("cT", k.cnt["cT"])

                        def v_y(e, ei=ei, t_ggt=t_ggt, yf=st["yts_free"],
                                c2=st["cs2_free"]):
                            e.w(t_ggt, yf, c2)
                            e.inc(e.E.tensor_tensor(
                                yTs[:, ei, :], cs2[:], GBg[:], ALU.mult), "cV")
                        k.run("vector", v_y)
                        ts["y"].append(("cV", k.cnt["cV"]))
                        if ei == 7:
                            st["gbg_free"] = ts["y"][-1]
                            st["gp_free"][cb] = ("cT", k.cnt["cT"])

                    def p_om(qsub, half):
                        of = ts["out_free"] if ts["out_free"] else st["out_free"]
                        ob = (qsub * 2 + half) % 2

                        def t_om(e, qsub=qsub, half=half, ob=ob, t_ow=t_ow, of=of,
                                 t_rlc=ts["rlc"], y_toks=list(ts["y"])):
                            e.w(t_ow, of, t_rlc)
                            ins = None
                            for ei in range(8):
                                e.w(y_toks[ei])
                                ins = e.E.matmul(
                                    OUT2[ob][:],
                                    yTs[:, ei, qsub * 128:(qsub + 1) * 128],
                                    owb[b][:, ei, half * 512:(half + 1) * 512],
                                    start=(ei == 0), stop=(ei == 7))
                            e.inc(ins, "cT")
                        k.run("tensor", t_om)
                        t_out = ("cT", k.cnt["cT"])

                        def ev(e, h=h, qsub=qsub, half=half, ob=ob,
                               sc8=qc * 4 + qsub, t_out=t_out,
                               rd=st["res_done"], t_rlc=ts["rlc"]):
                            e.w(t_out, rd, t_rlc)
                            dstp = oacc[:, sc8, half * 512:(half + 1) * 512]
                            if h == 0:
                                e.inc(e.E.tensor_scalar(
                                    dstp, OUT2[ob][:], rlc[:, qsub:qsub + 1], None,
                                    ALU.mult), "cV")
                            else:
                                e.inc(e.E.scalar_tensor_tensor(
                                    dstp, OUT2[ob][:], rlc[:, qsub:qsub + 1], dstp,
                                    ALU.mult, ALU.add), "cV")
                        k.run("vector", ev)
                        ts["out_free"] = ("cV", k.cnt["cV"])
                        if qsub == 3 and half == 1:
                            st["yts_free"] = ("cT", k.cnt["cT"])
                            st["out2_free"] = ts["out_free"]
                            st["rlc_free"] = ts["out_free"]
                            if owner is not None:
                                ow_free[owner] = ("cT", k.cnt["cT"])

                    pieces = [p_rl]
                    for ei in range(8):
                        pieces.append(lambda ei=ei: p_ggy(ei))
                    for qsub in range(4):
                        for half in range(2):
                            pieces.append(lambda q=qsub, hf=half: p_om(q, hf))
                    return pieces


                pend = []
                issue_loads(0, 0, 0)
                idx = 0
                for sup in range(NSUP):
                    for h in range(FH):
                        b = idx % 2
                        t_ld, t_ow = ld_tok[idx]

                        for qc in range(NQC):
                            ci = idx * NQC + qc
                            cb = ci % 2
                            while pend and pend[0][0] <= ci - 2:
                                pend.pop(0)[1]()
                            scbase = (sup * SUP + qc * QC) // 128

                            def t_go(e, h=h, scbase=scbase, bar=bar,
                                     tg=tok_gates, gf=st["gbp_free"],
                                     gr=st["gbp_rl"], gh=st["gbp_hold"]):
                                e.bar(bar)
                                e.w(tg, gf, gr, gh)
                                ins = None
                                for qsub in range(4):
                                    ins = e.E.transpose(
                                        GBp[0:8, qsub * 128:(qsub + 1) * 128],
                                        g_all[:, scbase + qsub,
                                              64 + h * 8:64 + h * 8 + 8],
                                        identf[:])
                                e.inc(ins, "cT")
                            k.run("tensor", t_go)
                            t_goT = ("cT", k.cnt["cT"])

                            def v_g4(e, cb=cb, t_goT=t_goT, gpf=st["gp_free"][cb]):
                                e.w(t_goT, gpf)
                                e.inc(e.E.tensor_copy(Gp[cb][:], GBp[0:8, :]), "cV")
                            k.run("vector", v_g4)
                            t_g4 = ("cV", k.cnt["cV"])
                            st["gbp_hold"] = t_g4

                            exp_toks = []
                            mm_toks = []
                            pvl_toks = []
                            ps2_toks = []
                            prev_pt = st["pt_free"]
                            prev_ps2 = st["ps2_free"]

                            def emit_sc(kc):
                                sb_ = kc % 2

                                def t_sc(e, b=b, sb_=sb_, kc=kc, qc=qc, t_ld=t_ld,
                                         bar=bar,
                                         ef=(exp_toks[kc - 2] if kc >= 2 else None)):
                                    e.bar(bar)
                                    e.w(t_ld, ef)
                                    e.inc(e.E.matmul(
                                        SC[sb_][:],
                                        kfb[b][:, kc * 128:(kc + 1) * 128],
                                        qhb[b][:, qc * QC:(qc + 1) * QC],
                                        start=True, stop=True), "cT")
                                k.run("tensor", t_sc)
                                mm_toks.append(("cT", k.cnt["cT"]))

                                def s_ex(e, sb_=sb_, kc=kc, mt=mm_toks[kc], bar=bar,
                                         ptf=(prev_pt[kc] if prev_pt else None)):
                                    e.bar(bar)
                                    e.w(mt, ptf)
                                    e.inc(e.E.activation(
                                        Pt[:, kc, :], SC[sb_][:], AF.Exp,
                                        bias=C_EXPB), "cS")
                                k.run("scalar", s_ex)
                                exp_toks.append(("cS", k.cnt["cS"]))

                            def emit_ps2(kc):
                                p2 = kc // 2

                                def g_ps(e, kc=kc, p2=p2, te=exp_toks[kc],
                                         pf=(prev_ps2[p2] if prev_ps2 else None)):
                                    e.w(te, pf)
                                    e.inc(e.E.tensor_add(
                                        Ps2[:, p2, :], Pt[:, kc - 1, :],
                                        Pt[:, kc, :]), "cP")
                                k.run("gpsimd", g_ps)
                                ps2_toks.append(("cP", k.cnt["cP"]))

                            def emit_pvl(kc):
                                p2 = kc // 2

                                def t_pvl(e, b=b, kc=kc, p2=p2, te=exp_toks[kc],
                                          cf=(st["ctx_free"] if kc == 0 else None),
                                          lf=(st["l_free"] if kc == 0 else None),
                                          pst=(ps2_toks[p2] if kc % 2 == 1 else None)):
                                    e.w(te, cf, lf)
                                    ins = e.E.matmul(
                                        CTX[:], vtb[b][:, kc, :], Pt[:, kc, :],
                                        start=(kc == 0), stop=(kc == NS - 1))
                                    if kc % 2 == 1:
                                        e.w(pst)
                                        ins = e.E.matmul(
                                            LP[0:1, :], ones_colb[:], Ps2[:, p2, :],
                                            start=(kc == 1), stop=(kc == NS - 1))
                                    e.inc(ins, "cT")
                                k.run("tensor", t_pvl)
                                pvl_toks.append(("cT", k.cnt["cT"]))

                            for kc in range(NS):
                                emit_sc(kc)
                                if kc % 2 == 1:
                                    emit_ps2(kc)
                                if kc == 1 and pend:
                                    pend.pop(0)[1]()
                                if kc >= 2:
                                    emit_pvl(kc - 2)
                                if kc >= 3 and pend:
                                    pend.pop(0)[1]()
                            emit_pvl(NS - 2)
                            emit_pvl(NS - 1)

                            def v_recip(e, pl=pvl_toks[-1], rsf=st.get("rl_s_free")):
                                e.w(pl, rsf)
                                e.inc(e.E.reciprocal(rl_s[:], LP[0:1, :]), "cV")
                            k.run("vector", v_recip)
                            t_recip = ("cV", k.cnt["cV"])
                            st["l_free"] = t_recip
                            st["rl_s_free"] = None

                            st["pt_free"] = pvl_toks
                            st["ps2_free"] = [pvl_toks[2 * p + 1]
                                              for p in range(NS // 2)]
                            pend.extend((ci, f) for f in make_tail(
                                (ci, b, sup, h, qc, t_ow, t_recip, t_g4,
                                 pvl_toks, b if qc == NQC - 1 else None)))

                        kf_free[b] = mm_toks[-1]
                        qh_free[b] = mm_toks[-1]
                        vt_free[b] = pvl_toks[-1]
                        while pend and pend[0][0] <= idx * NQC - 1:
                            pend.pop(0)[1]()
                        if idx + 1 < NSUP * FH:
                            nsup, nh = divmod(idx + 1, FH)
                            issue_loads(idx + 1, nsup, nh)
                        idx += 1

                    pend.extend((idx * NQC - 1, f) for f in resid_pieces(
                        RB, src, dst, sup, oacc, bar, fuse_hn,
                        {"oe_key": (lambda: st["out2_free"])}))
                while pend:
                    pend.pop(0)[1]()
            return RB["res_done"], (("gS0", k.cnt["gS0"]), ("gS1", k.cnt["gS1"]))


        def ffn_full(src, dst, bar, res_tok, hn_in, fuse_next):
            if hn_in is None:
                tok_hn = ln_phase(src, bar, res_tok)
            else:
                tok_hn = hn_in
            bar1 = k.snap()
            with ExitStack() as ps:
                g_all = ps.enter_context(nc.sbuf_tensor(_nm(), [128, NS, 128], F32))
                with ExitStack() as p1:
                    hnf = p1.enter_context(nc.sbuf_tensor(_nm(), [128, 8, S], BF16))
                    tok_hnf = load_hnf(hnf, tok_hn, None, bar1)
                    tok_gates = emit_sel(hnf, g_all, tok_hnf, bar1, p1)
                    tok_qk = emit_ffn_qk(hnf, tok_hnf, bar1, p1)
                    tok_v = emit_ffn_v(hnf, g_all, tok_hnf, tok_gates, bar1, p1)
                bar4 = k.snap()
                tok_res, hn_out = emit_ffn_attn(src, dst, g_all, tok_qk, tok_v,
                                                tok_gates, bar4, fuse_next)
            return k.snap(), tok_res, hn_out

        def emit_attn_qkv(tok_hnf, hnf, bar):
            with ExitStack() as ps:
                cosT = ps.enter_context(nc.sbuf_tensor(_nm(), [128, NS, 64], F32))
                s1T = ps.enter_context(nc.sbuf_tensor(_nm(), [128, NS, 64], F32))
                wqb = [ps.enter_context(nc.sbuf_tensor(_nm(), [128, 8, 512], BF16))
                       for _ in range(2)]
                qp = [ps.enter_context(nc.psum_tensor(_nm("ps"), [128, 512], F32)) for _ in range(2)]
                t1 = ps.enter_context(nc.sbuf_tensor(_nm(), [128, 8, 64], F32))
                t2 = ps.enter_context(nc.sbuf_tensor(_nm(), [128, 8, 64], F32))
                stg = [ps.enter_context(nc.sbuf_tensor(_nm(), [128, 512], BF16)) for _ in range(2)]

                def s_tbl(e, bar=bar):
                    e.bar(bar)
                    e.inc(e.E.dma_start(
                        cosT[:], cos_in.rearrange("(sc p) d -> p sc d", p=128)), "sC", 16)
                    e.inc(e.E.dma_start(
                        s1T[:], s1_in.rearrange("(sc p) d -> p sc d", p=128)), "sC", 16)
                k.run("sync", s_tbl)
                t_tbl = ("sC", k.cnt["sC"])

                wfree = [None, None]
                evtok = [None, None]
                sttok = [None, None]
                rope_busy = None
                for cc in range(6):
                    wb = cc % 2
                    is_v = cc >= 4

                    def s_w(e, cc=cc, wb=wb, wf=wfree[wb], bar=bar):
                        e.bar(bar)
                        e.w(*prep_toks, wf)
                        e.inc(e.E.dma_start(
                            wqb[wb][:],
                            p_wqkvT.rearrange("(c p) r -> p c r", p=128)
                            [:, :, cc * 512:(cc + 1) * 512]), "sW%d" % wb, 16)
                    k.run("sync", s_w)
                    t_w = ("sW%d" % wb, k.cnt["sW%d" % wb])

                    for sc in range(NS):
                        b = sc % 2

                        def t_mm(e, wb=wb, b=b, sc=sc, t_w=t_w, tok_hnf=tok_hnf,
                                 pf=evtok[b], bar=bar):
                            e.bar(bar)
                            e.w(t_w, tok_hnf, pf)
                            ins = None
                            for d in range(8):
                                ins = e.E.matmul(
                                    qp[b][:], hnf[:, d, sc * 128:(sc + 1) * 128],
                                    wqb[wb][:, d, :], start=(d == 0), stop=(d == 7))
                            e.inc(ins, "cT")
                        k.run("tensor", t_mm)
                        t_mm_t = ("cT", k.cnt["cT"])
                        if sc == NS - 1:
                            wfree[wb] = t_mm_t

                        if is_v:
                            def s_ev(e, b=b, t_mm_t=t_mm_t, sf=sttok[b], bar=bar):
                                e.bar(bar)
                                e.w(t_mm_t, sf)
                                e.inc(e.E.activation(stg[b][:], qp[b][:], AF.Copy),
                                      "cS")
                            k.run("scalar", s_ev)
                            evtok[b] = ("cS", k.cnt["cS"])
                        else:
                            def v_rope(e, b=b, sc=sc, t_mm_t=t_mm_t, t_tbl=t_tbl,
                                       sf=sttok[b], bar=bar, rbusy=rope_busy):
                                e.bar(bar)
                                e.w(t_mm_t, t_tbl, sf, rbusy)
                                qv = qp[b][:].rearrange("p (a b) -> p a b", a=8)
                                e.inc(e.E.tensor_tensor(
                                    t1[:], qv,
                                    cosT[:, sc:sc + 1, :].broadcast_to([128, 8, 64]),
                                    ALU.mult), "cV")
                                e.inc(e.E.tensor_tensor(
                                    t2[:, :, 0:32], qv[:, :, 32:64],
                                    s1T[:, sc:sc + 1, 0:32].broadcast_to([128, 8, 32]),
                                    ALU.mult), "cV")
                                e.inc(e.E.tensor_tensor(
                                    t2[:, :, 32:64], qv[:, :, 0:32],
                                    s1T[:, sc:sc + 1, 32:64].broadcast_to([128, 8, 32]),
                                    ALU.mult), "cV")
                                e.w(e.tok("cV"))
                                e.inc(e.E.tensor_tensor(
                                    stg[b][:].rearrange("p (a b) -> p a b", a=8),
                                    t1[:], t2[:], ALU.add), "cV")
                            k.run("vector", v_rope)
                            rope_busy = ("cV", k.cnt["cV"])
                            evtok[b] = rope_busy

                        def g_st(e, b=b, cc=cc, sc=sc, ev=evtok[b]):
                            e.w(ev)
                            dsts = (aq_h, ak_h, av_h)[cc // 2]
                            co = (cc % 2) * 512
                            e.inc(e.E.dma_start(
                                dsts[sc * 128:(sc + 1) * 128, co:co + 512],
                                stg[b][:]), "gS%d" % b, 16)
                        k.run("gpsimd", g_st)
                        sttok[b] = ("gS%d" % b, k.cnt["gS%d" % b])
            return (("gS0", k.cnt["gS0"]), ("gS1", k.cnt["gS1"]))

        def emit_attn_attn(src, dst, tok_qkv, bar, fuse_hn):
            with ExitStack() as ps:
                kpb = [ps.enter_context(nc.sbuf_tensor(_nm(), [128, S], BF16)) for _ in range(2)]
                qpb = [ps.enter_context(nc.sbuf_tensor(_nm(), [128, SUP], BF16)) for _ in range(2)]
                vab = [[ps.enter_context(nc.sbuf_tensor(_nm(), [128, NS, 65], BF16))
                        for _ in range(2)] for _ in range(2)]
                woT = ps.enter_context(nc.sbuf_tensor(_nm(), [128, 8, D], BF16))
                ctxf = [ps.enter_context(nc.sbuf_tensor(_nm(), [128, SUP], BF16))
                        for _ in range(8)]
                Pt2 = [ps.enter_context(nc.sbuf_tensor(_nm(), [128, NS, 512], BF16))
                       for _ in range(2)]
                SCa = [ps.enter_context(nc.psum_tensor(_nm("ps"), [128, 1024], F32))
                       for _ in range(2)]
                CTXA = [ps.enter_context(nc.psum_tensor(_nm("ps"), [65, 512], F32))
                        for _ in range(2)]
                RLP = ps.enter_context(nc.psum_tensor(_nm("ps"), [64, 512], F32))
                OUT = ps.enter_context(nc.psum_tensor(_nm("ps"), [128, 512], F32))
                rla = [ps.enter_context(nc.sbuf_tensor(_nm(), [1, 512], F32))
                       for _ in range(2)]
                cs_ = ps.enter_context(nc.sbuf_tensor(_nm(), [64, 512], F32))
                oacc = ps.enter_context(nc.sbuf_tensor(_nm(), [128, SUP // 128, D], F32))
                RB = make_resid(ps)

                def v_ones(e, bar=bar):
                    e.bar(bar)
                    for bb in range(2):
                        for hh in range(2):
                            e.inc(e.E.memset(vab[bb][hh][:, :, 64:65], 1.0), "cV")
                k.run("vector", v_ones)
                t_ones = ("cV", k.cnt["cV"])

                def s_wo(e, bar=bar):
                    e.bar(bar)
                    e.w(*prep_toks)
                    e.inc(e.E.dma_start(
                        woT[:], p_woutT.rearrange("(c p) r -> p c r", p=128)), "sC", 16)
                k.run("sync", s_wo)
                t_wo = ("sC", k.cnt["sC"])

                kp_free = [None, None]
                qp_free = [None, None]
                va_free = [None, None]
                ld_tok = {}

                def issue_loads(idx, sup, pair):
                    b = idx % 2

                    def s_ld(e, sup=sup, pair=pair, b=b, kf=kp_free[b],
                             qf=qp_free[b], vf=va_free[b], t_ones=t_ones, bar=bar):
                        e.bar(bar)
                        e.w(*tok_qkv, t_ones, kf, qf, vf)
                        e.inc(e.E.dma_start_transpose(
                            kpb[b][:], ak_h[:, pair * 128:(pair + 1) * 128]),
                            "sL%d" % b, 16)
                        e.inc(e.E.dma_start_transpose(
                            qpb[b][:], aq_h[sup * SUP:(sup + 1) * SUP,
                                            pair * 128:(pair + 1) * 128]),
                            "sL%d" % b, 16)
                        for hh in range(2):
                            hcol = (pair * 2 + hh) * 64
                            e.inc(e.E.dma_start(
                                vab[b][hh][:, :, 0:64],
                                av_h[:, hcol:hcol + 64]
                                .rearrange("(sc p) c -> p sc c", p=128)),
                                "sL%d" % b, 16)
                    k.run("sync", s_ld)
                    ld_tok[idx] = ("sL%d" % b, k.cnt["sL%d" % b])

                st = {
                    "pt_free": [None, None],
                    "ctx_free": [None, None],
                    "rlp_free": None,
                    "cs_free": None,
                    "rla_free": [None, None],
                    "out_free": None,
                    "res_done": None,
                }
                ctxf_free = [None] * 8

                def make_tail(c):
                    (b, sup, pair, qc, recips, pv_last) = c
                    ts = {}

                    def p_hh(hh):
                        def t_rlp(e, hh=hh, tr=recips[hh], rf=st["rlp_free"]):
                            e.w(tr, rf)
                            e.inc(e.E.matmul(RLP[:], ones_rowf[:, 0:64],
                                             rla[hh][:], start=True, stop=True),
                                  "cT")
                        k.run("tensor", t_rlp)
                        t_rlp_t = ("cT", k.cnt["cT"])
                        st["rla_free"][hh] = t_rlp_t

                        def v_cs(e, hh=hh, pl=pv_last[hh], cf=st["cs_free"]):
                            e.w(pl, cf)
                            e.inc(e.E.tensor_copy(cs_[:], CTXA[hh][0:64, :]), "cV")
                        k.run("vector", v_cs)
                        t_cs = ("cV", k.cnt["cV"])
                        st["ctx_free"][hh] = t_cs

                        def v_n(e, pair=pair, hh=hh, qc=qc, t_cs=t_cs,
                                t_rlp_t=t_rlp_t, cff=ctxf_free[pair]):
                            e.w(t_cs, t_rlp_t, cff)
                            e.inc(e.E.tensor_tensor(
                                ctxf[pair][hh * 64:(hh + 1) * 64,
                                           qc * QC:(qc + 1) * QC],
                                cs_[:], RLP[:], ALU.mult), "cV")
                        k.run("vector", v_n)
                        t_n = ("cV", k.cnt["cV"])
                        st["cs_free"] = t_n
                        st["rlp_free"] = t_n
                        ts[hh] = t_n

                    return [lambda: p_hh(0), lambda: p_hh(1)], ts

                pend = []
                issue_loads(0, 0, 0)
                idx = 0
                for sup in range(NSUP):
                    for pair in range(8):
                        b = idx % 2
                        t_ld = ld_tok[idx]

                        for qc in range(NQC):
                            aci = idx * NQC + qc
                            while pend and pend[0][0] <= aci - 2:
                                pend.pop(0)[1]()
                            exp_toks = [[], []]
                            mm_toks = [[], []]
                            pv_toks = [[], []]
                            prev_pt = [st["pt_free"][0], st["pt_free"][1]]

                            def emit_sc(p, hh):
                                base = hh * 64

                                def t_sc(e, b=b, p=p, hh=hh, base=base, qc=qc,
                                         t_ld=t_ld, bar=bar,
                                         ef=(exp_toks[hh][p - 1] if p >= 1 else None)):
                                    e.bar(bar)
                                    e.w(t_ld, ef)
                                    ins = None
                                    for half in range(2):
                                        kc = 2 * p + half
                                        ins = e.E.matmul(
                                            SCa[hh][:, half * 512:(half + 1) * 512],
                                            kpb[b][base:base + 64,
                                                   kc * 128:(kc + 1) * 128],
                                            qpb[b][base:base + 64,
                                                   qc * QC:(qc + 1) * QC],
                                            start=True, stop=True)
                                    e.inc(ins, "cT")
                                k.run("tensor", t_sc)
                                mm_toks[hh].append(("cT", k.cnt["cT"]))

                                def s_ex(e, p=p, hh=hh, mt=mm_toks[hh][p], bar=bar,
                                         ptf=(prev_pt[hh][p] if prev_pt[hh] else None)):
                                    e.bar(bar)
                                    e.w(mt, ptf)
                                    e.inc(e.E.activation(
                                        Pt2[hh][:, 2 * p:2 * p + 2, :]
                                        .rearrange("p a b -> p (a b)"),
                                        SCa[hh][:], AF.Exp, bias=C_EXPB), "cS")
                                k.run("scalar", s_ex)
                                exp_toks[hh].append(("cS", k.cnt["cS"]))

                            def emit_pv(p, hh):
                                def t_pv(e, b=b, p=p, hh=hh, te=exp_toks[hh][p],
                                         cf=(st["ctx_free"][hh] if p == 0 else None)):
                                    e.w(te, cf)
                                    ins = None
                                    for kc in (2 * p, 2 * p + 1):
                                        ins = e.E.matmul(
                                            CTXA[hh][:], vab[b][hh][:, kc, :],
                                            Pt2[hh][:, kc, :],
                                            start=(kc == 0), stop=(kc == NS - 1))
                                    e.inc(ins, "cT")
                                k.run("tensor", t_pv)
                                pv_toks[hh].append(("cT", k.cnt["cT"]))

                            for p in range(NS // 2):
                                emit_sc(p, 0)
                                emit_sc(p, 1)
                                if p <= 1 and pend:
                                    pend.pop(0)[1]()
                                if p >= 2:
                                    emit_pv(p - 2, 0)
                                    emit_pv(p - 2, 1)

                            for pp in (NS // 2 - 2, NS // 2 - 1):
                                emit_pv(pp, 0)
                                emit_pv(pp, 1)
                            while pend:
                                pend.pop(0)[1]()

                            recips = []
                            for hh in range(2):
                                def v_recip(e, hh=hh, pl=pv_toks[hh][-1],
                                            rf=st["rla_free"][hh]):
                                    e.w(pl, rf)
                                    e.inc(e.E.reciprocal(rla[hh][:],
                                                         CTXA[hh][64:65, :]), "cV")
                                k.run("vector", v_recip)
                                recips.append(("cV", k.cnt["cV"]))
                            st["pt_free"] = [pv_toks[0], pv_toks[1]]
                            pieces, _ = make_tail((b, sup, pair, qc, recips,
                                                   [pv_toks[0][-1], pv_toks[1][-1]]))
                            pend.extend((aci, f) for f in pieces)

                        kp_free[b] = mm_toks[1][-1]
                        qp_free[b] = mm_toks[1][-1]
                        va_free[b] = pv_toks[1][-1]
                        while pend and pend[0][0] <= idx * NQC - 1:
                            pend.pop(0)[1]()
                        if idx + 1 < NSUP * 8:
                            nsup, npair = divmod(idx + 1, 8)
                            issue_loads(idx + 1, nsup, npair)
                        idx += 1

                    def make_wout(wsup):
                        def piece(qc, qsub, half):
                            def t_wom(e, qc=qc, qsub=qsub, half=half, t_wo=t_wo):
                                e.w(("cV", k.cnt["cV"]), t_wo, st["out_free"])
                                ins = None
                                for pc in range(8):
                                    ins = e.E.matmul(
                                        OUT[:],
                                        ctxf[pc][:, qc * QC + qsub * 128:
                                                 qc * QC + (qsub + 1) * 128],
                                        woT[:, pc, half * 512:(half + 1) * 512],
                                        start=(pc == 0), stop=(pc == 7))
                                e.inc(ins, "cT")
                            k.run("tensor", t_wom)
                            t_o = ("cT", k.cnt["cT"])

                            def v_ev(e, qc=qc, qsub=qsub, half=half, t_o=t_o,
                                     rd=RB["res_done"]):
                                e.w(t_o, rd)
                                e.inc(e.E.tensor_copy(
                                    oacc[:, qc * 4 + qsub,
                                         half * 512:(half + 1) * 512],
                                    OUT[:]), "cV")
                            k.run("vector", v_ev)
                            st["out_free"] = ("cV", k.cnt["cV"])
                            if qc == NQC - 1 and qsub == 3 and half == 1:
                                for pc in range(8):
                                    ctxf_free[pc] = ("cT", k.cnt["cT"])
                        ps_ = []
                        for qc in range(NQC):
                            for qsub in range(4):
                                for half in range(2):
                                    ps_.append(lambda q=qc, s=qsub, hf=half:
                                               piece(q, s, hf))
                        return ps_
                    aci_end = idx * NQC - 1
                    pend.extend((aci_end, f) for f in make_wout(sup))
                    pend.extend((aci_end, f) for f in resid_pieces(
                        RB, src, dst, sup, oacc, bar, fuse_hn,
                        {"oe_key": (lambda: st["out_free"])}))
                while pend:
                    pend.pop(0)[1]()
            return RB["res_done"], (("gS0", k.cnt["gS0"]), ("gS1", k.cnt["gS1"]))


        def attn_full(src, dst, bar, res_tok, hn_in, fuse_next):
            if hn_in is None:
                tok_hn = ln_phase(src, bar, res_tok)
            else:
                tok_hn = hn_in
            bar1 = k.snap()
            with ExitStack() as ps:
                with ExitStack() as p1:
                    hnf = p1.enter_context(nc.sbuf_tensor(_nm(), [128, 8, S], BF16))
                    tok_hnf = load_hnf(hnf, tok_hn, None, bar1)
                    tok_qkv = emit_attn_qkv(tok_hnf, hnf, bar1)
                bar2 = k.snap()
                tok_res, hn_out = emit_attn_attn(src, dst, tok_qkv, bar2, fuse_next)
            return k.snap(), tok_res, hn_out

        with ExitStack() as _prep_ps:
            _pin = [_prep_ps.enter_context(nc.sbuf_tensor(_nm(), [128, D], F32)) for _ in range(2)]
            _pps = [_prep_ps.enter_context(nc.psum_tensor(_nm("ps"), [128, D], F32)) for _ in range(2)]
            _pst = [_prep_ps.enter_context(nc.sbuf_tensor(_nm(), [128, D], BF16)) for _ in range(2)]
            hn0_tok = ln_phase(x_in, k.snap(), None)
            prep(_prep_ps, _pin, _pps, _pst)
        prep_toks.extend(((("gS0", k.cnt["gS0"])), (("gS1", k.cnt["gS1"]))))
        prep_toks2.append(("gW", k.cnt["gW"]))

        chain = []
        if "f1" in parts:
            chain.append(("f", None))
        if "at" in parts:
            chain.append(("a", None))
        if "f2" in parts:
            chain.append(("f", None))
        srcs = [x_in, xA, xB]
        cur_bar = k.snap()
        res_tok = None
        hn_tok = hn0_tok
        src = x_in
        for i, (kind, _) in enumerate(chain):
            dst = out if i == len(chain) - 1 else srcs[i + 1]
            fuse_next = i < len(chain) - 1
            if kind == "f":
                cur_bar, res_tok, hn_tok = ffn_full(src, dst, cur_bar, res_tok,
                                                    hn_tok, fuse_next)
            else:
                cur_bar, res_tok, hn_tok = attn_full(src, dst, cur_bar, res_tok,
                                                     hn_tok, fuse_next)
            src = dst

        def s_fin(e):
            e.w(*[(s, k.cnt[s]) for s in k.cnt])
        k.run("sync", s_fin)

        k.check_deadlock()
        k.flush()
    return nc



def _host_constants(S):
    half = DH // 2
    inv = 1.0 / (10000.0 ** (np.arange(half, dtype=np.float64) / half))
    ang = np.arange(S, dtype=np.float64)[:, None] * inv[None, :]
    cos = np.cos(ang).astype(np.float32)
    sin = np.sin(ang).astype(np.float32)
    ropecos = np.concatenate([cos, cos], axis=1)
    ropes1 = np.concatenate([-sin, sin], axis=1)
    consts = np.zeros((128, 8), np.float32)
    consts[:, 0] = C_EXPB
    consts[:, 1] = C_EPS
    consts[:, 2] = C_MH
    consts[:, 3] = C_QS_ATTN
    consts[:, 4] = C_QS_FFN
    identf = np.eye(128, dtype=np.float32)
    identb = np.eye(128, dtype=np.float32).astype(BF)
    e8 = np.zeros((8, 1024), np.float32)
    for e in range(8):
        e8[e, e * 128:(e + 1) * 128] = 1.0
    e8 = e8.astype(BF)
    return ropecos, ropes1, consts, identf, identb, e8


def kernel(L0_attn_x, L0_attn_w_qkv, L0_attn_w_out, L0_ffn_q_w, L0_ffn_k_w,
           L0_ffn_v_w, L0_ffn_o_w, L0_ffn_sel_v_w, L0_ffn_sel_o_w,
           L0_ffn_route_scale):
    S = L0_attn_x.shape[1]
    ropecos, ropes1, consts, identf, identb, e8 = _host_constants(S)
    nc = build_program(S=S)
    common = {
        "w_qkv": np.ascontiguousarray(L0_attn_w_qkv),
        "w_out": np.ascontiguousarray(L0_attn_w_out),
        "qw": np.ascontiguousarray(L0_ffn_q_w),
        "kw": np.ascontiguousarray(L0_ffn_k_w),
        "vw": np.ascontiguousarray(L0_ffn_v_w),
        "ow": np.ascontiguousarray(L0_ffn_o_w),
        "selv": np.ascontiguousarray(L0_ffn_sel_v_w),
        "selo": np.ascontiguousarray(L0_ffn_sel_o_w),
        "rs": np.ascontiguousarray(L0_ffn_route_scale).reshape(1, 1),
        "ropecos": ropecos, "ropes1": ropes1, "consts": consts,
        "identf": identf, "identb": identb, "e8": e8,
    }
    in_maps = []
    for c in range(8):
        m = dict(common)
        m["x"] = np.ascontiguousarray(L0_attn_x[c])
        in_maps.append(m)
    res = run_bass_kernel_spmd(nc, in_maps, core_ids=list(range(8)))
    return np.stack([res.results[c]["out"] for c in range(8)], axis=0)


# version 24 (final)
# speedup vs baseline: 1.0183x; 1.0063x over previous
import math
from contextlib import ExitStack

import numpy as np
import ml_dtypes

import concourse.bass as bass
import concourse.mybir as mybir
from concourse.bass_utils import run_bass_kernel_spmd

F32 = mybir.dt.float32
BF16 = mybir.dt.bfloat16
AF = mybir.ActivationFunctionType
ALU = mybir.AluOpType
AX = mybir.AxisListType
BF = ml_dtypes.bfloat16

D = 1024
NH = 16
DH = 64
FH = 8
FP = 128
FE = 8
ENGS = ("tensor", "vector", "scalar", "gpsimd", "sync")

C_EXPB = -8.0
C_EPS = 1e-5
C_MH = -0.5
C_QS_ATTN = 1.0 / math.sqrt(DH)
C_QS_FFN = 1.0 / math.sqrt(FP)


class FakeIns:
    __slots__ = ("incs",)

    def __init__(self):
        self.incs = []

    def then_inc(self, hdl, amt):
        self.incs.append((hdl, amt))
        return self


class RecEng:

    def __init__(self, rec):
        self._rec = rec

    def __getattr__(self, name):
        rec = self._rec

        def meth(*a, **kw):
            fi = FakeIns()
            rec.append((name, a, kw, fi))
            return fi
        return meth


class KB:

    def __init__(self, nc, blk, stack):
        self.nc, self.blk, self.stack = nc, blk, stack
        self.cnt = {}
        self.hdl = {}
        self.seen = {e: {} for e in ENGS}
        self.mir = {e: [] for e in ENGS}
        self.rec = {e: [] for e in ENGS}

    def sems(self, *names):
        for n in names:
            self.hdl[n] = self.stack.enter_context(self.nc.semaphore(n))
            self.cnt[n] = 0

    def run(self, eng, fn):
        fn(Em(self, eng, RecEng(self.rec[eng])))

    def flush(self):
        for eng in ENGS:
            rec = self.rec[eng]
            if not rec:
                continue

            def body(E, rec=rec):
                for it in rec:
                    if it[0] == "__wait__":
                        E.wait_ge(it[1], it[2])
                    else:
                        name, a, kw, fi = it
                        ins = getattr(E, name)(*a, **kw)
                        for hdl, amt in fi.incs:
                            ins.then_inc(hdl, amt)
            getattr(self.blk, eng)(body)

    def snap(self):
        return dict(self.cnt)

    def check_deadlock(self):
        ptr = {e: 0 for e in ENGS}
        cnt = {s: 0 for s in self.cnt}
        progress = True
        while progress:
            progress = False
            for e in ENGS:
                while ptr[e] < len(self.mir[e]):
                    it = self.mir[e][ptr[e]]
                    if it[0] == "w":
                        if cnt[it[1]] < it[2]:
                            break
                    else:
                        cnt[it[1]] += it[2]
                    ptr[e] += 1
                    progress = True
        stuck = {e: ptr[e] for e in ENGS if ptr[e] < len(self.mir[e])}
        if stuck:
            det = {}
            for e in stuck:
                it = self.mir[e][ptr[e]]
                det[e] = f"op {ptr[e]}/{len(self.mir[e])} waits {it[1]}>={it[2]} (now {cnt[it[1]]})"
            raise RuntimeError(f"DEADLOCK in built program: {det}")


class Em:
    def __init__(self, k, eng, E):
        self.k, self.eng, self.E = k, eng, E

    def w(self, *toks):
        flat = []
        for t in toks:
            if t is None:
                continue
            if isinstance(t, (list, tuple)) and t and isinstance(t[0], (list, tuple)):
                flat.extend(t)
            else:
                flat.append(t)
        for t in flat:
            if t is None:
                continue
            s, v = t
            if v <= 0 or self.k.seen[self.eng].get(s, 0) >= v:
                continue
            self.k.rec[self.eng].append(("__wait__", self.k.hdl[s], v))
            self.k.seen[self.eng][s] = v
            self.k.mir[self.eng].append(("w", s, v))

    def bar(self, snapshot):
        self.w(*snapshot.items())

    def inc(self, ins, sem, amt=1):
        self.k.cnt[sem] += amt
        ins.then_inc(self.k.hdl[sem], amt)
        self.k.mir[self.eng].append(("i", sem, amt))
        return (sem, self.k.cnt[sem])

    def tok(self, sem):
        return (sem, self.k.cnt[sem])


def build_program(S=4096, parts=("f1", "at", "f2"), dbg=()):
    NS = S // 128
    SUP = min(1024, S)
    NSUP = S // SUP
    QC = min(512, SUP)
    NQC = SUP // QC
    nc = bass.Bass("TRN2", target_bir_lowering=False)

    x_in = nc.dram_tensor("x", [S, D], F32, kind="ExternalInput").ap()
    w_qkv = nc.dram_tensor("w_qkv", [3 * D, D], F32, kind="ExternalInput").ap()
    w_out = nc.dram_tensor("w_out", [D, D], F32, kind="ExternalInput").ap()
    qw = nc.dram_tensor("qw", [FH * FP, D], F32, kind="ExternalInput").ap()
    kw = nc.dram_tensor("kw", [FH * FP, D], F32, kind="ExternalInput").ap()
    vw = nc.dram_tensor("vw", [FH * FE, D, FP], F32, kind="ExternalInput").ap()
    ow = nc.dram_tensor("ow", [FH * FE, FP, D], F32, kind="ExternalInput").ap()
    selv = nc.dram_tensor("selv", [FH * FE, D], F32, kind="ExternalInput").ap()
    selo = nc.dram_tensor("selo", [FH * FE, D], F32, kind="ExternalInput").ap()
    rs_in = nc.dram_tensor("rs", [1, 1], F32, kind="ExternalInput").ap()
    identf_in = nc.dram_tensor("identf", [128, 128], F32, kind="ExternalInput").ap()
    identb_in = nc.dram_tensor("identb", [128, 128], BF16, kind="ExternalInput").ap()
    e8_in = nc.dram_tensor("e8", [8, 1024], BF16, kind="ExternalInput").ap()
    cos_in = nc.dram_tensor("ropecos", [S, DH], F32, kind="ExternalInput").ap()
    s1_in = nc.dram_tensor("ropes1", [S, DH], F32, kind="ExternalInput").ap()
    consts_in = nc.dram_tensor("consts", [128, 8], F32, kind="ExternalInput").ap()
    out = nc.dram_tensor("out", [S, D], F32, kind="ExternalOutput").ap()

    def scr(name, shape, dt):
        kind = "ExternalOutput" if name in dbg else "Internal"
        return nc.dram_tensor(name, shape, dt, kind=kind).ap()

    xA = scr("xA", [S, D], F32)
    xB = scr("xB", [S, D], F32)
    hn_h = scr("hn_h", [S, D], BF16)
    fq_h = scr("fq_h", [FH, 128, S], BF16)
    fk_h = scr("fk_h", [FH, 128, S], BF16)
    fv_h = scr("fv_h", [FH, S, FP], BF16)
    aq_h = scr("aq_h", [S, D], BF16)
    ak_h = scr("ak_h", [S, D], BF16)
    av_h = scr("av_h", [S, D], BF16)
    p_selT = scr("p_selT", [D, 128], BF16)
    p_qwT = scr("p_qwT", [D, FH * FP], BF16)
    p_kwT = scr("p_kwT", [D, FH * FP], BF16)
    p_wqkvT = scr("p_wqkvT", [D, 3 * D], BF16)
    p_woutT = scr("p_woutT", [D, D], BF16)
    p_owb = scr("p_owb", [FH * FE, FP, D], BF16)
    p_vwb = scr("p_vwb", [FH, 8, 128, 2, 512], BF16)

    _uid = [0]

    def _nm(base="t"):
        _uid[0] += 1
        return f"{base}_{_uid[0]}"

    stack = ExitStack()
    with stack:
        blk = stack.enter_context(nc.Block())
        k = KB(nc, blk, stack)
        k.sems("cT", "cV", "cS", "cP", "gW", "gS", "sIN", "sT", "sW",
               "sR0", "sR1", "sL0", "sL1", "gV0", "gV1", "gO0", "gO1",
               "gS0", "gS1", "gR0", "gR1", "sW0", "sW1", "sC", "sP0", "sP1")
        sb = lambda shape, dt: stack.enter_context(nc.sbuf_tensor(_nm(), shape, dt))
        prep_toks = []
        prep_toks2 = []

        ct = sb([128, 8], F32)
        identf = sb([128, 128], F32)
        identb = sb([128, 128], BF16)
        ones_rowf = sb([1, 64], F32)
        ones_colb = sb([128, 1], BF16)
        e8c = sb([8, 1024], BF16)
        rs_v = sb([128, 1], F32)
        rs_o = sb([128, 1], F32)

        from concourse.hw_specs import get_activation_tables
        _tables = list(get_activation_tables(nc.m.arch).values())

        def _set_id(*funcs):
            for i, fs in enumerate(_tables):
                if all(f in fs for f in funcs):
                    return i
            raise ValueError(funcs)
        ID_NLEXP = _set_id(AF.Ln, AF.Exp)
        ID_SIG = _set_id(AF.Sigmoid)

        def preload_tbl(e, set_id):
            e.E.add_instruction(mybir.InstLoadActFuncSet(
                name=nc.get_next_instruction_name(),
                act_func_set_id=set_id, ins=[], outs=[]))

        nc.const_aps.register(F32, C_EXPB, ct[:, 0:1])
        nc.const_aps.register(F32, C_EPS, ct[:, 1:2])
        nc.const_aps.register(F32, C_MH, ct[:, 2:3])
        nc.const_aps.register(F32, C_QS_ATTN, ct[:, 3:4])
        nc.const_aps.register(F32, C_QS_FFN, ct[:, 4:5])

        def g_init(e):
            e.inc(e.E.dma_start(ct[:], consts_in), "gW", 16)
            e.inc(e.E.dma_start(identf[:], identf_in), "gW", 16)
            e.inc(e.E.dma_start(identb[:], identb_in), "gW", 16)
            e.inc(e.E.dma_start(rs_v[:], rs_in.partition_broadcast(128).squeeze(1)),
                  "gW", 16)
            e.inc(e.E.dma_start(e8c[:], e8_in), "gW", 16)
        k.run("gpsimd", g_init)
        t_ginit = ("gW", k.cnt["gW"])

        def s_tblinit(e):
            preload_tbl(e, ID_NLEXP)
        k.run("scalar", s_tblinit)

        def v_init(e):
            e.w(t_ginit)
            e.inc(e.E.memset(ones_rowf[:], 1.0), "cV")
            e.inc(e.E.memset(ones_colb[:], 1.0), "cV")
            e.inc(e.E.tensor_scalar_mul(rs_o[:], rs_v[:], 0.5), "cV")
        k.run("vector", v_init)

        def prep(ps, pin, pps, pst):
            if True:
                jobs = []
                jobs.append((selv[:, :], 64, p_selT, 1.0, 0))
                jobs.append((selo[:, :], 64, p_selT, 1.0, 64))
                for r in range(8):
                    jobs.append((qw[r * 128:(r + 1) * 128, :], 128, p_qwT,
                                 C_QS_FFN, r * 128))
                for r in range(8):
                    jobs.append((kw[r * 128:(r + 1) * 128, :], 128, p_kwT,
                                 1.0, r * 128))
                for r in range(8):
                    jobs.append((w_qkv[r * 128:(r + 1) * 128, :], 128, p_wqkvT,
                                 C_QS_ATTN, r * 128))
                for r in range(8, 24):
                    jobs.append((w_qkv[r * 128:(r + 1) * 128, :], 128, p_wqkvT,
                                 1.0, r * 128))
                for r in range(8):
                    jobs.append((w_out[r * 128:(r + 1) * 128, :], 128, p_woutT,
                                 1.0, r * 128))

                def g_cast(e):
                    for he in range(FH * FE):
                        e.inc(e.E.dma_start(p_owb[he], ow[he]), "gW", 16)
                    for h in range(FH):
                        for dc in range(8):
                            for eh in range(2):
                                e.inc(e.E.dma_start(
                                    p_vwb[h, dc, :, eh, :]
                                    .rearrange("p (ee c) -> p ee c", ee=4),
                                    vw[h * 8 + eh * 4:h * 8 + eh * 4 + 4,
                                       dc * 128:(dc + 1) * 128, :]
                                    .rearrange("ee d c -> d ee c")), "gW", 16)
                k.run("gpsimd", g_cast)

                ev_tok = [None, None]
                st_tok = [None, None]
                for j, (src, nr, dst, scale, co) in enumerate(jobs):
                    b = j % 2

                    def s_load(e, src=src, nr=nr, b=b, ev=ev_tok[b]):
                        e.w(ev)
                        e.inc(e.E.dma_start(pin[b][0:nr, :], src), "sP%d" % b, 16)
                    k.run("sync", s_load)
                    in_tok = ("sP%d" % b, k.cnt["sP%d" % b])

                    def t_tr(e, nr=nr, b=b, in_tok=in_tok, ev=ev_tok[b]):
                        e.w(t_ginit, in_tok, ev)
                        ins = None
                        for c in range(8):
                            ins = e.E.transpose(pps[b][:, c * nr:(c + 1) * nr],
                                                pin[b][0:nr, c * 128:(c + 1) * 128],
                                                identf[0:nr, 0:nr])
                        e.inc(ins, "cT")
                    k.run("tensor", t_tr)
                    tr_tok = ("cT", k.cnt["cT"])

                    def s_ev(e, b=b, scale=scale, nr=nr, tr=tr_tok, st=st_tok[b]):
                        e.w(tr, st)
                        if scale == 1.0:
                            ins = e.E.activation(pst[b][:, 0:8 * nr],
                                                 pps[b][:, 0:8 * nr], AF.Copy)
                        else:
                            ins = e.E.activation(pst[b][:, 0:8 * nr],
                                                 pps[b][:, 0:8 * nr], AF.Copy,
                                                 scale=scale)
                        e.inc(ins, "cS")
                    k.run("scalar", s_ev)
                    ev_tok[b] = ("cS", k.cnt["cS"])

                    def g_st(e, b=b, dst=dst, co=co, nr=nr, ev=ev_tok[b]):
                        e.w(ev)
                        dview = dst.rearrange("(c p) r -> p c r", p=128)
                        e.inc(e.E.dma_start(
                            dview[:, :, co:co + nr],
                            pst[b][:, 0:8 * nr].rearrange("p (c r) -> p c r", c=8)),
                            "gS%d" % b, 16)
                    k.run("gpsimd", g_st)
                    st_tok[b] = ("gS%d" % b, k.cnt["gS%d" % b])


        def ln_phase(src, bar, res_tok):
            with ExitStack() as ps:
                xt = [ps.enter_context(nc.sbuf_tensor(_nm(), [128, D], F32)) for _ in range(2)]
                sq = ps.enter_context(nc.sbuf_tensor(_nm(), [128, D], F32))
                hnt = [ps.enter_context(nc.sbuf_tensor(_nm(), [128, D], BF16)) for _ in range(2)]
                st = ps.enter_context(nc.sbuf_tensor(_nm(), [128, 2, 8], F32))
                hn_done = [None, None]
                last_vneg = [None, None]
                hn_stored = [None, None]
                for sc in range(NS):
                    b = sc % 2
                    s_ = st[:, b, :]

                    def s_ld(e, sc=sc, b=b, hd=hn_done[b], bar=bar, rt=res_tok,
                             hs=hn_stored[b], lv=last_vneg[b]):
                        e.bar(bar)
                        e.w(rt, hd, hs, lv)
                        e.inc(e.E.dma_start(xt[b][:], src[sc * 128:(sc + 1) * 128, :]),
                              "sR%d" % b, 16)
                    k.run("sync", s_ld)
                    tin = ("sR%d" % b, k.cnt["sR%d" % b])

                    def v_s1(e, b=b, s_=s_, tin=tin, bar=bar, hd=hn_done[b]):
                        e.bar(bar)
                        e.w(tin, hd)
                        e.inc(e.E.reduce_sum(s_[:, 0:1], xt[b][:], axis=AX.X), "cV")
                    k.run("vector", v_s1)
                    t_s1 = ("cV", k.cnt["cV"])

                    def v_s2(e, b=b, s_=s_, tin=tin, bar=bar, lv=last_vneg[b]):
                        e.bar(bar)
                        e.w(tin, lv)
                        e.inc(e.E.tensor_mul(sq[:], xt[b][:], xt[b][:]), "cV")
                        e.w(e.tok("cV"))
                        e.inc(e.E.reduce_sum(s_[:, 1:2], sq[:], axis=AX.X), "cV")
                    k.run("vector", v_s2)
                    t_s2 = ("cV", k.cnt["cV"])

                    def v_mid(e, s_=s_, t_s1=t_s1, t_s2=t_s2):
                        e.w(t_s1, t_s2)
                        e.inc(e.E.tensor_scalar_mul(s_[:, 2:3], s_[:, 0:1], 1.0 / D), "cV")
                        e.inc(e.E.tensor_scalar_mul(s_[:, 3:4], s_[:, 1:2], 1.0 / D), "cV")
                        e.w(e.tok("cV"))
                        e.inc(e.E.tensor_mul(s_[:, 4:5], s_[:, 2:3], s_[:, 2:3]), "cV")
                        e.w(e.tok("cV"))
                        e.inc(e.E.tensor_sub(s_[:, 3:4], s_[:, 3:4], s_[:, 4:5]), "cV")
                    k.run("vector", v_mid)
                    t_var = ("cV", k.cnt["cV"])

                    def s_mid(e, s_=s_, t_var=t_var):
                        e.w(t_var)
                        e.inc(e.E.activation(s_[:, 5:6], s_[:, 3:4], AF.Ln,
                                             bias=C_EPS), "cS")
                        e.w(e.tok("cS"))
                        e.inc(e.E.activation(s_[:, 5:6], s_[:, 5:6], AF.Exp,
                                             scale=C_MH), "cS")
                    k.run("scalar", s_mid)
                    t_rstd = ("cS", k.cnt["cS"])

                    def v_neg(e, s_=s_, t_rstd=t_rstd):
                        e.w(t_rstd)
                        e.inc(e.E.tensor_scalar(s_[:, 6:7], s_[:, 2:3], s_[:, 5:6],
                                                -1.0, ALU.mult, ALU.mult), "cV")
                    k.run("vector", v_neg)
                    last_vneg[b] = ("cV", k.cnt["cV"])

                    def v_hn(e, b=b, s_=s_, tn=last_vneg[b], hs=hn_stored[b]):
                        e.w(tn, hs)
                        e.inc(e.E.tensor_scalar(hnt[b][:], xt[b][:], s_[:, 5:6],
                                                s_[:, 6:7], ALU.mult, ALU.add), "cV")
                    k.run("vector", v_hn)
                    hn_done[b] = ("cV", k.cnt["cV"])

                    def g_st(e, b=b, sc=sc, hd=hn_done[b]):
                        e.w(hd)
                        e.inc(e.E.dma_start(hn_h[sc * 128:(sc + 1) * 128, :],
                                            hnt[b][:]), "gS%d" % b, 16)
                    k.run("gpsimd", g_st)
                    hn_stored[b] = ("gS%d" % b, k.cnt["gS%d" % b])
            return (("gS0", k.cnt["gS0"]), ("gS1", k.cnt["gS1"]))

        def load_hnf(hnf, tok_hn, free_tok, bar):
            def s_ld(e, bar=bar):
                e.bar(bar)
                e.w(tok_hn, free_tok)
                for c in range(8):
                    e.inc(e.E.dma_start_transpose(
                        hnf[:, c, :], hn_h[:, c * 128:(c + 1) * 128]), "sT", 16)
            k.run("sync", s_ld)
            return ("sT", k.cnt["sT"])

        def emit_sel(hnf, g_all, tok_hnf, bar, ps):
            if True:
                selT = ps.enter_context(nc.sbuf_tensor(_nm(), [128, 8, 128], BF16))
                lg = [ps.enter_context(nc.psum_tensor(_nm("ps"), [128, 128], F32)) for _ in range(2)]
                pr = ps.enter_context(nc.sbuf_tensor(_nm(), [128, 16, 8], F32))
                mr = ps.enter_context(nc.sbuf_tensor(_nm(), [128, 16, 8], F32))
                vv = ps.enter_context(nc.sbuf_tensor(_nm(), [128, 16, 4], F32))

                def s_w(e, bar=bar):
                    e.bar(bar)
                    e.w(*prep_toks)
                    e.inc(e.E.dma_start(
                        selT[:], p_selT.rearrange("(c p) r -> p c r", p=128)), "sC", 16)
                k.run("sync", s_w)
                t_selT = ("sC", k.cnt["sC"])

                def s_sig_tbl(e):
                    preload_tbl(e, ID_SIG)
                k.run("scalar", s_sig_tbl)
                sig_done = [None, None]
                gate_done = None
                for sc in range(NS):
                    b = sc % 2

                    def t_mm(e, sc=sc, b=b, t_selT=t_selT, tok_hnf=tok_hnf,
                             sd=sig_done[b], bar=bar):
                        e.bar(bar)
                        e.w(t_selT, tok_hnf, sd)
                        ins = None
                        for d in range(8):
                            ins = e.E.matmul(lg[b][:],
                                             hnf[:, d, sc * 128:(sc + 1) * 128],
                                             selT[:, d, :],
                                             start=(d == 0), stop=(d == 7))
                        e.inc(ins, "cT")
                    k.run("tensor", t_mm)
                    t_lg = ("cT", k.cnt["cT"])

                    def s_sig(e, b=b, t_lg=t_lg, gd=gate_done, bar=bar):
                        e.bar(bar)
                        e.w(t_lg, gd)
                        e.inc(e.E.activation(pr[:].rearrange("p a b -> p (a b)"),
                                             lg[b][:], AF.Sigmoid), "cS")
                    k.run("scalar", s_sig)
                    sig_done[b] = ("cS", k.cnt["cS"])

                    def v_top(e, sc=sc, sd=sig_done[b], bar=bar):
                        e.bar(bar)
                        e.w(sd)
                        e.inc(e.E.reduce_max(vv[:, :, 0:1].squeeze(2), pr[:],
                                             axis=AX.X), "cV")
                        e.w(e.tok("cV"))
                        e.inc(e.E.tensor_tensor(mr[:], pr[:],
                                                vv[:, :, 0:1].broadcast_to([128, 16, 8]),
                                                ALU.is_ge), "cV")
                        e.w(e.tok("cV"))
                        e.inc(e.E.scalar_tensor_tensor(mr[:], mr[:], -1e9, pr[:],
                                                       ALU.mult, ALU.add), "cV")
                        e.w(e.tok("cV"))
                        e.inc(e.E.reduce_max(vv[:, :, 1:2].squeeze(2), mr[:],
                                             axis=AX.X), "cV")
                        e.w(e.tok("cV"))
                        e.inc(e.E.tensor_add(vv[:, :, 2:3], vv[:, :, 0:1],
                                             vv[:, :, 1:2]), "cV")
                        e.w(e.tok("cV"))
                        e.inc(e.E.tensor_scalar_max(vv[:, :, 2:3], vv[:, :, 2:3],
                                                    1e-9), "cV")
                        e.w(e.tok("cV"))
                        e.inc(e.E.reciprocal(vv[:, :, 2:3], vv[:, :, 2:3]), "cV")
                        e.w(e.tok("cV"))
                        e.inc(e.E.tensor_scalar(vv[:, 0:8, 3:4], vv[:, 0:8, 2:3],
                                                rs_v[:], None, ALU.mult), "cV")
                        e.inc(e.E.tensor_scalar(vv[:, 8:16, 3:4], vv[:, 8:16, 2:3],
                                                rs_o[:], None, ALU.mult), "cV")
                        e.w(e.tok("cV"))
                        e.inc(e.E.tensor_tensor(mr[:], pr[:],
                                                vv[:, :, 1:2].broadcast_to([128, 16, 8]),
                                                ALU.is_ge), "cV")
                        e.w(e.tok("cV"))
                        e.inc(e.E.tensor_mul(mr[:], mr[:], pr[:]), "cV")
                        e.w(e.tok("cV"))
                        e.inc(e.E.tensor_tensor(
                            g_all[:, sc, :].rearrange("p (a b) -> p a b", a=16),
                            mr[:], vv[:, :, 3:4].broadcast_to([128, 16, 8]),
                            ALU.mult), "cV")
                    k.run("vector", v_top)
                    gate_done = ("cV", k.cnt["cV"])

                def s_nl_tbl(e):
                    preload_tbl(e, ID_NLEXP)
                k.run("scalar", s_nl_tbl)
            return gate_done

        def emit_ffn_qk(hnf, tok_hnf, bar, ps):
            if True:
                wbuf = [ps.enter_context(nc.sbuf_tensor(_nm(), [128, 8, 128], BF16))
                        for _ in range(2)]
                qp = [ps.enter_context(nc.psum_tensor(_nm("ps"), [128, 512], F32)) for _ in range(2)]
                stg = [ps.enter_context(nc.sbuf_tensor(_nm(), [128, S], BF16)) for _ in range(2)]
                wfree = [None, None]
                evtok = [None, None]
                sttok = [None, None]
                NSQ = S // 512
                for part, (wsrc, wdst) in enumerate(((p_qwT, fq_h), (p_kwT, fk_h))):
                    for pc in range(8):
                        wb = (part * 8 + pc) % 2

                        def s_w(e, wsrc=wsrc, pc=pc, wb=wb, wf=wfree[wb], bar=bar):
                            e.bar(bar)
                            e.w(*prep_toks, wf)
                            e.inc(e.E.dma_start(
                                wbuf[wb][:],
                                wsrc.rearrange("(c p) r -> p c r", p=128)
                                [:, :, pc * 128:(pc + 1) * 128]), "sW%d" % wb, 16)
                        k.run("sync", s_w)
                        t_w = ("sW%d" % wb, k.cnt["sW%d" % wb])

                        for scq in range(NSQ):
                            b = scq % 2

                            def t_mm(e, wb=wb, b=b, scq=scq, t_w=t_w,
                                     tok_hnf=tok_hnf, pf=evtok[b], bar=bar):
                                e.bar(bar)
                                e.w(t_w, tok_hnf, pf)
                                ins = None
                                for d in range(8):
                                    ins = e.E.matmul(
                                        qp[b][:], wbuf[wb][:, d, :],
                                        hnf[:, d, scq * 512:(scq + 1) * 512],
                                        start=(d == 0), stop=(d == 7))
                                e.inc(ins, "cT")
                            k.run("tensor", t_mm)
                            t_mm_t = ("cT", k.cnt["cT"])
                            if scq == NSQ - 1:
                                wfree[wb] = t_mm_t

                            wb2 = (part * 8 + pc) % 2

                            def s_ev(e, b=b, wb2=wb2, scq=scq, t_mm_t=t_mm_t,
                                     sf=(sttok[wb2] if scq == 0 else None), bar=bar):
                                e.bar(bar)
                                e.w(t_mm_t, sf)
                                e.inc(e.E.activation(
                                    stg[wb2][:, scq * 512:(scq + 1) * 512],
                                    qp[b][:], AF.Copy), "cS")
                            k.run("scalar", s_ev)
                            evtok[b] = ("cS", k.cnt["cS"])
                            if scq == NSQ - 1:
                                def g_st(e, wb2=wb2, pc=pc, wdst=wdst, ev=evtok[b]):
                                    e.w(ev)
                                    e.inc(e.E.dma_start(wdst[pc], stg[wb2][:]),
                                          "gS%d" % wb2, 16)
                                k.run("gpsimd", g_st)
                                sttok[wb2] = ("gS%d" % wb2, k.cnt["gS%d" % wb2])
            return (("gS0", k.cnt["gS0"]), ("gS1", k.cnt["gS1"]))

        def emit_ffn_v(hnf, g_all, tok_hnf, tok_gates, bar, ps):
            if True:
                vwb = [ps.enter_context(nc.sbuf_tensor(_nm(), [128, 8, 2, 512], BF16))
                       for _ in range(2)]
                xv = [ps.enter_context(nc.psum_tensor(_nm("ps"), [128, D], F32)) for _ in range(2)]
                vacc = ps.enter_context(nc.sbuf_tensor(_nm(), [128, FP], F32))
                vst = [ps.enter_context(nc.sbuf_tensor(_nm(), [128, NS, FP], BF16))
                       for _ in range(2)]
                vw_free = [None, None]
                vst_free = [None, None]
                xv_free = [None, None]
                vacc_free = None
                for h in range(FH):
                    hb = h % 2

                    def g_w(e, h=h, hb=hb, vf=vw_free[hb], bar=bar):
                        e.bar(bar)
                        e.w(*prep_toks2, vf)
                        e.inc(e.E.dma_start(
                            vwb[hb][:],
                            p_vwb[h].rearrange("c p e f -> p c e f")),
                            "gV%d" % hb, 16)
                    k.run("sync", g_w)
                    t_vw = ("gV%d" % hb, k.cnt["gV%d" % hb])

                    for sc in range(NS):
                        b = sc % 2

                        def t_mm(e, hb=hb, b=b, sc=sc, t_vw=t_vw, tok_hnf=tok_hnf,
                                 xf=xv_free[b], bar=bar):
                            e.bar(bar)
                            e.w(t_vw, tok_hnf, xf)
                            for eh in range(2):
                                ins = None
                                for dd in range(8):
                                    ins = e.E.matmul(
                                        xv[b][:, eh * 512:(eh + 1) * 512],
                                        hnf[:, dd, sc * 128:(sc + 1) * 128],
                                        vwb[hb][:, dd, eh, :],
                                        start=(dd == 0), stop=(dd == 7))
                                e.inc(ins, "cT")
                        k.run("tensor", t_mm)
                        t_xv = ("cT", k.cnt["cT"])

                        def v_g(e, h=h, b=b, sc=sc, t_xv=t_xv, vaf=vacc_free,
                                tg=tok_gates, bar=bar):
                            e.bar(bar)
                            e.w(t_xv, vaf, tg)
                            for ei in range(8):
                                gcol = g_all[:, sc, h * 8 + ei:h * 8 + ei + 1]
                                if ei == 0:
                                    e.inc(e.E.tensor_scalar(
                                        vacc[:], xv[b][:, 0:128], gcol, None,
                                        ALU.mult), "cV")
                                else:
                                    e.inc(e.E.scalar_tensor_tensor(
                                        vacc[:], xv[b][:, ei * 128:(ei + 1) * 128],
                                        gcol, vacc[:], ALU.mult, ALU.add), "cV")
                                e.w(e.tok("cV"))
                        k.run("vector", v_g)
                        xv_free[b] = ("cV", k.cnt["cV"])

                        def s_ev(e, hb=hb, sc=sc, tg=xv_free[b], vf=vst_free[hb],
                                 bar=bar):
                            e.bar(bar)
                            e.w(tg, vf)
                            e.inc(e.E.activation(vst[hb][:, sc, :], vacc[:],
                                                 AF.Copy), "cS")
                        k.run("scalar", s_ev)
                        vacc_free = ("cS", k.cnt["cS"])
                    vw_free[hb] = ("cT", k.cnt["cT"])

                    def g_st(e, h=h, hb=hb, tv=vacc_free):
                        e.w(tv)
                        e.inc(e.E.dma_start(
                            fv_h[h].rearrange("(sc p) c -> p sc c", p=128),
                            vst[hb][:]), "gS%d" % hb, 16)
                    k.run("gpsimd", g_st)
                    vst_free[hb] = ("gS%d" % hb, k.cnt["gS%d" % hb])
            return (("gS0", k.cnt["gS0"]), ("gS1", k.cnt["gS1"]))

        def make_resid(ps):
            B = {
                "xr": [ps.enter_context(nc.sbuf_tensor(_nm(), [128, D], F32))
                       for _ in range(2)],
                "xo": [ps.enter_context(nc.sbuf_tensor(_nm(), [128, D], F32))
                       for _ in range(2)],
                "sq": ps.enter_context(nc.sbuf_tensor(_nm(), [128, D], F32)),
                "st": ps.enter_context(nc.sbuf_tensor(_nm(), [128, 2, 8], F32)),
                "hnt": [ps.enter_context(nc.sbuf_tensor(_nm(), [128, D], BF16))
                        for _ in range(2)],
                "hn_done": [None, None],
                "sq_free": None,
            }
            return B

        def emit_resid_super(B, src, dst, sup, out_evac, res_done, oacc, bar,
                             fuse_hn):
            xr, xo, sqf, stf, hnt = B["xr"], B["xo"], B["sq"], B["st"], B["hnt"]
            for i8 in range(SUP // 128):
                rb = i8 % 2
                row = sup * SUP + i8 * 128
                s_ = stf[:, rb, :]

                def s_ld(e, rb=rb, row=row, oe=out_evac, rd=res_done, bar=bar,
                         hd=B["hn_done"][rb]):
                    e.bar(bar)
                    e.w(oe, rd, hd)
                    e.inc(e.E.dma_start(xr[rb][:], src[row:row + 128, :]),
                          "sR%d" % rb, 16)
                k.run("sync", s_ld)
                t_x = ("sR%d" % rb, k.cnt["sR%d" % rb])

                def v_add(e, rb=rb, i8=i8, t_x=t_x, oe=out_evac, rd=res_done,
                          hd=B["hn_done"][rb]):
                    e.w(t_x, oe, rd, hd)
                    e.inc(e.E.tensor_add(xo[rb][:], oacc[:, i8, :], xr[rb][:]),
                          "cV")
                k.run("vector", v_add)
                t_add = ("cV", k.cnt["cV"])

                def g_st(e, rb=rb, row=row, t_add=t_add, dst=dst):
                    e.w(t_add)
                    e.inc(e.E.dma_start(dst[row:row + 128, :], xo[rb][:]),
                          "gR%d" % rb, 16)
                k.run("gpsimd", g_st)
                res_done = (("gR0", k.cnt["gR0"]), ("gR1", k.cnt["gR1"]))

                if fuse_hn:
                    def v_st(e, rb=rb, s_=s_, t_add=t_add, sqf_f=B["sq_free"]):
                        e.w(t_add, sqf_f)
                        e.inc(e.E.tensor_mul(sqf[:], xo[rb][:], xo[rb][:]), "cV")
                        e.inc(e.E.reduce_sum(s_[:, 0:1], xo[rb][:], axis=AX.X), "cV")
                        e.w(e.tok("cV"))
                        e.inc(e.E.reduce_sum(s_[:, 1:2], sqf[:], axis=AX.X), "cV")
                        e.w(e.tok("cV"))
                        e.inc(e.E.tensor_scalar_mul(s_[:, 2:3], s_[:, 0:1], 1.0 / D),
                              "cV")
                        e.inc(e.E.tensor_scalar_mul(s_[:, 3:4], s_[:, 1:2], 1.0 / D),
                              "cV")
                        e.w(e.tok("cV"))
                        e.inc(e.E.tensor_mul(s_[:, 4:5], s_[:, 2:3], s_[:, 2:3]),
                              "cV")
                        e.w(e.tok("cV"))
                        e.inc(e.E.tensor_sub(s_[:, 3:4], s_[:, 3:4], s_[:, 4:5]),
                              "cV")
                    k.run("vector", v_st)
                    t_var = ("cV", k.cnt["cV"])
                    B["sq_free"] = t_var

                    def s_mid(e, s_=s_, t_var=t_var):
                        e.w(t_var)
                        e.inc(e.E.activation(s_[:, 5:6], s_[:, 3:4], AF.Ln,
                                             bias=C_EPS), "cS")
                        e.w(e.tok("cS"))
                        e.inc(e.E.activation(s_[:, 5:6], s_[:, 5:6], AF.Exp,
                                             scale=C_MH), "cS")
                    k.run("scalar", s_mid)
                    t_rstd = ("cS", k.cnt["cS"])

                    def v_hn(e, rb=rb, s_=s_, t_rstd=t_rstd,
                             hd=B["hn_done"][rb]):
                        e.w(t_rstd, hd)
                        e.inc(e.E.tensor_scalar(s_[:, 6:7], s_[:, 2:3], s_[:, 5:6],
                                                -1.0, ALU.mult, ALU.mult), "cV")
                        e.w(e.tok("cV"))
                        e.inc(e.E.tensor_scalar(hnt[rb][:], xo[rb][:], s_[:, 5:6],
                                                s_[:, 6:7], ALU.mult, ALU.add), "cV")
                    k.run("vector", v_hn)
                    t_hn = ("cV", k.cnt["cV"])

                    def g_hn(e, rb=rb, row=row, t_hn=t_hn):
                        e.w(t_hn, ("sT", k.cnt["sT"]))
                        e.inc(e.E.dma_start(hn_h[row:row + 128, :], hnt[rb][:]),
                              "gS%d" % rb, 16)
                    k.run("gpsimd", g_hn)
                    B["hn_done"][rb] = ("gS%d" % rb, k.cnt["gS%d" % rb])
            return res_done

        def emit_ffn_attn(src, dst, g_all, tok_qk, tok_v, tok_gates, bar, fuse_hn):
            with ExitStack() as ps:
                kfb = [ps.enter_context(nc.sbuf_tensor(_nm(), [128, S], BF16)) for _ in range(2)]
                qhb = [ps.enter_context(nc.sbuf_tensor(_nm(), [128, SUP], BF16)) for _ in range(2)]
                vtb = [ps.enter_context(nc.sbuf_tensor(_nm(), [128, NS, FP], BF16))
                       for _ in range(2)]
                owb = [ps.enter_context(nc.sbuf_tensor(_nm(), [128, 8, D], BF16))
                       for _ in range(2)]
                Pt = ps.enter_context(nc.sbuf_tensor(_nm(), [128, NS, 512], BF16))
                Ps2 = ps.enter_context(nc.sbuf_tensor(_nm(), [128, NS // 2, 512], BF16))
                SC = [ps.enter_context(nc.psum_tensor(_nm("ps"), [128, 512], F32)) for _ in range(2)]
                CTX = ps.enter_context(nc.psum_tensor(_nm("ps"), [128, 512], F32))
                LP = ps.enter_context(nc.psum_tensor(_nm("ps"), [128, 512], F32))
                GBp = ps.enter_context(nc.psum_tensor(_nm("ps"), [128, 512], F32))
                GBg = ps.enter_context(nc.psum_tensor(_nm("ps"), [128, 512], F32))
                OUT = GBp
                OUT2 = [ps.enter_context(nc.psum_tensor(_nm("ps"), [128, 512], F32))
                        for _ in range(2)]
                rl_s = ps.enter_context(nc.sbuf_tensor(_nm(), [1, 512], F32))
                rlc = ps.enter_context(nc.sbuf_tensor(_nm(), [128, 4], F32))
                Gp = [ps.enter_context(nc.sbuf_tensor(_nm(), [8, 512], BF16)) for _ in range(2)]
                cs2 = ps.enter_context(nc.sbuf_tensor(_nm(), [128, 512], F32))
                yTs = ps.enter_context(nc.sbuf_tensor(_nm(), [128, 8, 512], BF16))
                oacc = ps.enter_context(nc.sbuf_tensor(_nm(), [128, SUP // 128, D], F32))
                RB = make_resid(ps)

                kf_free = [None, None]
                qh_free = [None, None]
                vt_free = [None, None]
                ow_free = [None, None]
                ld_tok = {}

                def issue_loads(idx, sup, h):
                    b = idx % 2

                    def s_ld(e, sup=sup, h=h, b=b, kf=kf_free[b], qf=qh_free[b],
                             vf=vt_free[b], bar=bar):
                        e.bar(bar)
                        e.w(*tok_qk, *tok_v, kf, qf, vf)
                        e.inc(e.E.dma_start(kfb[b][:], fk_h[h]), "sL%d" % b, 16)
                        e.inc(e.E.dma_start(
                            qhb[b][:], fq_h[h][:, sup * SUP:(sup + 1) * SUP]),
                            "sL%d" % b, 16)
                        e.inc(e.E.dma_start(
                            vtb[b][:],
                            fv_h[h].rearrange("(sc p) c -> p sc c", p=128)),
                            "sL%d" % b, 16)
                    k.run("sync", s_ld)
                    t_s = ("sL%d" % b, k.cnt["sL%d" % b])

                    def g_ow(e, h=h, b=b, of=ow_free[b], bar=bar):
                        e.bar(bar)
                        e.w(*prep_toks2, of)
                        e.inc(e.E.dma_start(
                            owb[b][:], p_owb[h * 8:(h + 1) * 8]
                            .rearrange("ee p c -> p ee c")), "gO%d" % b, 16)
                    k.run("sync", g_ow)
                    ld_tok[idx] = (t_s, ("gO%d" % b, k.cnt["gO%d" % b]))

                st = {
                    "pt_free": None,
                    "ctx_free": None,
                    "l_free": None,
                    "gp_free": [None, None],
                    "gbp_free": None,
                    "gbp_hold": None,
                    "gbp_rl": None,
                    "gbg_free": None,
                    "cs2_free": None,
                    "rlc_free": None,
                    "yts_free": None,
                    "out_free": None,
                    "out2_free": None,
                    "res_done": None,
                    "rl_s_free": None,
                    "ps2_free": None,
                }

                def make_tail(c):
                    (ci, b, sup, h, qc, t_ow, t_recip, t_g4, pvl_toks) = c
                    cb = ci % 2
                    ts = {"y": [], "out_free": None, "rlc": None}

                    def p_rl():
                        def t_rlt(e, t_recip=t_recip, of=st["out_free"],
                                  rf=st["rlc_free"], gb=st["gbp_hold"]):
                            e.w(t_recip, of, rf, gb)
                            ins = None
                            for qsub in range(4):
                                ins = e.E.transpose(
                                    OUT[0:128, 256 + qsub:256 + qsub + 1],
                                    rl_s[0:1, qsub * 128:(qsub + 1) * 128],
                                    identf[0:1, 0:1])
                            e.inc(ins, "cT")
                        k.run("tensor", t_rlt)
                        t_rlt_t = ("cT", k.cnt["cT"])
                        st["gbp_free"] = t_rlt_t

                        def v_rlc(e, t_rlt_t=t_rlt_t):
                            e.w(t_rlt_t)
                            e.inc(e.E.tensor_copy(rlc[:], OUT[0:128, 256:260]), "cV")
                            e.w(e.tok("cV"), pvl_toks[-1])
                            e.inc(e.E.tensor_copy(cs2[:], CTX[:]), "cV")
                        k.run("vector", v_rlc)
                        t_rlc = ("cV", k.cnt["cV"])
                        ts["rlc"] = t_rlc
                        st["rlc_free"] = t_rlc
                        st["ctx_free"] = t_rlc
                        st["cs2_free"] = t_rlc
                        st["out_free"] = t_rlc
                        st["gbp_rl"] = t_rlc

                    def p_ggy(ei):
                        def t_gg(e, ei=ei, cb=cb, gp=t_g4, gf=st["gbg_free"],
                                 yp=(ts["y"][ei - 1] if ei else None)):
                            e.w(gp, gf, yp)
                            e.inc(e.E.matmul(
                                GBg[:], e8c[:, ei * 128:(ei + 1) * 128],
                                Gp[cb][:], start=True, stop=True), "cT")
                        k.run("tensor", t_gg)
                        t_ggt = ("cT", k.cnt["cT"])

                        def v_y(e, ei=ei, t_ggt=t_ggt, yf=st["yts_free"],
                                c2=st["cs2_free"]):
                            e.w(t_ggt, yf, c2)
                            e.inc(e.E.tensor_tensor(
                                yTs[:, ei, :], cs2[:], GBg[:], ALU.mult), "cV")
                        k.run("vector", v_y)
                        ts["y"].append(("cV", k.cnt["cV"]))
                        if ei == 7:
                            st["gbg_free"] = ts["y"][-1]

                    def p_om(qsub, half):
                        of = ts["out_free"] if ts["out_free"] else st["out_free"]
                        ob = (qsub * 2 + half) % 2

                        def t_om(e, qsub=qsub, half=half, ob=ob, t_ow=t_ow, of=of,
                                 t_rlc=ts["rlc"], y_toks=list(ts["y"])):
                            e.w(t_ow, of, t_rlc)
                            ins = None
                            for ei in range(8):
                                e.w(y_toks[ei])
                                ins = e.E.matmul(
                                    OUT2[ob][:],
                                    yTs[:, ei, qsub * 128:(qsub + 1) * 128],
                                    owb[b][:, ei, half * 512:(half + 1) * 512],
                                    start=(ei == 0), stop=(ei == 7))
                            e.inc(ins, "cT")
                        k.run("tensor", t_om)
                        t_out = ("cT", k.cnt["cT"])

                        def ev(e, h=h, qsub=qsub, half=half, ob=ob,
                               sc8=qc * 4 + qsub, t_out=t_out,
                               rd=st["res_done"], t_rlc=ts["rlc"]):
                            e.w(t_out, rd, t_rlc)
                            dstp = oacc[:, sc8, half * 512:(half + 1) * 512]
                            if h == 0:
                                e.inc(e.E.tensor_scalar(
                                    dstp, OUT2[ob][:], rlc[:, qsub:qsub + 1], None,
                                    ALU.mult), "cV")
                            else:
                                e.inc(e.E.scalar_tensor_tensor(
                                    dstp, OUT2[ob][:], rlc[:, qsub:qsub + 1], dstp,
                                    ALU.mult, ALU.add), "cV")
                        k.run("vector", ev)
                        ts["out_free"] = ("cV", k.cnt["cV"])
                        if qsub == 3 and half == 1:
                            st["yts_free"] = ("cT", k.cnt["cT"])
                            st["out2_free"] = ts["out_free"]

                    pieces = [p_rl]
                    for ei in range(8):
                        pieces.append(lambda ei=ei: p_ggy(ei))
                    for qsub in range(4):
                        for half in range(2):
                            pieces.append(lambda q=qsub, hf=half: p_om(q, hf))
                    return pieces


                pend = []
                pend_sink = {}
                def pend_done():
                    pb = pend_sink.pop("pb", None)
                    if pb is not None:
                        ow_free[pb] = ("cT", k.cnt["cT"])
                issue_loads(0, 0, 0)
                idx = 0
                resid_q = None
                for sup in range(NSUP):
                    for h in range(FH):
                        b = idx % 2
                        t_ld, t_ow = ld_tok[idx]

                        for qc in range(NQC):
                            ci = idx * NQC + qc
                            cb = ci % 2
                            scbase = (sup * SUP + qc * QC) // 128

                            def t_go(e, h=h, scbase=scbase, bar=bar,
                                     tg=tok_gates, gf=st["gbp_free"],
                                     gr=st["gbp_rl"], gh=st["gbp_hold"]):
                                e.bar(bar)
                                e.w(tg, gf, gr, gh)
                                ins = None
                                for qsub in range(4):
                                    ins = e.E.transpose(
                                        GBp[0:8, qsub * 128:(qsub + 1) * 128],
                                        g_all[:, scbase + qsub,
                                              64 + h * 8:64 + h * 8 + 8],
                                        identf[:])
                                e.inc(ins, "cT")
                            k.run("tensor", t_go)
                            t_goT = ("cT", k.cnt["cT"])

                            def v_g4(e, cb=cb, t_goT=t_goT, gpf=st["gp_free"][cb]):
                                e.w(t_goT, gpf)
                                e.inc(e.E.tensor_copy(Gp[cb][:], GBp[0:8, :]), "cV")
                            k.run("vector", v_g4)
                            t_g4 = ("cV", k.cnt["cV"])
                            st["gbp_hold"] = t_g4

                            exp_toks = []
                            mm_toks = []
                            pvl_toks = []
                            ps2_toks = []
                            prev_pt = st["pt_free"]
                            prev_ps2 = st["ps2_free"]

                            def emit_sc(kc):
                                sb_ = kc % 2

                                def t_sc(e, b=b, sb_=sb_, kc=kc, qc=qc, t_ld=t_ld,
                                         bar=bar,
                                         ef=(exp_toks[kc - 2] if kc >= 2 else None)):
                                    e.bar(bar)
                                    e.w(t_ld, ef)
                                    e.inc(e.E.matmul(
                                        SC[sb_][:],
                                        kfb[b][:, kc * 128:(kc + 1) * 128],
                                        qhb[b][:, qc * QC:(qc + 1) * QC],
                                        start=True, stop=True), "cT")
                                k.run("tensor", t_sc)
                                mm_toks.append(("cT", k.cnt["cT"]))

                                def s_ex(e, sb_=sb_, kc=kc, mt=mm_toks[kc], bar=bar,
                                         ptf=(prev_pt[kc] if prev_pt else None)):
                                    e.bar(bar)
                                    e.w(mt, ptf)
                                    e.inc(e.E.activation(
                                        Pt[:, kc, :], SC[sb_][:], AF.Exp,
                                        bias=C_EXPB), "cS")
                                k.run("scalar", s_ex)
                                exp_toks.append(("cS", k.cnt["cS"]))

                            def emit_ps2(kc):
                                p2 = kc // 2

                                def g_ps(e, kc=kc, p2=p2, te=exp_toks[kc],
                                         pf=(prev_ps2[p2] if prev_ps2 else None)):
                                    e.w(te, pf)
                                    e.inc(e.E.tensor_add(
                                        Ps2[:, p2, :], Pt[:, kc - 1, :],
                                        Pt[:, kc, :]), "cP")
                                k.run("gpsimd", g_ps)
                                ps2_toks.append(("cP", k.cnt["cP"]))

                            def emit_pvl(kc):
                                p2 = kc // 2

                                def t_pvl(e, b=b, kc=kc, p2=p2, te=exp_toks[kc],
                                          cf=(st["ctx_free"] if kc == 0 else None),
                                          lf=(st["l_free"] if kc == 0 else None),
                                          pst=(ps2_toks[p2] if kc % 2 == 1 else None)):
                                    e.w(te, cf, lf)
                                    ins = e.E.matmul(
                                        CTX[:], vtb[b][:, kc, :], Pt[:, kc, :],
                                        start=(kc == 0), stop=(kc == NS - 1))
                                    if kc % 2 == 1:
                                        e.w(pst)
                                        ins = e.E.matmul(
                                            LP[0:1, :], ones_colb[:], Ps2[:, p2, :],
                                            start=(kc == 1), stop=(kc == NS - 1))
                                    e.inc(ins, "cT")
                                k.run("tensor", t_pvl)
                                pvl_toks.append(("cT", k.cnt["cT"]))

                            for kc in range(NS):
                                emit_sc(kc)
                                if kc % 2 == 1:
                                    emit_ps2(kc)
                                if kc == 1 and pend:
                                    pend.pop(0)()
                                if kc >= 2:
                                    emit_pvl(kc - 2)
                                if kc >= 3 and pend:
                                    pend.pop(0)()
                                    if not pend:
                                        pend_done()
                                if (kc == min(24, NS - 1) and not pend
                                        and resid_q is not None):
                                    st["res_done"] = emit_resid_super(
                                        RB, src, dst, resid_q, st["out2_free"],
                                        st["res_done"], oacc, bar, fuse_hn)
                                    resid_q = None
                            emit_pvl(NS - 2)
                            emit_pvl(NS - 1)
                            while pend:
                                pend.pop(0)()
                            pend_done()
                            if resid_q is not None:
                                st["res_done"] = emit_resid_super(
                                    RB, src, dst, resid_q, st["out2_free"],
                                    st["res_done"], oacc, bar, fuse_hn)
                                resid_q = None

                            def v_recip(e, pl=pvl_toks[-1], rsf=st.get("rl_s_free")):
                                e.w(pl, rsf)
                                e.inc(e.E.reciprocal(rl_s[:], LP[0:1, :]), "cV")
                            k.run("vector", v_recip)
                            t_recip = ("cV", k.cnt["cV"])
                            st["l_free"] = t_recip
                            st["rl_s_free"] = None

                            st["pt_free"] = pvl_toks
                            st["ps2_free"] = [pvl_toks[2 * p + 1]
                                              for p in range(NS // 2)]
                            pend = make_tail((ci, b, sup, h, qc, t_ow, t_recip,
                                              t_g4, pvl_toks))
                            if qc == NQC - 1:
                                pend_sink["pb"] = b

                        kf_free[b] = mm_toks[-1]
                        qh_free[b] = mm_toks[-1]
                        vt_free[b] = pvl_toks[-1]
                        if idx + 1 < NSUP * FH:
                            nsup, nh = divmod(idx + 1, FH)
                            issue_loads(idx + 1, nsup, nh)
                        idx += 1

                    resid_q = sup
                while pend:
                    pend.pop(0)()
                pend_done()
                if resid_q is not None:
                    st["res_done"] = emit_resid_super(
                        RB, src, dst, resid_q, st["out2_free"], st["res_done"],
                        oacc, bar, fuse_hn)
            return st["res_done"], (("gS0", k.cnt["gS0"]), ("gS1", k.cnt["gS1"]))


        def ffn_full(src, dst, bar, res_tok, hn_in, fuse_next):
            if hn_in is None:
                tok_hn = ln_phase(src, bar, res_tok)
            else:
                tok_hn = hn_in
            bar1 = k.snap()
            with ExitStack() as ps:
                g_all = ps.enter_context(nc.sbuf_tensor(_nm(), [128, NS, 128], F32))
                with ExitStack() as p1:
                    hnf = p1.enter_context(nc.sbuf_tensor(_nm(), [128, 8, S], BF16))
                    tok_hnf = load_hnf(hnf, tok_hn, None, bar1)
                    tok_gates = emit_sel(hnf, g_all, tok_hnf, bar1, p1)
                    tok_qk = emit_ffn_qk(hnf, tok_hnf, bar1, p1)
                    tok_v = emit_ffn_v(hnf, g_all, tok_hnf, tok_gates, bar1, p1)
                bar4 = k.snap()
                tok_res, hn_out = emit_ffn_attn(src, dst, g_all, tok_qk, tok_v,
                                                tok_gates, bar4, fuse_next)
            return k.snap(), tok_res, hn_out

        def emit_attn_qkv(tok_hnf, hnf, bar):
            with ExitStack() as ps:
                cosT = ps.enter_context(nc.sbuf_tensor(_nm(), [128, NS, 64], F32))
                s1T = ps.enter_context(nc.sbuf_tensor(_nm(), [128, NS, 64], F32))
                wqb = [ps.enter_context(nc.sbuf_tensor(_nm(), [128, 8, 512], BF16))
                       for _ in range(2)]
                qp = [ps.enter_context(nc.psum_tensor(_nm("ps"), [128, 512], F32)) for _ in range(2)]
                t1 = ps.enter_context(nc.sbuf_tensor(_nm(), [128, 8, 64], F32))
                t2 = ps.enter_context(nc.sbuf_tensor(_nm(), [128, 8, 64], F32))
                stg = [ps.enter_context(nc.sbuf_tensor(_nm(), [128, 512], BF16)) for _ in range(2)]

                def s_tbl(e, bar=bar):
                    e.bar(bar)
                    e.inc(e.E.dma_start(
                        cosT[:], cos_in.rearrange("(sc p) d -> p sc d", p=128)), "sC", 16)
                    e.inc(e.E.dma_start(
                        s1T[:], s1_in.rearrange("(sc p) d -> p sc d", p=128)), "sC", 16)
                k.run("sync", s_tbl)
                t_tbl = ("sC", k.cnt["sC"])

                wfree = [None, None]
                evtok = [None, None]
                sttok = [None, None]
                rope_busy = None
                for cc in range(6):
                    wb = cc % 2
                    is_v = cc >= 4

                    def s_w(e, cc=cc, wb=wb, wf=wfree[wb], bar=bar):
                        e.bar(bar)
                        e.w(*prep_toks, wf)
                        e.inc(e.E.dma_start(
                            wqb[wb][:],
                            p_wqkvT.rearrange("(c p) r -> p c r", p=128)
                            [:, :, cc * 512:(cc + 1) * 512]), "sW%d" % wb, 16)
                    k.run("sync", s_w)
                    t_w = ("sW%d" % wb, k.cnt["sW%d" % wb])

                    for sc in range(NS):
                        b = sc % 2

                        def t_mm(e, wb=wb, b=b, sc=sc, t_w=t_w, tok_hnf=tok_hnf,
                                 pf=evtok[b], bar=bar):
                            e.bar(bar)
                            e.w(t_w, tok_hnf, pf)
                            ins = None
                            for d in range(8):
                                ins = e.E.matmul(
                                    qp[b][:], hnf[:, d, sc * 128:(sc + 1) * 128],
                                    wqb[wb][:, d, :], start=(d == 0), stop=(d == 7))
                            e.inc(ins, "cT")
                        k.run("tensor", t_mm)
                        t_mm_t = ("cT", k.cnt["cT"])
                        if sc == NS - 1:
                            wfree[wb] = t_mm_t

                        if is_v:
                            def s_ev(e, b=b, t_mm_t=t_mm_t, sf=sttok[b], bar=bar):
                                e.bar(bar)
                                e.w(t_mm_t, sf)
                                e.inc(e.E.activation(stg[b][:], qp[b][:], AF.Copy),
                                      "cS")
                            k.run("scalar", s_ev)
                            evtok[b] = ("cS", k.cnt["cS"])
                        else:
                            def v_rope(e, b=b, sc=sc, t_mm_t=t_mm_t, t_tbl=t_tbl,
                                       sf=sttok[b], bar=bar, rbusy=rope_busy):
                                e.bar(bar)
                                e.w(t_mm_t, t_tbl, sf, rbusy)
                                qv = qp[b][:].rearrange("p (a b) -> p a b", a=8)
                                e.inc(e.E.tensor_tensor(
                                    t1[:], qv,
                                    cosT[:, sc:sc + 1, :].broadcast_to([128, 8, 64]),
                                    ALU.mult), "cV")
                                e.inc(e.E.tensor_tensor(
                                    t2[:, :, 0:32], qv[:, :, 32:64],
                                    s1T[:, sc:sc + 1, 0:32].broadcast_to([128, 8, 32]),
                                    ALU.mult), "cV")
                                e.inc(e.E.tensor_tensor(
                                    t2[:, :, 32:64], qv[:, :, 0:32],
                                    s1T[:, sc:sc + 1, 32:64].broadcast_to([128, 8, 32]),
                                    ALU.mult), "cV")
                                e.w(e.tok("cV"))
                                e.inc(e.E.tensor_tensor(
                                    stg[b][:].rearrange("p (a b) -> p a b", a=8),
                                    t1[:], t2[:], ALU.add), "cV")
                            k.run("vector", v_rope)
                            rope_busy = ("cV", k.cnt["cV"])
                            evtok[b] = rope_busy

                        def g_st(e, b=b, cc=cc, sc=sc, ev=evtok[b]):
                            e.w(ev)
                            dsts = (aq_h, ak_h, av_h)[cc // 2]
                            co = (cc % 2) * 512
                            e.inc(e.E.dma_start(
                                dsts[sc * 128:(sc + 1) * 128, co:co + 512],
                                stg[b][:]), "gS%d" % b, 16)
                        k.run("gpsimd", g_st)
                        sttok[b] = ("gS%d" % b, k.cnt["gS%d" % b])
            return (("gS0", k.cnt["gS0"]), ("gS1", k.cnt["gS1"]))

        def emit_attn_attn(src, dst, tok_qkv, bar, fuse_hn):
            with ExitStack() as ps:
                kpb = [ps.enter_context(nc.sbuf_tensor(_nm(), [128, S], BF16)) for _ in range(2)]
                qpb = [ps.enter_context(nc.sbuf_tensor(_nm(), [128, SUP], BF16)) for _ in range(2)]
                vab = [[ps.enter_context(nc.sbuf_tensor(_nm(), [128, NS, 65], BF16))
                        for _ in range(2)] for _ in range(2)]
                woT = ps.enter_context(nc.sbuf_tensor(_nm(), [128, 8, D], BF16))
                ctxf = [ps.enter_context(nc.sbuf_tensor(_nm(), [128, SUP], BF16))
                        for _ in range(8)]
                Pt2 = [ps.enter_context(nc.sbuf_tensor(_nm(), [128, NS, 512], BF16))
                       for _ in range(2)]
                SCa = [ps.enter_context(nc.psum_tensor(_nm("ps"), [128, 1024], F32))
                       for _ in range(2)]
                CTXA = [ps.enter_context(nc.psum_tensor(_nm("ps"), [65, 512], F32))
                        for _ in range(2)]
                RLP = ps.enter_context(nc.psum_tensor(_nm("ps"), [64, 512], F32))
                OUT = ps.enter_context(nc.psum_tensor(_nm("ps"), [128, 512], F32))
                rla = [ps.enter_context(nc.sbuf_tensor(_nm(), [1, 512], F32))
                       for _ in range(2)]
                cs_ = ps.enter_context(nc.sbuf_tensor(_nm(), [64, 512], F32))
                oacc = ps.enter_context(nc.sbuf_tensor(_nm(), [128, SUP // 128, D], F32))
                RB = make_resid(ps)

                def v_ones(e, bar=bar):
                    e.bar(bar)
                    for bb in range(2):
                        for hh in range(2):
                            e.inc(e.E.memset(vab[bb][hh][:, :, 64:65], 1.0), "cV")
                k.run("vector", v_ones)
                t_ones = ("cV", k.cnt["cV"])

                def s_wo(e, bar=bar):
                    e.bar(bar)
                    e.w(*prep_toks)
                    e.inc(e.E.dma_start(
                        woT[:], p_woutT.rearrange("(c p) r -> p c r", p=128)), "sC", 16)
                k.run("sync", s_wo)
                t_wo = ("sC", k.cnt["sC"])

                kp_free = [None, None]
                qp_free = [None, None]
                va_free = [None, None]
                ld_tok = {}

                def issue_loads(idx, sup, pair):
                    b = idx % 2

                    def s_ld(e, sup=sup, pair=pair, b=b, kf=kp_free[b],
                             qf=qp_free[b], vf=va_free[b], t_ones=t_ones, bar=bar):
                        e.bar(bar)
                        e.w(*tok_qkv, t_ones, kf, qf, vf)
                        e.inc(e.E.dma_start_transpose(
                            kpb[b][:], ak_h[:, pair * 128:(pair + 1) * 128]),
                            "sL%d" % b, 16)
                        e.inc(e.E.dma_start_transpose(
                            qpb[b][:], aq_h[sup * SUP:(sup + 1) * SUP,
                                            pair * 128:(pair + 1) * 128]),
                            "sL%d" % b, 16)
                        for hh in range(2):
                            hcol = (pair * 2 + hh) * 64
                            e.inc(e.E.dma_start(
                                vab[b][hh][:, :, 0:64],
                                av_h[:, hcol:hcol + 64]
                                .rearrange("(sc p) c -> p sc c", p=128)),
                                "sL%d" % b, 16)
                    k.run("sync", s_ld)
                    ld_tok[idx] = ("sL%d" % b, k.cnt["sL%d" % b])

                st = {
                    "pt_free": [None, None],
                    "ctx_free": [None, None],
                    "rlp_free": None,
                    "cs_free": None,
                    "rla_free": [None, None],
                    "out_free": None,
                    "res_done": None,
                }
                ctxf_free = [None] * 8

                def make_tail(c):
                    (b, sup, pair, qc, recips, pv_last) = c
                    ts = {}

                    def p_hh(hh):
                        def t_rlp(e, hh=hh, tr=recips[hh], rf=st["rlp_free"]):
                            e.w(tr, rf)
                            e.inc(e.E.matmul(RLP[:], ones_rowf[:, 0:64],
                                             rla[hh][:], start=True, stop=True),
                                  "cT")
                        k.run("tensor", t_rlp)
                        t_rlp_t = ("cT", k.cnt["cT"])
                        st["rla_free"][hh] = t_rlp_t

                        def v_cs(e, hh=hh, pl=pv_last[hh], cf=st["cs_free"]):
                            e.w(pl, cf)
                            e.inc(e.E.tensor_copy(cs_[:], CTXA[hh][0:64, :]), "cV")
                        k.run("vector", v_cs)
                        t_cs = ("cV", k.cnt["cV"])
                        st["ctx_free"][hh] = t_cs

                        def v_n(e, pair=pair, hh=hh, qc=qc, t_cs=t_cs,
                                t_rlp_t=t_rlp_t, cff=ctxf_free[pair]):
                            e.w(t_cs, t_rlp_t, cff)
                            e.inc(e.E.tensor_tensor(
                                ctxf[pair][hh * 64:(hh + 1) * 64,
                                           qc * QC:(qc + 1) * QC],
                                cs_[:], RLP[:], ALU.mult), "cV")
                        k.run("vector", v_n)
                        t_n = ("cV", k.cnt["cV"])
                        st["cs_free"] = t_n
                        st["rlp_free"] = t_n
                        ts[hh] = t_n

                    return [lambda: p_hh(0), lambda: p_hh(1)], ts

                pend = []
                issue_loads(0, 0, 0)
                idx = 0
                resid_q = None
                for sup in range(NSUP):
                    for pair in range(8):
                        b = idx % 2
                        t_ld = ld_tok[idx]

                        for qc in range(NQC):
                            exp_toks = [[], []]
                            mm_toks = [[], []]
                            pv_toks = [[], []]
                            prev_pt = [st["pt_free"][0], st["pt_free"][1]]

                            def emit_sc(p, hh):
                                base = hh * 64

                                def t_sc(e, b=b, p=p, hh=hh, base=base, qc=qc,
                                         t_ld=t_ld, bar=bar,
                                         ef=(exp_toks[hh][p - 1] if p >= 1 else None)):
                                    e.bar(bar)
                                    e.w(t_ld, ef)
                                    ins = None
                                    for half in range(2):
                                        kc = 2 * p + half
                                        ins = e.E.matmul(
                                            SCa[hh][:, half * 512:(half + 1) * 512],
                                            kpb[b][base:base + 64,
                                                   kc * 128:(kc + 1) * 128],
                                            qpb[b][base:base + 64,
                                                   qc * QC:(qc + 1) * QC],
                                            start=True, stop=True)
                                    e.inc(ins, "cT")
                                k.run("tensor", t_sc)
                                mm_toks[hh].append(("cT", k.cnt["cT"]))

                                def s_ex(e, p=p, hh=hh, mt=mm_toks[hh][p], bar=bar,
                                         ptf=(prev_pt[hh][p] if prev_pt[hh] else None)):
                                    e.bar(bar)
                                    e.w(mt, ptf)
                                    e.inc(e.E.activation(
                                        Pt2[hh][:, 2 * p:2 * p + 2, :]
                                        .rearrange("p a b -> p (a b)"),
                                        SCa[hh][:], AF.Exp, bias=C_EXPB), "cS")
                                k.run("scalar", s_ex)
                                exp_toks[hh].append(("cS", k.cnt["cS"]))

                            def emit_pv(p, hh):
                                def t_pv(e, b=b, p=p, hh=hh, te=exp_toks[hh][p],
                                         cf=(st["ctx_free"][hh] if p == 0 else None)):
                                    e.w(te, cf)
                                    ins = None
                                    for kc in (2 * p, 2 * p + 1):
                                        ins = e.E.matmul(
                                            CTXA[hh][:], vab[b][hh][:, kc, :],
                                            Pt2[hh][:, kc, :],
                                            start=(kc == 0), stop=(kc == NS - 1))
                                    e.inc(ins, "cT")
                                k.run("tensor", t_pv)
                                pv_toks[hh].append(("cT", k.cnt["cT"]))

                            for p in range(NS // 2):
                                emit_sc(p, 0)
                                emit_sc(p, 1)
                                if p <= 1 and pend:
                                    pend.pop(0)()
                                if p >= 2:
                                    emit_pv(p - 2, 0)
                                    emit_pv(p - 2, 1)
                                if (p == min(10, NS // 2 - 1) and not pend
                                        and resid_q is not None):
                                    st["res_done"] = emit_resid_super(
                                        RB, src, dst, resid_q, st["out_free"],
                                        st["res_done"], oacc, bar, fuse_hn)
                                    resid_q = None
                            for pp in (NS // 2 - 2, NS // 2 - 1):
                                emit_pv(pp, 0)
                                emit_pv(pp, 1)
                            while pend:
                                pend.pop(0)()

                            recips = []
                            for hh in range(2):
                                def v_recip(e, hh=hh, pl=pv_toks[hh][-1],
                                            rf=st["rla_free"][hh]):
                                    e.w(pl, rf)
                                    e.inc(e.E.reciprocal(rla[hh][:],
                                                         CTXA[hh][64:65, :]), "cV")
                                k.run("vector", v_recip)
                                recips.append(("cV", k.cnt["cV"]))
                            st["pt_free"] = [pv_toks[0], pv_toks[1]]
                            pieces, _ = make_tail((b, sup, pair, qc, recips,
                                                   [pv_toks[0][-1], pv_toks[1][-1]]))
                            pend = pieces

                        kp_free[b] = mm_toks[1][-1]
                        qp_free[b] = mm_toks[1][-1]
                        va_free[b] = pv_toks[1][-1]
                        if idx + 1 < NSUP * 8:
                            nsup, npair = divmod(idx + 1, 8)
                            issue_loads(idx + 1, nsup, npair)
                        idx += 1

                    while pend:
                        pend.pop(0)()
                    t_allpairs = ("cV", k.cnt["cV"])

                    out_evac = st["out_free"]
                    for qc in range(NQC):
                        for qsub in range(4):
                            for half in range(2):
                                def t_wom(e, qc=qc, qsub=qsub, half=half,
                                          t_allpairs=t_allpairs, t_wo=t_wo,
                                          oe=out_evac):
                                    e.w(t_allpairs, t_wo, oe)
                                    ins = None
                                    for pc in range(8):
                                        ins = e.E.matmul(
                                            OUT[:],
                                            ctxf[pc][:, qc * QC + qsub * 128:
                                                     qc * QC + (qsub + 1) * 128],
                                            woT[:, pc, half * 512:(half + 1) * 512],
                                            start=(pc == 0), stop=(pc == 7))
                                    e.inc(ins, "cT")
                                k.run("tensor", t_wom)
                                t_o = ("cT", k.cnt["cT"])

                                def v_ev(e, qc=qc, qsub=qsub, half=half, t_o=t_o,
                                         rd=st["res_done"]):
                                    e.w(t_o, rd)
                                    e.inc(e.E.tensor_copy(
                                        oacc[:, qc * 4 + qsub,
                                             half * 512:(half + 1) * 512],
                                        OUT[:]), "cV")
                                k.run("vector", v_ev)
                                out_evac = ("cV", k.cnt["cV"])
                    st["out_free"] = out_evac
                    for pc in range(8):
                        ctxf_free[pc] = ("cT", k.cnt["cT"])

                    resid_q = sup
                if resid_q is not None:
                    st["res_done"] = emit_resid_super(
                        RB, src, dst, resid_q, st["out_free"], st["res_done"],
                        oacc, bar, fuse_hn)
            return st["res_done"], (("gS0", k.cnt["gS0"]), ("gS1", k.cnt["gS1"]))


        def attn_full(src, dst, bar, res_tok, hn_in, fuse_next):
            if hn_in is None:
                tok_hn = ln_phase(src, bar, res_tok)
            else:
                tok_hn = hn_in
            bar1 = k.snap()
            with ExitStack() as ps:
                with ExitStack() as p1:
                    hnf = p1.enter_context(nc.sbuf_tensor(_nm(), [128, 8, S], BF16))
                    tok_hnf = load_hnf(hnf, tok_hn, None, bar1)
                    tok_qkv = emit_attn_qkv(tok_hnf, hnf, bar1)
                bar2 = k.snap()
                tok_res, hn_out = emit_attn_attn(src, dst, tok_qkv, bar2, fuse_next)
            return k.snap(), tok_res, hn_out

        with ExitStack() as _prep_ps:
            _pin = [_prep_ps.enter_context(nc.sbuf_tensor(_nm(), [128, D], F32)) for _ in range(2)]
            _pps = [_prep_ps.enter_context(nc.psum_tensor(_nm("ps"), [128, D], F32)) for _ in range(2)]
            _pst = [_prep_ps.enter_context(nc.sbuf_tensor(_nm(), [128, D], BF16)) for _ in range(2)]
            hn0_tok = ln_phase(x_in, k.snap(), None)
            prep(_prep_ps, _pin, _pps, _pst)
        prep_toks.extend(((("gS0", k.cnt["gS0"])), (("gS1", k.cnt["gS1"]))))
        prep_toks2.append(("gW", k.cnt["gW"]))

        chain = []
        if "f1" in parts:
            chain.append(("f", None))
        if "at" in parts:
            chain.append(("a", None))
        if "f2" in parts:
            chain.append(("f", None))
        srcs = [x_in, xA, xB]
        cur_bar = k.snap()
        res_tok = None
        hn_tok = hn0_tok
        src = x_in
        for i, (kind, _) in enumerate(chain):
            dst = out if i == len(chain) - 1 else srcs[i + 1]
            fuse_next = i < len(chain) - 1
            if kind == "f":
                cur_bar, res_tok, hn_tok = ffn_full(src, dst, cur_bar, res_tok,
                                                    hn_tok, fuse_next)
            else:
                cur_bar, res_tok, hn_tok = attn_full(src, dst, cur_bar, res_tok,
                                                     hn_tok, fuse_next)
            src = dst

        def s_fin(e):
            e.w(*[(s, k.cnt[s]) for s in k.cnt])
        k.run("sync", s_fin)

        k.check_deadlock()
        k.flush()
    return nc



def _host_constants(S):
    half = DH // 2
    inv = 1.0 / (10000.0 ** (np.arange(half, dtype=np.float64) / half))
    ang = np.arange(S, dtype=np.float64)[:, None] * inv[None, :]
    cos = np.cos(ang).astype(np.float32)
    sin = np.sin(ang).astype(np.float32)
    ropecos = np.concatenate([cos, cos], axis=1)
    ropes1 = np.concatenate([-sin, sin], axis=1)
    consts = np.zeros((128, 8), np.float32)
    consts[:, 0] = C_EXPB
    consts[:, 1] = C_EPS
    consts[:, 2] = C_MH
    consts[:, 3] = C_QS_ATTN
    consts[:, 4] = C_QS_FFN
    identf = np.eye(128, dtype=np.float32)
    identb = np.eye(128, dtype=np.float32).astype(BF)
    e8 = np.zeros((8, 1024), np.float32)
    for e in range(8):
        e8[e, e * 128:(e + 1) * 128] = 1.0
    e8 = e8.astype(BF)
    return ropecos, ropes1, consts, identf, identb, e8


def kernel(L0_attn_x, L0_attn_w_qkv, L0_attn_w_out, L0_ffn_q_w, L0_ffn_k_w,
           L0_ffn_v_w, L0_ffn_o_w, L0_ffn_sel_v_w, L0_ffn_sel_o_w,
           L0_ffn_route_scale):
    S = L0_attn_x.shape[1]
    ropecos, ropes1, consts, identf, identb, e8 = _host_constants(S)
    nc = build_program(S=S)
    common = {
        "w_qkv": np.ascontiguousarray(L0_attn_w_qkv),
        "w_out": np.ascontiguousarray(L0_attn_w_out),
        "qw": np.ascontiguousarray(L0_ffn_q_w),
        "kw": np.ascontiguousarray(L0_ffn_k_w),
        "vw": np.ascontiguousarray(L0_ffn_v_w),
        "ow": np.ascontiguousarray(L0_ffn_o_w),
        "selv": np.ascontiguousarray(L0_ffn_sel_v_w),
        "selo": np.ascontiguousarray(L0_ffn_sel_o_w),
        "rs": np.ascontiguousarray(L0_ffn_route_scale).reshape(1, 1),
        "ropecos": ropecos, "ropes1": ropes1, "consts": consts,
        "identf": identf, "identb": identb, "e8": e8,
    }
    in_maps = []
    for c in range(8):
        m = dict(common)
        m["x"] = np.ascontiguousarray(L0_attn_x[c])
        in_maps.append(m)
    res = run_bass_kernel_spmd(nc, in_maps, core_ids=list(range(8)))
    return np.stack([res.results[c]["out"] for c in range(8)], axis=0)
